# version 05
# baseline (speedup 1.0000x reference)
import numpy as np
import ml_dtypes
from contextlib import ExitStack

import concourse.bass as bass
import concourse.mybir as mybir
from concourse.bass_utils import run_bass_kernel_spmd

F32 = mybir.dt.float32
BF16 = mybir.dt.bfloat16
AF = mybir.ActivationFunctionType
ALU = mybir.AluOpType
AX = mybir.AxisListType

N_CORES = 8
T = 2048
D = 4096
NT128 = T // 128
ND128 = D // 128
EPS = 1e-5
BF = ml_dtypes.bfloat16


class Sem:
    __slots__ = ("h", "n")

    def __init__(self, h):
        self.h = h
        self.n = 0


class Prog:
    def __init__(self):
        self.nc = bass.Bass("TRN2", target_bir_lowering=False, debug=False)
        self.stack = ExitStack()
        self._uid = 0
        self.em = Em(self)
        self.ds = self.sem("ds")
        self.hs = self.sem("hs")
        self.ts = self.sem("ts")
        self.vs = self.sem("vs")
        self.ss = self.sem("ss")
        self.gs = self.sem("gs")

    def uid(self, p):
        self._uid += 1
        return f"{p}{self._uid}"

    def sem(self, name):
        return Sem(self.stack.enter_context(self.nc.semaphore(self.uid(name))))

    def sbuf(self, name, shape, dt):
        return self.stack.enter_context(self.nc.sbuf_tensor(self.uid(name), shape, dt))

    def psum(self, name, shape, dt):
        return self.stack.enter_context(self.nc.psum_tensor(self.uid(name), shape, dt))

    def dram(self, name, shape, dt, kind="Internal"):
        return self.nc.dram_tensor(self.uid(name), shape, dt, kind=kind)


class Em:
    ENGS = ("g", "y", "t", "v", "s")

    def __init__(self, P):
        self.P = P
        self.q = {e: [] for e in self.ENGS}

    def wait(self, e, sem, val=None):
        v = sem.n if val is None else val
        if v > 0:
            self.q[e].append(lambda eng, _s=sem.h, _v=v: eng.wait_ge(_s, _v))

    def op(self, e, build, inc=None, k=1):
        if inc is None:
            self.q[e].append(build)
            return None
        inc.n += k
        self.q[e].append(lambda eng, _b=build, _s=inc.h, _k=k: _b(eng).then_inc(_s, _k))
        return inc.n

    def dma(self, e, out, in_, **kw):
        sem = self.P.ds if e == "g" else self.P.hs
        sem.n += 16
        self.q[e].append(
            lambda eng, _o=out, _i=in_, _k=dict(kw), _s=sem.h: eng.dma_start(
                _o, _i, **_k).then_inc(_s, 16))
        return sem.n

    def flush(self):
        nc = self.P.nc
        qs = {e: lst for e, lst in self.q.items() if lst}
        self.q = {e: [] for e in self.ENGS}
        if not qs:
            return
        with nc.Block() as b:
            decs = {"g": b.gpsimd, "y": b.sync, "t": b.tensor,
                    "v": b.vector, "s": b.scalar}
            for e, lst in qs.items():
                def runner(eng, _lst=lst):
                    for f in _lst:
                        f(eng)
                decs[e](runner)


from contextlib import contextmanager


@contextmanager
def scope(P):
    st = ExitStack()
    old = P.stack
    P.stack = st
    try:
        yield
    finally:
        for e in Em.ENGS:
            P.em.wait(e, P.ds)
            P.em.wait(e, P.hs)
        P.em.flush()
        st.close()
        P.stack = old



def emit_setup(P):
    em = P.em
    C = {}
    C["eye"] = P.sbuf("eye", [128, 128], BF16)
    C["eyef"] = eyef = P.sbuf("eyef", [128, 128], F32)
    C["ones_row"] = P.sbuf("onesrow", [1, 512], BF16)
    C["sel8"] = P.sbuf("sel8", [8, 128], BF16)
    C["eps"] = P.sbuf("epsc", [128, 1], F32)
    self8 = P.sbuf("self8", [8, 128], F32)
    t8 = P.sbuf("t8", [8, 128], F32)
    em.op("g", lambda g: g.iota(eyef[:], [[1, 128]], channel_multiplier=-1,
                                allow_small_or_imprecise_dtypes=True), inc=P.gs)
    em.op("g", lambda g: g.iota(self8[:], [[1, 128]], channel_multiplier=-16,
                                allow_small_or_imprecise_dtypes=True), inc=P.gs)
    em.op("g", lambda g: g.memset(C["ones_row"][:], 1.0), inc=P.gs)
    em.op("g", lambda g: g.memset(C["eps"][:], EPS), inc=P.gs)
    em.wait("v", P.gs)
    em.op("v", lambda v: v.tensor_single_scalar(C["eye"][:], eyef[:], 0.0,
                                                ALU.is_equal), inc=P.vs)
    em.wait("v", P.vs)
    em.op("v", lambda v: v.tensor_copy(eyef[:], C["eye"][:]), inc=P.vs)
    em.op("v", lambda v: v.tensor_single_scalar(t8[:], self8[:], -0.5, ALU.is_gt),
          inc=P.vs)
    em.op("v", lambda v: v.tensor_single_scalar(self8[:], self8[:], 15.5, ALU.is_lt),
          inc=P.vs)
    em.wait("v", P.vs)
    em.op("v", lambda v: v.tensor_tensor(C["sel8"][:], t8[:], self8[:], ALU.mult),
          inc=P.vs)
    em.flush()
    return C


def emit_ln(P, x_ap, xn_ap, C=None):
    em = P.em
    xbuf = [P.sbuf("lnx", [128, D], F32) for _ in range(2)]
    xnb = [P.sbuf("lnxn", [128, D], BF16) for _ in range(2)]
    sq = P.sbuf("lnsq", [128, D], F32)
    st = [P.sbuf("lnst", [128, 6], F32) for _ in range(2)]
    app = [0] * NT128
    stor = [0] * NT128
    for i in range(NT128):
        b = st[i % 2]
        if i >= 2:
            em.wait("s", P.hs, stor[i - 2])
        ld = em.dma("s", xbuf[i % 2][:], x_ap[i * 128:(i + 1) * 128, :])
        em.wait("s", P.hs, ld)
        em.op("s", lambda sc, _i=i, _b=b: sc.activation(
            sq[:], xbuf[_i % 2][:], AF.Square, accum_out=_b[:, 1:2]), inc=P.ss)
        sq_done = P.ss.n
        em.wait("v", P.hs, ld)
        em.op("v", lambda v, _i=i, _b=b: v.reduce_sum(
            _b[:, 0:1], xbuf[_i % 2][:], axis=AX.X), inc=P.vs)
        em.wait("v", P.vs)
        em.wait("v", P.ss, sq_done)
        em.op("v", lambda v, _b=b: v.tensor_single_scalar(
            _b[:, 2:3], _b[:, 0:1], 1.0 / D, ALU.mult), inc=P.vs)
        em.wait("v", P.vs)
        em.op("v", lambda v, _b=b: v.tensor_tensor(
            _b[:, 3:4], _b[:, 2:3], _b[:, 2:3], ALU.mult), inc=P.vs)
        em.wait("v", P.vs)
        em.op("v", lambda v, _b=b: v.scalar_tensor_tensor(
            _b[:, 1:2], _b[:, 1:2], 1.0 / D, _b[:, 3:4], ALU.mult, ALU.subtract),
            inc=P.vs)
        var_done = P.vs.n
        em.wait("s", P.vs, var_done)
        em.op("s", lambda sc, _b=b: sc.activation(
            _b[:, 4:5], _b[:, 1:2], AF.Ln, bias=C["eps"][:]))
        em.op("s", lambda sc, _b=b: sc.activation(
            _b[:, 5:6], _b[:, 4:5], AF.Exp, scale=-0.5), inc=P.ss)
        rstd_done = P.ss.n
        em.wait("v", P.ss, rstd_done)
        if i >= 2:
            em.wait("v", P.hs, stor[i - 2])
        em.op("v", lambda v, _i=i, _b=b: v.tensor_scalar(
            xnb[_i % 2][:], xbuf[_i % 2][:], _b[:, 2:3], _b[:, 5:6],
            ALU.subtract, ALU.mult), inc=P.vs)
        app[i] = P.vs.n
        em.wait("y", P.vs, app[i])
        stor[i] = em.dma("y", xn_ap[i * 128:(i + 1) * 128, :], xnb[i % 2][:])
    em.flush()


def emit_cast_load(P, dst_bf, src_f32, stagebuf):
    em = P.em
    p, f = dst_bf.tensor.shape if hasattr(dst_bf, "tensor") else (None, None)
    ld = em.dma("y", stagebuf, src_f32)
    em.wait("v", P.hs, ld)
    em.op("v", lambda v: v.tensor_copy(dst_bf, stagebuf), inc=P.vs)
    return P.vs.n


def emit_load_xnfm(P, xn_ap, xnfm, tok0, ntok, waits=None):
    em = P.em
    for k in range(ND128):
        em.dma("y", xnfm[:, k * ntok:(k + 1) * ntok],
               xn_ap[tok0:tok0 + ntok, k * 128:(k + 1) * 128], transpose=True)
    return P.hs.n



def build_l0ffn(P, C, x_ap, xo_ap, W):
    em = P.em
    xn = P.dram("xn", [T, D], BF16).ap()
    with scope(P):
        emit_ln(P, x_ap, xn, C)

    rwT = P.sbuf("rwT", [128, 8 * ND128], BF16)
    acatT = P.sbuf("acatT", [128, D], BF16)
    bcat = P.sbuf("bcat", [128, D], BF16)
    with scope(P):
        stg = P.sbuf("wstg", [128, D], F32)
        emit_cast_load(P, rwT[:], W["rwT"], stg[:, 0:256])
        em.wait("y", P.vs)
        emit_cast_load(P, acatT[:], W["AtL"], stg[:])
        em.wait("y", P.vs)
        emit_cast_load(P, bcat[:], W["BcatL"], stg[:])
    wprep = P.vs.n
    em.flush()

    xnfm = P.sbuf("xnfm", [128, ND128 * 512], BF16)
    rw4 = P.sbuf("rw4", [128, 32], F32)
    mrow = P.sbuf("mrow", [128, 4], F32)
    rwTt = P.sbuf("rwTt", [8, 512], BF16)
    rwexp = P.sbuf("rwexp", [128, 512], BF16)
    wh = P.sbuf("wh", [128, 512], BF16)
    xres = P.sbuf("xres", [128, D], F32)
    stage = [P.sbuf("stage", [128, 512], F32) for _ in range(2)]
    pL = P.psum("pL", [128, 512], F32)
    pR = P.psum("pR", [8, 512], F32)
    pE = P.psum("pE", [128, 512], F32)
    pH = P.psum("pH", [128, 512], F32)
    pD = [P.psum("pD", [128, 512], F32) for _ in range(2)]
    em.wait("t", P.vs, wprep)

    for tt in range(4):
        tok0 = tt * 512
        em.wait("y", P.ts)
        ldfm = emit_load_xnfm(P, xn, xnfm, tok0, 512)
        em.wait("t", P.hs, ldfm)
        for q in range(4):
            for k in range(ND128):
                em.op("t", lambda t, _q=q, _k=k: t.matmul(
                    pL[:, _q * 8:(_q + 1) * 8],
                    xnfm[:, _k * 512 + _q * 128:_k * 512 + (_q + 1) * 128],
                    rwT[:, _k * 8:(_k + 1) * 8],
                    start=(_k == 0), stop=(_k == ND128 - 1)),
                    inc=(P.ts if k == ND128 - 1 else None))
        em.wait("v", P.ts)
        for q in range(4):
            em.op("v", lambda v, _q=q: v.reduce_max(
                mrow[:, _q:_q + 1], pL[:, _q * 8:(_q + 1) * 8], axis=AX.X),
                inc=P.vs)
            em.wait("v", P.vs)
            em.op("v", lambda v, _q=q: v.tensor_scalar(
                rw4[:, _q * 8:(_q + 1) * 8], pL[:, _q * 8:(_q + 1) * 8],
                mrow[:, _q:_q + 1], mrow[:, _q:_q + 1],
                ALU.is_equal, ALU.mult), inc=P.vs)
        rw_done = P.vs.n
        em.wait("t", P.vs, rw_done)
        for q in range(4):
            em.op("t", lambda t, _q=q: t.transpose(
                pR[:, _q * 128:(_q + 1) * 128], rw4[:, _q * 8:(_q + 1) * 8],
                C["eyef"][:]), inc=P.ts)
        em.wait("v", P.ts)
        em.op("v", lambda v: v.tensor_copy(rwTt[:], pR[:]), inc=P.vs)
        em.wait("t", P.vs)
        em.op("t", lambda t: t.matmul(pE[:], C["sel8"][:], rwTt[:],
                                      start=True, stop=True), inc=P.ts)
        em.wait("v", P.ts)
        em.op("v", lambda v: v.tensor_copy(rwexp[:], pE[:]), inc=P.vs)
        for k in range(ND128):
            em.op("t", lambda t, _k=k: t.matmul(
                pH[:], acatT[:, _k * 128:(_k + 1) * 128],
                xnfm[:, _k * 512:(_k + 1) * 512],
                start=(_k == 0), stop=(_k == ND128 - 1)),
                inc=(P.ts if k == ND128 - 1 else None))
        em.wait("v", P.ts)
        em.wait("v", P.vs)
        em.op("v", lambda v: v.tensor_tensor(wh[:], pH[:], rwexp[:], ALU.mult),
              inc=P.vs)
        wh_done = P.vs.n
        em.wait("t", P.vs, wh_done)
        stt_done = []
        store_done = []
        for q in range(4):
            em.wait("s", P.vs)
            ldx = em.dma("s", xres[:], x_ap[tok0 + q * 128:tok0 + (q + 1) * 128, :])
            for do in range(8):
                i = q * 8 + do
                pd = pD[i % 2]
                if i >= 2:
                    em.wait("t", P.vs, stt_done[i - 2])
                em.op("t", lambda t, _q=q, _do=do, _pd=pd: t.matmul(
                    _pd[:], wh[:, _q * 128:(_q + 1) * 128],
                    bcat[:, _do * 512:(_do + 1) * 512],
                    start=True, stop=True), inc=P.ts)
                mm_done = P.ts.n
                stg = stage[i % 2]
                em.wait("v", P.ts, mm_done)
                em.wait("v", P.hs, ldx)
                if i >= 2:
                    em.wait("v", P.hs, store_done[i - 2])
                em.op("v", lambda v, _pd=pd, _do=do, _stg=stg: v.scalar_tensor_tensor(
                    _stg[:], _pd[:], 0.5, xres[:, _do * 512:(_do + 1) * 512],
                    ALU.mult, ALU.add), inc=P.vs)
                stt_done.append(P.vs.n)
                em.wait("y", P.vs)
                em.dma("y", xo_ap[tok0 + q * 128:tok0 + (q + 1) * 128,
                                  do * 512:(do + 1) * 512], stg[:])
                store_done.append(P.hs.n)
        em.flush()
    em.wait("g", P.ds)
    em.flush()


def _k_last(k):
    return k == ND128 - 1




NJT = 86
SC_LORA = 2.0


DBG_L1 = False


def emit_l1ffn_caches(P, W):
    em = P.em
    w1b = P.dram("w1b", [NJT, 128, 4224], BF16).ap()
    w3b = P.dram("w3b", [NJT, 128, 4224], BF16).ap()
    w2b = P.dram("w2b", [8, 128, NJT, 512], BF16).ap()
    for c in range(8):
        j0, j1 = c * 11, min((c + 1) * 11, NJT)
        if j0 < NJT:
            em.dma("g", w1b[j0:j1], W["w1L"][j0:j1])
            em.dma("g", w3b[j0:j1], W["w3L"][j0:j1])
        em.dma("g", w2b[c], W["w2L"][c])
    return dict(w1b=w1b, w3b=w3b, w2b=w2b, cast_done=P.ds.n)


def build_l1ffn(P, C, x_ap, xo_ap, W, caches=None):
    em = P.em
    xn = P.dram("xn", [T, D], BF16).ap()
    if DBG_L1:
        dbg_g = P.nc.dram_tensor("dbg_g", [4, 128, NJT * 512], BF16,
                                 kind="ExternalOutput").ap()
        dbg_xa = P.nc.dram_tensor("dbg_xa", [4, 16, 1024], BF16,
                                  kind="ExternalOutput").ap()

    if caches is None:
        caches = emit_l1ffn_caches(P, W)
    w1b, w3b, w2b = caches["w1b"], caches["w3b"], caches["w2b"]

    with scope(P):
        emit_ln(P, x_ap, xn, C)

    a1t = P.sbuf("a1t", [128, 512], BF16)
    a3t = P.sbuf("a3t", [128, 512], BF16)
    a2t = P.sbuf("a2t", [128, 1376], BF16)
    b2t = P.sbuf("b2t", [16, D], BF16)
    em.dma("g", a1t[:], W["a1tL"])
    em.dma("g", a3t[:], W["a3tL"])
    em.dma("g", a2t[:], W["a2tL"])
    em.dma("g", b2t[:], W["b2T"])
    smalls = P.ds.n
    em.flush()

    with scope(P):
        xnfm = P.sbuf("xnfm", [128, ND128 * 512], BF16)
        w1p = [P.sbuf("w1p", [128, 4224], BF16) for _ in range(2)]
        w3p = [P.sbuf("w3p", [128, 4224], BF16) for _ in range(2)]
        h1s = [P.sbuf("h1s", [128, 512], BF16) for _ in range(2)]
        g_all = P.sbuf("gall", [128, NJT * 512], BF16)
        xa1 = P.sbuf("xa1", [16, 512], BF16)
        xa3 = P.sbuf("xa3", [16, 512], BF16)
        ga2 = P.sbuf("ga2", [16, 512], BF16)
        w2buf = [P.sbuf("w2buf", [128, 4096], BF16) for _ in range(2)]
        xresd = P.sbuf("xresd", [128, 2048], F32)
        stage = [P.sbuf("stg", [128, 512], F32) for _ in range(4)]
        pb = [P.psum("pb", [128, 512], F32) for _ in range(8)]

        for e in ("t", "y", "g", "v", "s"):
            em.wait(e, P.ds, max(smalls, caches.get("cast_done", 0)))

        for tt in range(4):
            tok0 = tt * 512
            em.wait("y", P.ts)
            ldfm = emit_load_xnfm(P, xn, xnfm, tok0, 512)
            em.wait("t", P.hs, ldfm)

            em.wait("t", P.vs)
            for (bank, at, xs_out) in ((4, a1t, xa1), (5, a3t, xa3)):
                for k in range(ND128):
                    em.op("t", lambda t, _b=bank, _a=at, _k=k: t.matmul(
                        pb[_b][0:16, :], _a[:, _k * 16:(_k + 1) * 16],
                        xnfm[:, _k * 512:(_k + 1) * 512],
                        start=(_k == 0), stop=(_k == ND128 - 1)),
                        inc=(P.ts if k == ND128 - 1 else None))
                em.wait("s", P.ts)
                em.op("s", lambda sc, _b=bank, _o=xs_out: sc.activation(
                    _o[:], pb[_b][0:16, :], AF.Identity, scale=SC_LORA), inc=P.ss)
            lora_evac = P.ss.n
            em.wait("t", P.ss, lora_evac)

            t1d = [0] * NJT
            t3d = [0] * NJT
            sild = [0] * NJT
            gd = [0] * NJT
            ld_w = [0] * (NJT + 1)
            for js in range(NJT):
                if js == 0:
                    em.wait("y", P.ts)
                    em.dma("y", w1p[0][:], w1b[0])
                    em.dma("y", w3p[0][:], w3b[0])
                    ld_w[0] = P.hs.n
                if js + 1 < NJT:
                    if js >= 1:
                        em.wait("y", P.ts, t3d[js - 1])
                    em.dma("y", w1p[(js + 1) % 2][:], w1b[js + 1])
                    em.dma("y", w3p[(js + 1) % 2][:], w3b[js + 1])
                    ld_w[js + 1] = P.hs.n
                s = js % 2
                k1, k3 = 2 * s, 2 * s + 1
                em.wait("t", P.hs, ld_w[js])
                if js >= 2:
                    em.wait("t", P.ss, sild[js - 2])
                    em.wait("t", P.vs, gd[js - 2])
                for k in range(ND128):
                    em.op("t", lambda t, _p=pb[k1], _k=k, _s=s: t.matmul(
                        _p[:], w1p[_s][:, _k * 128:(_k + 1) * 128],
                        xnfm[:, _k * 512:(_k + 1) * 512],
                        start=(_k == 0), stop=False))
                em.op("t", lambda t, _p=pb[k1], _s=s: t.matmul(
                    _p[:], w1p[_s][0:16, 4096:4224], xa1[:],
                    start=False, stop=True), inc=P.ts)
                t1d[js] = P.ts.n
                for k in range(ND128):
                    em.op("t", lambda t, _p=pb[k3], _k=k, _s=s: t.matmul(
                        _p[:], w3p[_s][:, _k * 128:(_k + 1) * 128],
                        xnfm[:, _k * 512:(_k + 1) * 512],
                        start=(_k == 0), stop=False))
                em.op("t", lambda t, _p=pb[k3], _s=s: t.matmul(
                    _p[:], w3p[_s][0:16, 4096:4224], xa3[:],
                    start=False, stop=True), inc=P.ts)
                t3d[js] = P.ts.n
                em.wait("s", P.ts, t1d[js])
                if js >= 2:
                    em.wait("s", P.vs, gd[js - 2])
                em.op("s", lambda sc, _o=h1s[s], _p=pb[k1]: sc.activation(
                    _o[:], _p[:], AF.Silu), inc=P.ss)
                sild[js] = P.ss.n
                em.wait("v", P.ts, t3d[js])
                em.wait("v", P.ss, sild[js])
                em.op("v", lambda v, _i=h1s[s], _p=pb[k3], _js=js: v.tensor_tensor(
                    g_all[:, _js * 512:(_js + 1) * 512], _i[:], _p[:], ALU.mult),
                    inc=P.vs)
                gd[js] = P.vs.n

            if DBG_L1:
                em.wait("g", P.vs, gd[NJT - 1])
                em.dma("g", dbg_g[tt], g_all[:])
                em.wait("g", P.ss)
                em.dma("g", dbg_xa[tt, :, 0:512], xa1[:])
                em.dma("g", dbg_xa[tt, :, 512:1024], xa3[:])
            em.wait("t", P.vs, gd[NJT - 1])
            em.wait("t", P.ss, sild[NJT - 1])
            for hc in range(NJT):
                em.op("t", lambda t, _h=hc: t.matmul(
                    pb[0][0:16, :], a2t[:, _h * 16:(_h + 1) * 16],
                    g_all[:, _h * 512:(_h + 1) * 512],
                    start=(_h == 0), stop=(_h == NJT - 1)),
                    inc=(P.ts if hc == NJT - 1 else None))
            em.wait("s", P.ts)
            em.op("s", lambda sc: sc.activation(ga2[:], pb[0][0:16, :],
                                                AF.Identity, scale=SC_LORA),
                  inc=P.ss)
            ga2_done = P.ss.n
            em.wait("t", P.ss, ga2_done)

            sttd = {}
            for dof in range(8):
                ldx = 0
                for q in range(4):
                    ldx = em.dma("g", xresd[:, q * 512:(q + 1) * 512],
                                 x_ap[tok0 + q * 128:tok0 + (q + 1) * 128,
                                      dof * 512:(dof + 1) * 512])
                grp_n = (NJT + 7) // 8
                ld_g = [0] * grp_n
                tw = [0] * grp_n

                def issue_grp(grp, _dof=dof):
                    h0, h1_ = grp * 8, min((grp + 1) * 8, NJT)
                    nh = h1_ - h0
                    em.dma("y", w2buf[grp % 2][:, 0:nh * 512],
                           w2b[_dof][:, h0:h1_, :])
                    ld_g[grp] = P.hs.n

                em.wait("y", P.ts)
                issue_grp(0)
                issue_grp(1)
                pq = [pb[4 * (dof % 2) + q] for q in range(4)]
                if dof >= 2:
                    for q in range(4):
                        em.wait("t", P.vs, sttd[(dof - 2, q)])
                for grp in range(grp_n):
                    h0, h1_ = grp * 8, min((grp + 1) * 8, NJT)
                    em.wait("t", P.hs, ld_g[grp])
                    for hc in range(h0, h1_):
                        for q in range(4):
                            last = (hc == h1_ - 1 and q == 3)
                            em.op("t", lambda t, _h=hc, _q=q, _g=grp, _p=pq[q]: t.matmul(
                                _p[:],
                                g_all[:, _h * 512 + _q * 128:_h * 512 + (_q + 1) * 128],
                                w2buf[_g % 2][:, (_h - _g * 8) * 512:
                                              (_h - _g * 8 + 1) * 512],
                                start=(_h == 0), stop=False),
                                inc=(P.ts if last else None))
                    tw[grp] = P.ts.n
                    if grp + 2 < grp_n:
                        em.wait("y", P.ts, tw[grp])
                        issue_grp(grp + 2)
                for q in range(4):
                    em.op("t", lambda t, _q=q, _p=pq[q], _d=dof: t.matmul(
                        _p[:], ga2[:, _q * 128:(_q + 1) * 128],
                        b2t[:, _d * 512:(_d + 1) * 512],
                        start=False, stop=True), inc=P.ts)
                    mmq = P.ts.n
                    em.wait("v", P.ts, mmq)
                    em.wait("v", P.ds, ldx)
                    stg = stage[q]
                    em.op("v", lambda v, _q=q, _s=stg, _p=pq[q]: v.scalar_tensor_tensor(
                        _s[:], _p[:], 0.5, xresd[:, _q * 512:(_q + 1) * 512],
                        ALU.mult, ALU.add), inc=P.vs)
                    sttd[(dof, q)] = P.vs.n
                    em.wait("g", P.vs, sttd[(dof, q)])
                    em.dma("g", xo_ap[tok0 + q * 128:tok0 + (q + 1) * 128,
                                      dof * 512:(dof + 1) * 512], stg[:])
            em.flush()
    em.wait("g", P.ds)
    em.wait("g", P.hs)
    em.flush()
    return caches



L0A_STAGE = 9
L0A_DBG = False


def build_l0attn(P, C, x_ap, xo_ap, W):
    em = P.em
    xn = P.dram("xn", [T, D], BF16).ap()
    with scope(P):
        emit_ln(P, x_ap, xn, C)

    cc = P.sem("cc")
    ckv_loc = P.dram("ckvloc", [1, 5 * T], BF16)
    ckv_lrows = ckv_loc.ap().rearrange("a (d t) -> (a d) t", d=5)
    ag = P.dram("ag", [8, 5 * T], BF16)
    m2d = P.dram("m2d", [32, T], BF16).ap()
    ckv5d = P.dram("ckv5d", [8, 128, 80], BF16).ap()

    def ag_rows(d0, d1, b):
        return (ag.ap().rearrange("r (d t) -> d r t", d=5)
                [d0:d1, :, b * 256:(b + 1) * 256])

    wqk = P.sbuf("wqk", [12, 128], BF16)
    wvu = P.sbuf("wvu", [4, D], BF16)
    bqc = P.sbuf("bqc", [12, 1], F32)
    bkvc = P.sbuf("bkvc", [4, 1], F32)
    bqkh = P.sbuf("bqkh", [128, 1], F32)
    bqk4h = P.sbuf("bqk4h", [4, 32], F32)
    bvuC = P.sbuf("bvuC", [128, 32], F32)
    boR = P.sbuf("boR", [1, D], BF16)
    cqT = P.sbuf("cqT", [12, T], BF16)
    neg1r = P.sbuf("neg1r", [1, 4], BF16)
    ones4 = P.sbuf("ones4", [4, 1], BF16)
    onesf = P.sbuf("onesf", [1, 512], F32)
    mc4row = P.sbuf("mc4row", [1, T], BF16)
    em.dma("y", bqc[:], W["bqC"])
    em.dma("y", bkvc[:], W["bkvC"])
    em.dma("y", bqkh[:], W["bqkC"])
    em.dma("y", bqk4h[:], W["bqk4"])
    em.dma("y", bvuC[:], W["bvuC"])
    with scope(P):
        stgA = P.sbuf("stgA", [128, 4096], F32)
        emit_cast_load(P, wqk[:], W["Wqk"], stgA[0:12, 0:128])
        em.wait("y", P.vs)
        emit_cast_load(P, wvu[:], W["WvU"], stgA[0:4, :])
        em.wait("y", P.vs)
        emit_cast_load(P, boR[:], W["boR"], stgA[0:1, :])
    em.op("g", lambda g: g.memset(neg1r[:], -1.0), inc=P.gs)
    em.op("g", lambda g: g.memset(ones4[:], 1.0), inc=P.gs)
    em.op("g", lambda g: g.memset(onesf[:], 1.0), inc=P.gs)
    em.wait("v", P.hs)
    em.wait("v", P.vs)
    em.wait("v", P.gs)
    em.op("v", lambda v: v.tensor_single_scalar(bqkh[:], bqkh[:], 0.5, ALU.mult),
          inc=P.vs)
    em.op("v", lambda v: v.tensor_single_scalar(bqk4h[:], bqk4h[:], 0.5, ALU.mult),
          inc=P.vs)
    s4 = P.sbuf("s4", [128, 32], BF16)
    s4f = P.sbuf("s4f", [128, 32], F32)
    s4g = P.sbuf("s4g", [128, 32], F32)
    em.op("g", lambda g: g.iota(s4f[:], [[4, 32]], channel_multiplier=-1,
                                allow_small_or_imprecise_dtypes=True), inc=P.gs)
    em.wait("v", P.gs)
    em.op("v", lambda v: v.tensor_single_scalar(s4g[:], s4f[:], -3.5, ALU.is_gt),
          inc=P.vs)
    em.op("v", lambda v: v.tensor_single_scalar(s4f[:], s4f[:], 0.5, ALU.is_lt),
          inc=P.vs)
    em.wait("v", P.vs)
    em.op("v", lambda v: v.tensor_tensor(s4[:], s4g[:], s4f[:], ALU.mult),
          inc=P.vs)
    em.flush()

    with scope(P):
        xnfm = P.sbuf("xnfm", [128, ND128 * 512], BF16)
        wqL = P.sbuf("wqL", [128, 384], BF16)
        wkvL = P.sbuf("wkvL", [128, 128], BF16)
        ckvT = P.sbuf("ckvT", [4, T], BF16)
        neg1T = P.sbuf("neg1T", [1, T], BF16)
        qfm = P.sbuf("qfm", [128, 512], BF16)
        qsq = P.sbuf("qsq", [128, T], BF16)
        m2sb = P.sbuf("m2sb", [32, T], BF16)
        casq = P.sbuf("casq", [4, T], BF16)
        cnorms = P.sbuf("cnorms", [1, T], F32)
        mcb = P.sbuf("mcb", [1, 2], F32)
        ckvrows = P.sbuf("ckvrows", [5, T], BF16)
        ckv5s = P.sbuf("ckv5s", [128, 80], BF16)
        pA = P.psum("pA", [128, 512], F32)
        pB = P.psum("pB", [128, 512], F32)
        pC = P.psum("pC", [128, 512], F32)
        pTr = P.psum("pTr", [128, 1024], BF16)
        stgB = P.sbuf("stgB", [128, 384], F32)
        emit_cast_load(P, wqL[:], W["WqL"], stgB[:])
        em.wait("y", P.vs)
        emit_cast_load(P, wkvL[:], W["WkvL"], stgB[:, 0:128])
        em.op("g", lambda g: g.memset(neg1T[:], -1.0), inc=P.gs)
        wld = P.vs.n
        em.wait("t", P.vs, wld)
        for tc in range(4):
            em.wait("y", P.ts)
            ldfm = emit_load_xnfm(P, xn, xnfm, tc * 512, 512)
            em.wait("t", P.hs, ldfm)
            em.wait("t", P.ss)
            for k in range(ND128):
                em.op("t", lambda t, _k=k: t.matmul(
                    pA[0:4, :], wkvL[:, _k * 4:(_k + 1) * 4],
                    xnfm[:, _k * 512:(_k + 1) * 512],
                    start=(_k == 0), stop=(_k == ND128 - 1)),
                    inc=(P.ts if k == ND128 - 1 else None))
            em.wait("s", P.ts)
            em.op("s", lambda sc, _tc=tc: sc.activation(
                ckvT[:, _tc * 512:(_tc + 1) * 512], pA[0:4, :], AF.Identity,
                bias=bkvc[:]), inc=P.ss)
            for k in range(ND128):
                em.op("t", lambda t, _k=k: t.matmul(
                    pB[0:12, :], wqL[:, _k * 12:(_k + 1) * 12],
                    xnfm[:, _k * 512:(_k + 1) * 512],
                    start=(_k == 0), stop=(_k == ND128 - 1)),
                    inc=(P.ts if k == ND128 - 1 else None))
            em.wait("s", P.ts)
            em.op("s", lambda sc, _tc=tc: sc.activation(
                cqT[:, _tc * 512:(_tc + 1) * 512], pB[0:12, :], AF.Identity,
                bias=bqc[:]), inc=P.ss)
            cq_done = P.ss.n
            em.wait("t", P.ss, cq_done)
            em.op("t", lambda t, _tc=tc: t.matmul(
                pC[:], wqk[:], cqT[:, _tc * 512:(_tc + 1) * 512],
                start=True, stop=True), inc=P.ts)
            em.wait("s", P.ts)
            em.op("s", lambda sc: sc.activation(
                qfm[:], pC[:], AF.Identity, bias=bqkh[:], scale=0.5), inc=P.ss)
            em.wait("s", P.ss)
            em.op("s", lambda sc, _tc=tc: sc.activation(
                qsq[:, _tc * 512:(_tc + 1) * 512], qfm[:], AF.Square), inc=P.ss)
        if L0A_STAGE <= 0:
            em.wait("g", P.ds)
            em.flush()
            return
        em.wait("y", P.ss)
        em.wait("y", P.gs)
        em.dma("y", ckv_lrows[0:4, :], ckvT[:])
        em.dma("y", ckv_lrows[4:5, :], neg1T[:])
        em.wait("g", P.hs)
        em.wait("g", P.ds)
        em.op("g", lambda g: g.collective_compute(
            "AllGather", ALU.bypass, replica_groups=[list(range(N_CORES))],
            ins=[ckv_loc.ap().opt()], outs=[ag.ap().opt()]).then_inc(cc.h))
        em.wait("g", cc, 1)
        em.wait("y", cc, 1)
        em.wait("t", cc, 1)
        em.wait("s", cc, 1)
        em.wait("v", cc, 1)
        if L0A_STAGE <= 1:
            em.wait("g", P.ds)
            em.flush()
            return
        for b in range(8):
            em.wait("y", P.ts)
            ld = em.dma("y", ckvrows[:], ag_rows(0, 5, b))
            em.wait("s", P.hs, ld)
            em.op("s", lambda sc: sc.activation(casq[:], ckvrows[0:4, :],
                                                AF.Square), inc=P.ss)
            sq_done = P.ss.n
            em.wait("t", P.ss, sq_done)
            em.wait("t", P.vs)
            for tc in range(4):
                em.wait("t", P.vs)
                em.op("t", lambda t, _tc=tc: t.matmul(
                    pA[0:1, :], ones4[:], casq[:, _tc * 512:(_tc + 1) * 512],
                    start=True, stop=True), inc=P.ts)
                nmm = P.ts.n
                em.wait("v", P.ts, nmm)
                em.op("v", lambda v, _tc=tc: v.tensor_copy(
                    cnorms[0:1, _tc * 512:(_tc + 1) * 512], pA[0:1, :]), inc=P.vs)
            em.wait("v", P.vs)
            em.op("v", lambda v: v.reduce_max(mcb[0:1, 0:1], cnorms[:], axis=AX.X),
                  inc=P.vs)
            em.wait("v", P.vs)
            em.op("v", lambda v: v.tensor_single_scalar(
                mcb[0:1, 1:2], mcb[0:1, 0:1], 0.25, ALU.mult), inc=P.vs)
            em.wait("v", P.vs)
            em.op("v", lambda v, _b=b: v.tensor_scalar(
                mc4row[0:1, _b * 256:(_b + 1) * 256], onesf[0:1, 0:256],
                mcb[0:1, 1:2], None, ALU.mult), inc=P.vs)
            for kt in range(16):
                em.op("t", lambda t, _kt=kt: t.transpose(
                    pTr[:, _kt * 8:_kt * 8 + 5],
                    ckvrows[:, _kt * 128:(_kt + 1) * 128], C["eye"][0:5, 0:5]),
                    inc=(P.ts if kt == 15 else None))
            em.wait("v", P.ts)
            em.op("v", lambda v: v.tensor_copy(
                ckv5s[:].rearrange("p (k c) -> p k c", c=5),
                pTr[:, 0:128].rearrange("p (k c) -> p k c", c=8)[:, :, 0:5]),
                inc=P.vs)
            em.wait("y", P.vs)
            em.dma("y", ckv5d[b], ckv5s[:])
        mc_done = P.vs.n
        em.wait("t", P.vs, mc_done)
        em.wait("t", P.ss)
        for tc in range(4):
            em.wait("t", P.vs)
            em.op("t", lambda t, _tc=tc: t.matmul(
                pB[0:32, :], s4[:], qsq[:, _tc * 512:(_tc + 1) * 512],
                start=True, stop=False))
            em.op("t", lambda t, _tc=tc: t.matmul(
                pB[0:32, :], C["ones_row"][0:1, 0:32],
                mc4row[0:1, _tc * 512:(_tc + 1) * 512],
                start=False, stop=True), inc=P.ts)
            m2mm = P.ts.n
            em.wait("v", P.ts, m2mm)
            em.op("v", lambda v, _tc=tc: v.tensor_copy(
                m2sb[:, _tc * 512:(_tc + 1) * 512], pB[0:32, :]), inc=P.vs)
        em.wait("y", P.vs)
        em.dma("y", m2d, m2sb[:])
    if L0A_STAGE <= 2:
        em.wait("g", P.ds)
        em.flush()
        return

    if L0A_DBG:
        dmp = P.nc.dram_tensor("dmp", [64, T], BF16, kind="ExternalOutput").ap()
        em.wait("y", P.ss)
        em.wait("y", P.vs)
        em.dma("y", dmp[0:4, :], ckvT[:])
        em.dma("y", dmp[4:16, :], cqT[:])
        em.dma("y", dmp[16:48, :], m2sb[:])
        em.dma("y", dmp[48:49, :], mc4row[:])
        em.dma("y", dmp[49:50, 0:128], qsq[0:1, 0:128])
        em.wait("y", P.hs)
    with scope(P):
        qa = [P.sbuf("qa", [5, 8192], BF16) for _ in range(2)]
        ckva = [P.sbuf("ckva", [5, T], BF16) for _ in range(2)]
        ckv5 = [P.sbuf("ckv5", [128, 80], BF16) for _ in range(2)]
        pTb = [P.sbuf("pTbuf", [128, 8192], BF16) for _ in range(2)]
        c5sb = P.sbuf("c5sb", [5, 512], F32)
        ell = P.sbuf("ell", [1, 512], F32)
        ellf = P.sbuf("ellf", [1, 512], F32)
        ellr = P.sbuf("ellr", [1, 512], BF16)
        acn = P.sbuf("acn", [4, 512], BF16)
        afm = [P.sbuf("afm", [128, 1024], BF16) for _ in range(32)]
        wob = [P.sbuf("wob", [128, 4096], BF16) for _ in range(2)]
        xres = P.sbuf("xres", [128, 512], F32)
        stage = [P.sbuf("stg", [128, 512], F32) for _ in range(2)]
        pS = [P.psum("pS", [128, 512], F32) for _ in range(3)]
        pC5 = P.psum("pC5", [128, 512], F32)
        pR = P.psum("pR", [128, 512], F32)
        pV = [P.psum("pV", [128, 512], F32) for _ in range(2)]
        pX = P.psum("pX", [128, 512], F32)

        for half in range(2):
            for bi in range(4):
                b = half * 4 + bi
                s = b % 2
                em.wait("t", P.vs)
                em.wait("y", P.ts)
                lda = em.dma("y", ckva[s][:], ag_rows(0, 5, b))
                em.dma("y", qa[s][4:5, :], m2d[:, b * 256:(b + 1) * 256])
                em.dma("y", ckv5[s][:], ckv5d[b])
                ldm = P.hs.n
                em.wait("s", P.ts)
                for h in range(32):
                    em.wait("t", P.ss)
                    em.op("t", lambda t, _h=h, _b=b: t.matmul(
                        pX[0:4, 0:256], wqk[:, _h * 4:(_h + 1) * 4],
                        cqT[:, _b * 256:(_b + 1) * 256],
                        start=True, stop=True), inc=P.ts)
                    qmm = P.ts.n
                    em.wait("s", P.ts, qmm)
                    em.op("s", lambda sc, _s=s, _h=h: sc.activation(
                        qa[_s][0:4, _h * 256:(_h + 1) * 256], pX[0:4, 0:256],
                        AF.Identity, bias=bqk4h[:, _h:_h + 1], scale=0.5),
                        inc=P.ss)
                qa_done = P.ss.n
                em.wait("t", P.ss, qa_done)
                em.wait("t", P.hs, ldm)
                expd = {}
                for hh in range(16):
                    for kt in range(16):
                        pk = pS[kt % 2]
                        if kt >= 2:
                            em.wait("t", P.ss, expd[kt - 2])
                        em.op("t", lambda t, _s=s, _kt=kt, _hh=hh, _pk=pk: t.matmul(
                            _pk[:], ckva[_s][:, _kt * 128:(_kt + 1) * 128],
                            qa[_s][:, _hh * 512:(_hh + 1) * 512],
                            start=True, stop=True), inc=P.ts)
                        smm = P.ts.n
                        em.wait("s", P.ts, smm)
                        em.op("s", lambda sc, _s=s, _kt=kt, _pk=pk: sc.activation(
                            pTb[_s][:, _kt * 512:(_kt + 1) * 512], _pk[:], AF.Exp),
                            inc=P.ss)
                        expd[kt] = P.ss.n
                    exp_done = P.ss.n
                    em.wait("t", P.ss, exp_done)
                    for kt in range(16):
                        em.op("t", lambda t, _s=s, _kt=kt: t.matmul(
                            pC5[0:5, :], ckv5[_s][:, _kt * 5:(_kt + 1) * 5],
                            pTb[_s][:, _kt * 512:(_kt + 1) * 512],
                            start=(_kt == 0), stop=(_kt == 15)),
                            inc=(P.ts if kt == 15 else None))
                    c5_done = P.ts.n
                    em.wait("v", P.ts, c5_done)
                    em.op("v", lambda v: v.tensor_copy(c5sb[:], pC5[0:5, :]),
                          inc=P.vs)
                    em.wait("y", P.vs)
                    em.dma("y", ell[:], c5sb[4:5, :])
                    elld = P.hs.n
                    em.wait("v", P.hs, elld)
                    em.op("v", lambda v: v.reciprocal(ellf[:], ell[:]), inc=P.vs)
                    em.wait("v", P.vs)
                    em.op("v", lambda v: v.tensor_copy(ellr[:], ellf[:]), inc=P.vs)
                    em.wait("t", P.vs)
                    em.op("t", lambda t: t.matmul(pR[0:4, :], neg1r[:], ellr[:],
                                                  start=True, stop=True), inc=P.ts)
                    rbc_done = P.ts.n
                    em.wait("v", P.ts, rbc_done)
                    em.op("v", lambda v: v.tensor_tensor(
                        acn[:], c5sb[0:4, :], pR[0:4, :], ALU.mult), inc=P.vs)
                    acn_done = P.vs.n
                    em.wait("t", P.vs, acn_done)
                    em.wait("t", P.ss)
                    for j in range(2):
                        h = hh * 2 + j
                        em.op("t", lambda t, _h=h, _j=j: t.matmul(
                            pV[_j][:, 0:256], wvu[:, _h * 128:(_h + 1) * 128],
                            acn[:, _j * 256:(_j + 1) * 256],
                            start=True, stop=True), inc=P.ts)
                        vmm = P.ts.n
                        em.wait("s", P.ts, vmm)
                        em.op("s", lambda sc, _h=h, _j=j, _bi=bi: sc.activation(
                            afm[_h][:, _bi * 256:(_bi + 1) * 256],
                            pV[_j][:, 0:256], AF.Identity,
                            bias=bvuC[:, _h:_h + 1]), inc=P.ss)
            if L0A_STAGE <= 3:
                em.flush()
                continue
            em.wait("t", P.vs)
            em.wait("t", P.ss)
            hb = half * 1024
            sttd = {}
            for dof in range(8):
                ldg = [0] * 4
                twd = [0] * 4

                def issue_wo(grp, _dof=dof):
                    em.dma("g", wob[grp % 2][:],
                           W["WoL"][_dof][:, grp * 8:(grp + 1) * 8, :])
                    ldg[grp] = P.ds.n

                em.wait("g", P.ts)
                issue_wo(0)
                issue_wo(1)
                pq = [pS[0], pS[1], pS[2], pC5, pR, pV[0], pV[1], pX]
                if dof >= 1:
                    for tk in range(8):
                        em.wait("t", P.vs, sttd.get((dof - 1, tk), 0))
                for grp in range(4):
                    em.wait("t", P.ds, ldg[grp])
                    for a in range(grp * 8, grp * 8 + 8):
                        ws = (a - grp * 8) * 512
                        for tk in range(8):
                            em.op("t", lambda t, _a=a, _tk=tk, _g=grp, _w=ws,
                                  _hb=hb, _p=pq[tk]: t.matmul(
                                _p[:], afm[_a][:, (_hb // 128 % 8) * 0 + _tk * 128:
                                               (_tk + 1) * 128],
                                wob[_g % 2][:, _w:_w + 512],
                                start=(_a == 0), stop=False),
                                inc=(P.ts if (a == grp * 8 + 7 and tk == 7)
                                     else None))
                    twd[grp] = P.ts.n
                    if grp + 2 < 4:
                        em.wait("g", P.ts, twd[grp])
                        issue_wo(grp + 2)
                for tk in range(8):
                    em.op("t", lambda t, _tk=tk, _d=dof, _p=pq[tk]: t.matmul(
                        _p[:], C["ones_row"][0:1, 0:128],
                        boR[0:1, _d * 512:(_d + 1) * 512],
                        start=False, stop=True), inc=P.ts)
                    mmd = P.ts.n
                    em.wait("s", P.vs)
                    ldx = em.dma("s", xres[:],
                                 x_ap[hb + tk * 128:hb + (tk + 1) * 128,
                                      dof * 512:(dof + 1) * 512])
                    em.wait("v", P.ts, mmd)
                    em.wait("v", P.hs, ldx)
                    stg = stage[tk % 2]
                    em.op("v", lambda v, _s=stg, _p=pq[tk]: v.scalar_tensor_tensor(
                        _s[:], _p[:], 1.0, xres[:], ALU.mult, ALU.add), inc=P.vs)
                    sttd[(dof, tk)] = P.vs.n
                    em.wait("y", P.vs, sttd[(dof, tk)])
                    em.dma("y", xo_ap[hb + tk * 128:hb + (tk + 1) * 128,
                                      dof * 512:(dof + 1) * 512], stg[:])
            em.flush()
    em.wait("g", P.ds)
    em.wait("g", P.hs)
    em.flush()



def build_l1attn(P, C, x_ap, xo_ap, W):
    em = P.em
    xn = P.dram("xn", [T, D], BF16).ap()
    with scope(P):
        emit_ln(P, x_ap, xn, C)

    wdT = P.sbuf("wdT", [128, 4096], BF16)
    wuT = P.sbuf("wuT", [128, 4096], BF16)
    inwT = P.sbuf("inwT", [128, 552], BF16)
    outwT = P.sbuf("outwT", [128, 256], BF16)
    bdR = P.sbuf("bdR", [1, 128], BF16)
    buR = P.sbuf("buR", [1, 4096], BF16)
    rows = P.sbuf("rows", [1, 3840], F32)
    onesf32 = P.sbuf("onesf32", [1, 128], F32)
    em.dma("g", wdT[:], W["wdT"])
    em.dma("g", wuT[:], W["wuT"])
    em.dma("g", inwT[:], W["inwT"])
    em.dma("g", outwT[:], W["outwT"])
    em.dma("g", bdR[:], W["bdR"])
    em.dma("g", buR[:], W["buR"])
    em.dma("g", rows[0:1, 0:2016], W["convw7R"])
    em.dma("g", rows[0:1, 2016:2304], W["convbR"])
    em.dma("g", rows[0:1, 2304:2432], W["dtb16R"])
    em.dma("g", rows[0:1, 2432:2436], W["AlogR"])
    em.dma("g", rows[0:1, 2436:2440], W["DR"])
    em.dma("g", rows[0:1, 2440:2441], W["gateR"])
    em.dma("g", rows[0:1, 2560:3584], W["fcdR"])
    em.dma("g", rows[0:1, 3584:3840], W["normwR"])
    em.op("g", lambda g: g.memset(onesf32[:], 1.0), inc=P.gs)
    ld0 = P.ds.n
    em.flush()

    wcb = P.sbuf("wcb", [128, 2016], F32)
    cvb = P.sbuf("cvb", [128, 288], F32)
    dtb16 = P.sbuf("dtb16", [128, 128], F32)
    A7 = P.sbuf("A7", [128, 28], F32)
    Dbc = P.sbuf("Dbc", [128, 4], F32)
    fcdb = P.sbuf("fcdb", [128, 1024], F32)
    nwbc = P.sbuf("nwbc", [128, 256], F32)
    a0c = P.sbuf("a0c", [128, 1], F32)
    a1c = P.sbuf("a1c", [128, 1], F32)
    scrr = P.sbuf("scrr", [1, 64], F32)
    pb0 = P.psum("pcb", [128, 512], F32)
    em.wait("s", P.ds, ld0)
    em.wait("s", P.gs)
    em.op("s", lambda sc: sc.activation(scrr[0:1, 0:4], rows[0:1, 2432:2436],
                                        AF.Exp), inc=P.ss)
    em.op("s", lambda sc: sc.activation(scrr[0:1, 8:9], rows[0:1, 2440:2441],
                                        AF.Exp, scale=-1.0), inc=P.ss)
    em.wait("v", P.ss)
    em.op("v", lambda v: v.tensor_single_scalar(scrr[0:1, 0:4], scrr[0:1, 0:4],
                                                -1.0, ALU.mult), inc=P.vs)
    for r in range(7):
        em.wait("v", P.vs)
        em.op("v", lambda v, _r=r: v.tensor_copy(
            scrr[0:1, 16 + _r * 4:16 + (_r + 1) * 4], scrr[0:1, 0:4]), inc=P.vs)
    em.op("v", lambda v: v.tensor_single_scalar(scrr[0:1, 9:10], scrr[0:1, 8:9],
                                                1.0, ALU.add), inc=P.vs)
    em.wait("v", P.vs)
    em.op("v", lambda v: v.reciprocal(scrr[0:1, 10:11], scrr[0:1, 9:10]),
          inc=P.vs)
    em.wait("v", P.vs)
    em.op("v", lambda v: v.tensor_single_scalar(scrr[0:1, 11:12], scrr[0:1, 10:11],
                                                1.0, ALU.add), inc=P.vs)
    em.op("v", lambda v: v.tensor_scalar(scrr[0:1, 12:13], scrr[0:1, 10:11],
                                         -1.0, 1.0, ALU.mult, ALU.add), inc=P.vs)
    cst = P.vs.n
    em.wait("t", P.vs, cst)
    bcasts = [
        (wcb, 0, 2016), (cvb, 2016, 288), (dtb16, 2304, 128),
        (Dbc, 2436, 4), (fcdb, 2560, 1024), (nwbc, 3584, 256),
    ]
    for dst, off, n in bcasts:
        for c0 in range(0, n, 512):
            w = min(512, n - c0)
            em.op("t", lambda t, _o=off + c0, _w=w: t.matmul(
                pb0[:, 0:_w], onesf32[:], rows[0:1, _o:_o + _w],
                start=True, stop=True), inc=P.ts)
            mm = P.ts.n
            em.wait("v", P.ts, mm)
            em.op("v", lambda v, _d=dst, _c=c0, _w=w: v.tensor_copy(
                _d[:, _c:_c + _w], pb0[:, 0:_w]), inc=P.vs)
            em.wait("t", P.vs)
    for dst, off, n in [(A7, 16, 28), (a1c, 11, 1), (a0c, 12, 1)]:
        em.op("t", lambda t, _o=off, _n=n: t.matmul(
            pb0[:, 0:_n], onesf32[:], scrr[0:1, _o:_o + _n],
            start=True, stop=True), inc=P.ts)
        mm = P.ts.n
        em.wait("v", P.ts, mm)
        em.op("v", lambda v, _d=dst, _n=n: v.tensor_copy(_d[:], pb0[:, 0:_n]),
              inc=P.vs)
        em.wait("t", P.vs)
    em.flush()

    u0fm = P.sbuf("u0fm", [128, T], BF16)
    u1fm = P.sbuf("u1fm", [128, T], BF16)
    h2fm = P.sbuf("h2fm", [128, T], BF16)
    mixfm = P.sbuf("mixfm", [128, T], BF16)
    zx = P.sbuf("zx", [128, 16 * 552], BF16)
    dt2f = P.sbuf("dt2f", [128, 128], F32)
    dtw = P.sbuf("dtw", [128, 128], F32)
    dts = P.sbuf("dts", [128, 128], F32)
    xc = P.sbuf("xc", [128, 16 * 288], F32)
    zs = P.sbuf("zs", [128, 4096], F32)
    ya = P.sbuf("ya", [128, 4096], F32)
    ynb = P.sbuf("ynb", [128, 4096], BF16)
    yfm = [P.sbuf("yfm", [128, T], BF16) for _ in range(2)]
    Scum = P.sbuf("Scum", [128, 64], F32)
    Rcum = P.sbuf("Rcum", [128, 64], F32)
    bcd = P.sbuf("bcd", [128, 128], F32)
    scr28 = P.sbuf("scr28", [128, 28], F32)
    scr28b = P.sbuf("scr28b", [128, 28], F32)
    fcdc = P.sbuf("fcdc", [128, 8], F32)
    rmsc = P.sbuf("rmsc", [128, 2], F32)
    scrT = P.sbuf("scrT", [128, 288], F32)
    pZ = P.psum("pZ", [128, 512], F32)
    pZ2 = P.psum("pZ2", [128, 512], F32)
    pD = P.psum("pD", [128, 512], F32)
    pO = P.psum("pO", [128, 512], F32)
    pTb = P.psum("pTb2", [128, 1024], BF16)

    def zxs(i, c0, c1):
        return zx[:, i * 552 + c0:i * 552 + c1]

    def xcs(i, c0, c1):
        return xc[:, i * 288 + c0:i * 288 + c1]

    with scope(P):
        xnfm = P.sbuf("xnfmA", [128, ND128 * 512], BF16)
        u0tok = P.sbuf("u0tok", [128, 128], BF16)
        for tc in range(4):
            em.wait("y", P.ts)
            ldfm = emit_load_xnfm(P, xn, xnfm, tc * 512, 512)
            em.wait("t", P.hs, ldfm)
            for i4 in range(4):
                i = tc * 4 + i4
                em.wait("t", P.ss)
                em.wait("t", P.vs)
                for k in range(ND128):
                    em.op("t", lambda t, _k=k, _i4=i4: t.matmul(
                        pD[:, 0:128],
                        xnfm[:, _k * 512 + _i4 * 128:_k * 512 + (_i4 + 1) * 128],
                        wdT[:, _k * 128:(_k + 1) * 128],
                        start=(_k == 0), stop=False))
                em.op("t", lambda t: t.matmul(
                    pD[:, 0:128], C["ones_row"][0:1, 0:128], bdR[:],
                    start=False, stop=True), inc=P.ts)
                dmm = P.ts.n
                em.wait("s", P.ts, dmm)
                em.op("s", lambda sc: sc.activation(u0tok[:], pD[:, 0:128],
                                                    AF.Gelu), inc=P.ss)
                gel = P.ss.n
                em.wait("t", P.ss, gel)
                em.op("t", lambda t: t.transpose(pTb[:, 0:128], u0tok[:],
                                                 C["eye"][:]), inc=P.ts)
                tp = P.ts.n
                em.wait("v", P.ts, tp)
                em.op("v", lambda v, _i=i: v.tensor_copy(
                    u0fm[:, _i * 128:(_i + 1) * 128], pTb[:, 0:128]), inc=P.vs)
        em.flush()

    def hydra(ufm, outbuf):
        em.wait("t", P.vs)
        em.wait("t", P.ss)
        for i in range(16):
            pz = pZ if i % 2 == 0 else pZ2
            if i >= 1:
                em.wait("t", P.vs)
            em.op("t", lambda t, _i=i, _p=pz: t.matmul(
                _p[:], ufm[:, _i * 128:(_i + 1) * 128], inwT[:, 0:512],
                start=True, stop=True), inc=P.ts)
            em.op("t", lambda t, _i=i, _p=pz: t.matmul(
                pD[:, 0:40] if False else pD[:, 0:40],
                ufm[:, _i * 128:(_i + 1) * 128], inwT[:, 512:552],
                start=True, stop=True), inc=P.ts)
            zmm = P.ts.n
            em.wait("v", P.ts, zmm)
            em.op("v", lambda v, _i=i, _p=pz: v.tensor_copy(
                zxs(_i, 0, 512), _p[:]), inc=P.vs)
            em.op("v", lambda v, _i=i: v.tensor_copy(
                zxs(_i, 512, 544), pD[:, 0:32]), inc=P.vs)
            em.op("v", lambda v, _i=i: v.tensor_copy(
                dt2f[:, _i * 8:(_i + 1) * 8], pD[:, 32:40]), inc=P.vs)
        inproj_done = P.vs.n
        em.wait("v", P.vs, inproj_done)
        em.op("v", lambda v: v.tensor_tensor(
            dtw[:], dt2f[:], dtb16[:], ALU.add), inc=P.vs)
        dtw_done = P.vs.n
        em.wait("s", P.vs, dtw_done)
        em.op("s", lambda sc: sc.activation(dts[:], dtw[:], AF.Abs))
        em.op("s", lambda sc: sc.activation(dts[:], dts[:], AF.Exp, scale=-1.0))
        em.op("s", lambda sc: sc.activation(dts[:], dts[:], AF.Ln, bias=1.0))
        em.op("s", lambda sc: sc.activation(dtw[:], dtw[:], AF.Relu), inc=P.ss)
        sp_done = P.ss.n
        em.wait("v", P.ss, sp_done)
        em.op("v", lambda v: v.tensor_tensor(dts[:], dts[:], dtw[:], ALU.add),
              inc=P.vs)
        dts_done = P.vs.n
        em.wait("v", P.vs, dts_done)
        for i in range(16):
            l, sh = i // 2, i % 2
            taps = [(j, (l - 6 + j) * 2 + sh) for j in range(7) if l - 6 + j >= 0]
            first = True
            for (j, ti) in taps:
                if first:
                    em.op("v", lambda v, _i=i, _j=j, _ti=ti: v.tensor_tensor(
                        xcs(_i, 0, 288), zxs(_ti, 256, 544),
                        wcb[:, _j * 288:(_j + 1) * 288], ALU.mult), inc=P.vs)
                    first = False
                else:
                    em.op("v", lambda v, _i=i, _j=j, _ti=ti: v.tensor_tensor(
                        scrT[:], zxs(_ti, 256, 544),
                        wcb[:, _j * 288:(_j + 1) * 288], ALU.mult), inc=P.vs)
                    em.wait("v", P.vs)
                    em.op("v", lambda v, _i=i: v.tensor_tensor(
                        xcs(_i, 0, 288), scrT[:], xcs(_i, 0, 288), ALU.add),
                        inc=P.vs)
                    em.wait("v", P.vs)
            em.op("v", lambda v, _i=i: v.tensor_tensor(
                xcs(_i, 0, 288), xcs(_i, 0, 288), cvb[:], ALU.add), inc=P.vs)
            em.wait("v", P.vs)
        conv_done = P.vs.n
        em.wait("s", P.vs, conv_done)
        em.op("s", lambda sc: sc.activation(xc[:], xc[:], AF.Silu))
        em.op("s", lambda sc: sc.activation(
            zs[:].rearrange("p (i c) -> p i c", c=256),
            zx[:].rearrange("p (i c) -> p i c", c=552)[:, :, 0:256], AF.Silu),
            inc=P.ss)
        silu_done = P.ss.n
        em.wait("v", P.ss, silu_done)
        for sh in range(2):
            for l in range(8):
                i = l * 2 + sh
                if l == 0:
                    em.op("v", lambda v, _i=i: v.tensor_copy(
                        Scum[:, _i * 4:(_i + 1) * 4], dts[:, _i * 8:_i * 8 + 4]),
                        inc=P.vs)
                else:
                    em.op("v", lambda v, _i=i: v.tensor_tensor(
                        Scum[:, _i * 4:(_i + 1) * 4],
                        Scum[:, (_i - 2) * 4:(_i - 1) * 4],
                        dts[:, _i * 8:_i * 8 + 4], ALU.add), inc=P.vs)
                em.wait("v", P.vs)
            for l in range(7, -1, -1):
                i = l * 2 + sh
                if l == 7:
                    em.op("v", lambda v, _i=i: v.tensor_copy(
                        Rcum[:, _i * 4:(_i + 1) * 4],
                        dts[:, _i * 8 + 4:_i * 8 + 8]), inc=P.vs)
                else:
                    em.op("v", lambda v, _i=i: v.tensor_tensor(
                        Rcum[:, _i * 4:(_i + 1) * 4],
                        Rcum[:, (_i + 2) * 4:(_i + 3) * 4],
                        dts[:, _i * 8 + 4:_i * 8 + 8], ALU.add), inc=P.vs)
                em.wait("v", P.vs)
        for sh in range(2):
            for s2 in range(8):
                for tau in range(8):
                    em.op("v", lambda v, _sh=sh, _s2=s2, _tau=tau:
                          v.scalar_tensor_tensor(
                        scrT[:, 0:16], xcs(_s2 * 2 + _sh, 256, 272), 1.0,
                        xcs(_tau * 2 + _sh, 272, 288), ALU.mult, ALU.mult,
                        accum_out=bcd[:, _sh * 64 + _s2 * 8 + _tau:
                                      _sh * 64 + _s2 * 8 + _tau + 1]), inc=P.vs)
        bcd_done = P.vs.n
        em.wait("g", P.vs, bcd_done)
        em.op("g", lambda g: g.memset(ya[:], 0.0), inc=P.gs)
        em.wait("v", P.gs)
        em.wait("v", P.vs, bcd_done)
        for sh in range(2):
            for t_ in range(8):
                s2s = [s2 for s2 in range(8) if s2 != t_]
                for idx, s2 in enumerate(s2s):
                    if s2 < t_:
                        em.op("v", lambda v, _x=idx, _s2=s2, _t=t_, _sh=sh:
                              v.tensor_tensor(
                            scr28[:, _x * 4:(_x + 1) * 4],
                            Scum[:, ((_t - 1) * 2 + _sh) * 4:
                                 ((_t - 1) * 2 + _sh + 1) * 4],
                            Scum[:, (_s2 * 2 + _sh) * 4:(_s2 * 2 + _sh + 1) * 4],
                            ALU.subtract), inc=P.vs)
                    else:
                        em.op("v", lambda v, _x=idx, _s2=s2, _t=t_, _sh=sh:
                              v.tensor_tensor(
                            scr28[:, _x * 4:(_x + 1) * 4],
                            Rcum[:, ((_t + 1) * 2 + _sh) * 4:
                                 ((_t + 1) * 2 + _sh + 1) * 4],
                            Rcum[:, (_s2 * 2 + _sh) * 4:(_s2 * 2 + _sh + 1) * 4],
                            ALU.subtract), inc=P.vs)
                em.wait("v", P.vs)
                em.op("v", lambda v: v.tensor_tensor(scr28[:], scr28[:], A7[:],
                                                     ALU.mult), inc=P.vs)
                coef_pre = P.vs.n
                em.wait("s", P.vs, coef_pre)
                em.op("s", lambda sc: sc.activation(scr28b[:], scr28[:], AF.Exp),
                      inc=P.ss)
                em.wait("v", P.ss)
                for idx, s2 in enumerate(s2s):
                    dcol = 0 if s2 < t_ else 4
                    em.op("v", lambda v, _x=idx, _s2=s2, _sh=sh, _d=dcol:
                          v.tensor_tensor(
                        scr28b[:, _x * 4:(_x + 1) * 4],
                        scr28b[:, _x * 4:(_x + 1) * 4],
                        dts[:, (_s2 * 2 + _sh) * 8 + _d:
                             (_s2 * 2 + _sh) * 8 + _d + 4], ALU.mult), inc=P.vs)
                    tau = t_ - 1 if s2 < t_ else t_ + 1
                    em.wait("v", P.vs)
                    em.op("v", lambda v, _x=idx, _s2=s2, _sh=sh, _tau=tau:
                          v.tensor_scalar(
                        scr28b[:, _x * 4:(_x + 1) * 4],
                        scr28b[:, _x * 4:(_x + 1) * 4],
                        bcd[:, _sh * 64 + _s2 * 8 + _tau:
                            _sh * 64 + _s2 * 8 + _tau + 1], None, ALU.mult),
                        inc=P.vs)
                    em.wait("v", P.vs)
                    ti = t_ * 2 + sh
                    si = s2 * 2 + sh
                    for h in range(4):
                        em.op("v", lambda v, _x=idx, _h=h, _ti=ti, _si=si:
                              v.scalar_tensor_tensor(
                            ya[:, _ti * 256 + _h * 64:_ti * 256 + (_h + 1) * 64],
                            xcs(_si, _h * 64, (_h + 1) * 64),
                            scr28b[:, _x * 4 + _h:_x * 4 + _h + 1],
                            ya[:, _ti * 256 + _h * 64:_ti * 256 + (_h + 1) * 64],
                            ALU.mult, ALU.add), inc=P.vs)
                    em.wait("v", P.vs)
        for i in range(16):
            for h in range(4):
                em.op("v", lambda v, _i=i, _h=h: v.scalar_tensor_tensor(
                    scrT[:, 0:256], xcs(_i, 0, 256), 1.0,
                    fcdb[:, _h * 256:(_h + 1) * 256], ALU.mult, ALU.mult,
                    accum_out=fcdc[:, _h:_h + 1]), inc=P.vs)
            em.wait("v", P.vs)
            em.op("v", lambda v: v.tensor_tensor(fcdc[:, 4:8], fcdc[:, 0:4],
                                                 Dbc[:], ALU.add), inc=P.vs)
            em.wait("v", P.vs)
            for h in range(4):
                em.op("v", lambda v, _i=i, _h=h: v.scalar_tensor_tensor(
                    ya[:, _i * 256 + _h * 64:_i * 256 + (_h + 1) * 64],
                    xcs(_i, _h * 64, (_h + 1) * 64), fcdc[:, 4 + _h:5 + _h],
                    ya[:, _i * 256 + _h * 64:_i * 256 + (_h + 1) * 64],
                    ALU.mult, ALU.add), inc=P.vs)
            em.wait("v", P.vs)
            em.op("v", lambda v, _i=i: v.tensor_tensor(
                ya[:, _i * 256:(_i + 1) * 256], ya[:, _i * 256:(_i + 1) * 256],
                zs[:, _i * 256:(_i + 1) * 256], ALU.mult), inc=P.vs)
            em.wait("v", P.vs)
            em.op("v", lambda v, _i=i: v.scalar_tensor_tensor(
                scrT[:, 0:256], ya[:, _i * 256:(_i + 1) * 256], 1.0,
                ya[:, _i * 256:(_i + 1) * 256], ALU.mult, ALU.mult,
                accum_out=rmsc[:, 0:1]), inc=P.vs)
            ssq_done = P.vs.n
            em.wait("s", P.vs, ssq_done)
            em.op("s", lambda sc: sc.activation(rmsc[:, 1:2], rmsc[:, 0:1],
                                                AF.Ln, scale=1.0 / 256,
                                                bias=C["eps"][:]))
            em.op("s", lambda sc: sc.activation(rmsc[:, 1:2], rmsc[:, 1:2],
                                                AF.Exp, scale=-0.5), inc=P.ss)
            rstd_done = P.ss.n
            em.wait("v", P.ss, rstd_done)
            em.op("v", lambda v, _i=i: v.scalar_tensor_tensor(
                ynb[:, _i * 256:(_i + 1) * 256], ya[:, _i * 256:(_i + 1) * 256],
                rmsc[:, 1:2], nwbc[:], ALU.mult, ALU.mult), inc=P.vs)
            ynb_done = P.vs.n
            em.wait("t", P.vs, ynb_done)
            for c in range(2):
                em.op("t", lambda t, _i=i, _c=c: t.transpose(
                    pTb[:, _c * 128:(_c + 1) * 128],
                    ynb[:, _i * 256 + _c * 128:_i * 256 + (_c + 1) * 128],
                    C["eye"][:]), inc=(P.ts if c == 1 else None))
            tp_done = P.ts.n
            em.wait("v", P.ts, tp_done)
            for c in range(2):
                em.op("v", lambda v, _i=i, _c=c: v.tensor_copy(
                    yfm[_c][:, _i * 128:(_i + 1) * 128],
                    pTb[:, _c * 128:(_c + 1) * 128]), inc=P.vs)
            em.wait("t", P.vs)
        yfm_done = P.vs.n
        em.wait("t", P.vs, yfm_done)
        for c4 in range(4):
            em.op("t", lambda t, _c=c4: t.matmul(
                pO[:], outwT[:, 0:128], yfm[0][:, _c * 512:(_c + 1) * 512],
                start=True, stop=False))
            em.op("t", lambda t, _c=c4: t.matmul(
                pO[:], outwT[:, 128:256], yfm[1][:, _c * 512:(_c + 1) * 512],
                start=False, stop=True), inc=P.ts)
            omm = P.ts.n
            em.wait("v", P.ts, omm)
            em.op("v", lambda v, _c=c4: v.tensor_copy(
                outbuf[:, _c * 512:(_c + 1) * 512], pO[:]), inc=P.vs)
            em.wait("t", P.vs)
        em.flush()

    hydra(u0fm, u1fm)
    hydra(u1fm, h2fm)

    with scope(P):
        xres = P.sbuf("xresB", [128, 512], F32)
        xnr = P.sbuf("xnrB", [128, 512], BF16)
        stage = [P.sbuf("stgB", [128, 512], F32) for _ in range(2)]
        em.wait("v", P.vs)
        em.op("v", lambda v: v.tensor_scalar(
            mixfm[:], u0fm[:], a1c[:, 0:1], None, ALU.mult), inc=P.vs)
        em.wait("v", P.vs)
        em.op("v", lambda v: v.scalar_tensor_tensor(
            mixfm[:], h2fm[:], a0c[:, 0:1], mixfm[:], ALU.mult, ALU.add),
            inc=P.vs)
        mix_done = P.vs.n
        em.wait("t", P.vs, mix_done)
        for i in range(16):
            for hc in range(8):
                pz = pZ if (i * 8 + hc) % 2 == 0 else pZ2
                if i * 8 + hc >= 2:
                    em.wait("t", P.vs)
                em.op("t", lambda t, _i=i, _h=hc, _p=pz: t.matmul(
                    _p[:], mixfm[:, _i * 128:(_i + 1) * 128],
                    wuT[:, _h * 512:(_h + 1) * 512],
                    start=True, stop=False))
                em.op("t", lambda t, _h=hc, _p=pz: t.matmul(
                    _p[:], C["ones_row"][0:1, 0:128],
                    buR[0:1, _h * 512:(_h + 1) * 512],
                    start=False, stop=True), inc=P.ts)
                umm = P.ts.n
                ldx = em.dma("g", xres[:],
                             x_ap[i * 128:(i + 1) * 128,
                                  hc * 512:(hc + 1) * 512])
                ldn = em.dma("g", xnr[:],
                             xn[i * 128:(i + 1) * 128,
                                hc * 512:(hc + 1) * 512])
                stg = stage[(i * 8 + hc) % 2]
                em.wait("v", P.ts, umm)
                em.wait("v", P.ds, ldn)
                em.op("v", lambda v, _p=pz, _s=stg: v.scalar_tensor_tensor(
                    _s[:], _p[:], 1.0, xres[:], ALU.mult, ALU.add), inc=P.vs)
                em.wait("v", P.vs)
                em.op("v", lambda v, _s=stg: v.tensor_tensor(
                    _s[:], _s[:], xnr[:], ALU.add), inc=P.vs)
                sdone = P.vs.n
                em.wait("g", P.vs, sdone)
                em.dma("g", xo_ap[i * 128:(i + 1) * 128,
                                  hc * 512:(hc + 1) * 512], stg[:])
        em.flush()
    em.wait("g", P.ds)
    em.wait("g", P.hs)
    em.flush()


def _pm(a):
    return np.ascontiguousarray(a, dtype=np.float32)


def _w13L(w, bT):
    base = np.asarray(w).T.reshape(32, 128, 86, 128).transpose(2, 1, 0, 3) \
        .reshape(86, 128, 4096)
    out = np.zeros((86, 128, 4224), np.float32)
    out[:, :, :4096] = base
    bt = np.asarray(bT).reshape(86, 128, 16).transpose(0, 2, 1)
    out[:, 0:16, 4096:4224] = bt
    return _pm(out)


def _w2L(w):
    return _pm(np.asarray(w).T.reshape(86, 128, 8, 512).transpose(2, 1, 0, 3))


HOST_LAYOUTS = {
    "l1ffn": [
        ("w1L", (86, 128, 4224), lambda i: _w13L(i["L1_ffn_w1"], i["L1_ffn_lora_B1"])),
        ("w3L", (86, 128, 4224), lambda i: _w13L(i["L1_ffn_w3"], i["L1_ffn_lora_B3"])),
        ("w2L", (8, 128, 86, 512), lambda i: _w2L(i["L1_ffn_w2"])),
        ("a1tL", (128, 512), lambda i: _pm(np.asarray(i["L1_ffn_lora_A1"]).T
            .reshape(32, 128, 16).transpose(1, 0, 2).reshape(128, 512))),
        ("a3tL", (128, 512), lambda i: _pm(np.asarray(i["L1_ffn_lora_A3"]).T
            .reshape(32, 128, 16).transpose(1, 0, 2).reshape(128, 512))),
        ("a2tL", (128, 1376), lambda i: _pm(np.asarray(i["L1_ffn_lora_A2"]).T
            .reshape(86, 128, 16).transpose(1, 0, 2).reshape(128, 1376))),
        ("b2T", (16, 4096), lambda i: _pm(np.asarray(i["L1_ffn_lora_B2"]).T)),
    ],
    "l1attn": [
        ("wdT", (128, 4096), lambda i: _pm(np.asarray(i["L1_attn_w_down"]).T
            .reshape(32, 128, 128).transpose(1, 0, 2).reshape(128, 4096))),
        ("wuT", (128, 4096), lambda i: _pm(np.asarray(i["L1_attn_w_up"]).T)),
        ("inwT", (128, 552), lambda i: _pm(np.asarray(i["L1_attn_hy_in_w"]).T)),
        ("outwT", (128, 256), lambda i: _pm(np.asarray(i["L1_attn_hy_out_w"]).T
            .reshape(2, 128, 128).transpose(1, 0, 2).reshape(128, 256))),
        ("bdR", (1, 128), lambda i: _pm(np.asarray(i["L1_attn_b_down"])
            .reshape(1, 128))),
        ("buR", (1, 4096), lambda i: _pm(np.asarray(i["L1_attn_b_up"])
            .reshape(1, 4096))),
        ("convw7R", (1, 2016), lambda i: _pm(np.asarray(i["L1_attn_hy_conv_w"])
            [:, 0, :].T.reshape(1, 2016))),
        ("convbR", (1, 288), lambda i: _pm(np.asarray(i["L1_attn_hy_conv_b"])
            .reshape(1, 288))),
        ("dtb16R", (1, 128), lambda i: _pm(np.tile(
            np.asarray(i["L1_attn_hy_dt_bias"]), 32).reshape(1, 128))),
        ("AlogR", (1, 4), lambda i: _pm(np.asarray(i["L1_attn_hy_A_log"])
            .reshape(1, 4))),
        ("DR", (1, 4), lambda i: _pm(np.asarray(i["L1_attn_hy_D"]).reshape(1, 4))),
        ("gateR", (1, 1), lambda i: _pm(np.asarray(i["L1_attn_gate1"])
            .reshape(1, 1))),
        ("fcdR", (1, 1024), lambda i: _pm(np.asarray(i["L1_attn_hy_fcD_w"])
            .reshape(1, 1024))),
        ("normwR", (1, 256), lambda i: _pm(np.asarray(i["L1_attn_hy_norm_w"])
            .reshape(1, 256))),
    ],
    "l0attn": [
        ("WqL", (128, 384), lambda i: _pm(np.asarray(i["L0_attn_Wq_d"])
            .reshape(32, 128, 12).transpose(1, 0, 2).reshape(128, 384))),
        ("WkvL", (128, 128), lambda i: _pm(np.asarray(i["L0_attn_Wkv_d"])
            .reshape(32, 128, 4).transpose(1, 0, 2).reshape(128, 128))),
        ("Wqk", (12, 128), lambda i: _pm(i["L0_attn_W_qk"])),
        ("WvU", (4, 4096), lambda i: _pm(i["L0_attn_Wv_u"])),
        ("WoL", (8, 128, 32, 512), lambda i: _pm(np.asarray(i["L0_attn_Wo"])
            .reshape(32, 128, 8, 512).transpose(2, 1, 0, 3))),
        ("bqC", (12, 1), lambda i: _pm(np.asarray(i["L0_attn_bq_d"]).reshape(12, 1))),
        ("bkvC", (4, 1), lambda i: _pm(np.asarray(i["L0_attn_bkv_d"]).reshape(4, 1))),
        ("bqkC", (128, 1), lambda i: _pm(np.asarray(i["L0_attn_b_qk"]).reshape(128, 1))),
        ("bqk4", (4, 32), lambda i: _pm(np.asarray(i["L0_attn_b_qk"]).reshape(32, 4).T)),
        ("bvuC", (128, 32), lambda i: _pm(np.asarray(i["L0_attn_bv_u"])
            .reshape(32, 128).T)),
        ("boR", (1, 4096), lambda i: _pm(np.asarray(i["L0_attn_bo"]).reshape(1, 4096))),
    ],
    "l0ffn": [
        ("rwT", (128, 256), lambda i: _pm(np.asarray(i["L0_ffn_route_w"]).T
            .reshape(32, 128, 8).transpose(1, 0, 2).reshape(128, 256))),
        ("AtL", (128, 4096), lambda i: _pm(np.asarray(i["L0_ffn_A"]).reshape(128, 4096).T
            .reshape(32, 128, 128).transpose(1, 0, 2).reshape(128, 4096))),
        ("BcatL", (128, 4096), lambda i: _pm(np.asarray(i["L0_ffn_Bw"])
            .transpose(0, 2, 1).reshape(128, 4096))),
    ],
}

BUILDERS = {}


def build_sub_program(sub):
    P = Prog()
    x_in = P.nc.dram_tensor("x_in", [T, D], F32, kind="ExternalInput").ap()
    x_out = P.nc.dram_tensor("x_out", [T, D], F32, kind="ExternalOutput").ap()
    W = {}
    for lname, shape, _fn in HOST_LAYOUTS[sub]:
        W[lname] = P.nc.dram_tensor(lname, list(shape), F32,
                                    kind="ExternalInput").ap()
    C = emit_setup(P)
    base = sub.rstrip("2")
    BUILDERS[base](P, C, x_in, x_out, W)
    P.stack.close()
    return P.nc


def shard_x(x_full):
    out = []
    for c in range(N_CORES):
        out.append(np.ascontiguousarray(
            x_full[:, c * 256:(c + 1) * 256, :]).reshape(T, D))
    return out


def unshard_x(slices):
    full = np.empty((8, 2048, 4096), np.float32)
    for c in range(N_CORES):
        full[:, c * 256:(c + 1) * 256, :] = slices[c].reshape(8, 256, D)
    return full


def run_sublayer(sub, x_full, inputs):
    nc = build_sub_program(sub)
    xs = shard_x(np.asarray(x_full, np.float32))
    wmap = {lname: fn(inputs) for lname, _s, fn in HOST_LAYOUTS[sub]}
    maps = [dict(x_in=xs[c], **wmap) for c in range(N_CORES)]
    res = run_bass_kernel_spmd(nc, maps, core_ids=list(range(N_CORES)))
    return unshard_x([np.asarray(res.results[c]["x_out"]) for c in range(N_CORES)])


BUILDERS["l0ffn"] = build_l0ffn
BUILDERS["l1ffn"] = build_l1ffn
BUILDERS["l0attn"] = build_l0attn
BUILDERS["l1attn"] = build_l1attn

SUB_ORDER = ["l0ffn", "l0attn", "l0ffn", "l1ffn", "l1attn", "l1ffn"]

_prog_cache = {}


def build_fused():
    P = Prog()
    x_in = P.nc.dram_tensor("x_in", [T, D], F32, kind="ExternalInput").ap()
    x_out = P.nc.dram_tensor("x_out", [T, D], F32, kind="ExternalOutput").ap()
    W = {}
    for sub in ("l0ffn", "l0attn", "l1ffn", "l1attn"):
        W[sub] = {ln: P.nc.dram_tensor(ln, list(shape), F32,
                                       kind="ExternalInput").ap()
                  for ln, shape, _fn in HOST_LAYOUTS[sub]}
    C = emit_setup(P)
    bufs = [P.dram(f"xb{i}", [T, D], F32).ap()
            for i in range(len(SUB_ORDER) - 1)]
    chain = [x_in] + bufs + [x_out]
    caches = emit_l1ffn_caches(P, W["l1ffn"]) if "l1ffn" in SUB_ORDER else None
    P.em.flush()
    for i, sub in enumerate(SUB_ORDER):
        with scope(P):
            if sub == "l1ffn":
                caches = build_l1ffn(P, C, chain[i], chain[i + 1],
                                     W["l1ffn"], caches)
            else:
                BUILDERS[sub](P, C, chain[i], chain[i + 1], W[sub])
    P.stack.close()
    return P.nc


def kernel(**inputs):
    if "fused" not in _prog_cache:
        _prog_cache["fused"] = build_fused()
    xs = shard_x(np.asarray(inputs["L0_attn_x"], np.float32))
    wmap = {}
    for sub in ("l0ffn", "l0attn", "l1ffn", "l1attn"):
        for ln, _s, fn in HOST_LAYOUTS[sub]:
            wmap[ln] = fn(inputs)
    maps = [dict(x_in=xs[c], **wmap) for c in range(N_CORES)]
    res = run_bass_kernel_spmd(_prog_cache["fused"], maps,
                               core_ids=list(range(N_CORES)))
    return unshard_x([np.asarray(res.results[c]["x_out"])
                      for c in range(N_CORES)])


# version 06
# speedup vs baseline: 1.0044x; 1.0044x over previous
import numpy as np
import ml_dtypes
from contextlib import ExitStack

import concourse.bass as bass
import concourse.mybir as mybir
from concourse.bass_utils import run_bass_kernel_spmd

F32 = mybir.dt.float32
BF16 = mybir.dt.bfloat16
AF = mybir.ActivationFunctionType
ALU = mybir.AluOpType
AX = mybir.AxisListType

N_CORES = 8
T = 2048
D = 4096
NT128 = T // 128
ND128 = D // 128
EPS = 1e-5
BF = ml_dtypes.bfloat16


class Sem:
    __slots__ = ("h", "n")

    def __init__(self, h):
        self.h = h
        self.n = 0


class Prog:
    def __init__(self):
        self.nc = bass.Bass("TRN2", target_bir_lowering=False, debug=False)
        self.stack = ExitStack()
        self._uid = 0
        self.em = Em(self)
        self.ds = self.sem("ds")
        self.hs = self.sem("hs")
        self.ts = self.sem("ts")
        self.vs = self.sem("vs")
        self.ss = self.sem("ss")
        self.gs = self.sem("gs")

    def uid(self, p):
        self._uid += 1
        return f"{p}{self._uid}"

    def sem(self, name):
        return Sem(self.stack.enter_context(self.nc.semaphore(self.uid(name))))

    def sbuf(self, name, shape, dt):
        return self.stack.enter_context(self.nc.sbuf_tensor(self.uid(name), shape, dt))

    def psum(self, name, shape, dt):
        return self.stack.enter_context(self.nc.psum_tensor(self.uid(name), shape, dt))

    def dram(self, name, shape, dt, kind="Internal"):
        return self.nc.dram_tensor(self.uid(name), shape, dt, kind=kind)


class Em:
    ENGS = ("g", "y", "t", "v", "s")

    def __init__(self, P):
        self.P = P
        self.q = {e: [] for e in self.ENGS}

    def wait(self, e, sem, val=None):
        v = sem.n if val is None else val
        if v > 0:
            self.q[e].append(lambda eng, _s=sem.h, _v=v: eng.wait_ge(_s, _v))

    def op(self, e, build, inc=None, k=1):
        if inc is None:
            self.q[e].append(build)
            return None
        inc.n += k
        self.q[e].append(lambda eng, _b=build, _s=inc.h, _k=k: _b(eng).then_inc(_s, _k))
        return inc.n

    def dma(self, e, out, in_, **kw):
        sem = self.P.ds if e == "g" else self.P.hs
        sem.n += 16
        self.q[e].append(
            lambda eng, _o=out, _i=in_, _k=dict(kw), _s=sem.h: eng.dma_start(
                _o, _i, **_k).then_inc(_s, 16))
        return sem.n

    def flush(self):
        nc = self.P.nc
        qs = {e: lst for e, lst in self.q.items() if lst}
        self.q = {e: [] for e in self.ENGS}
        if not qs:
            return
        with nc.Block() as b:
            decs = {"g": b.gpsimd, "y": b.sync, "t": b.tensor,
                    "v": b.vector, "s": b.scalar}
            for e, lst in qs.items():
                def runner(eng, _lst=lst):
                    for f in _lst:
                        f(eng)
                decs[e](runner)


from contextlib import contextmanager


@contextmanager
def scope(P):
    st = ExitStack()
    old = P.stack
    P.stack = st
    try:
        yield
    finally:
        for e in Em.ENGS:
            P.em.wait(e, P.ds)
            P.em.wait(e, P.hs)
        P.em.flush()
        st.close()
        P.stack = old



def emit_setup(P):
    em = P.em
    C = {}
    C["eye"] = P.sbuf("eye", [128, 128], BF16)
    C["eyef"] = eyef = P.sbuf("eyef", [128, 128], F32)
    C["ones_row"] = P.sbuf("onesrow", [1, 512], BF16)
    C["sel8"] = P.sbuf("sel8", [8, 128], BF16)
    C["eps"] = P.sbuf("epsc", [128, 1], F32)
    self8 = P.sbuf("self8", [8, 128], F32)
    t8 = P.sbuf("t8", [8, 128], F32)
    em.op("g", lambda g: g.iota(eyef[:], [[1, 128]], channel_multiplier=-1,
                                allow_small_or_imprecise_dtypes=True), inc=P.gs)
    em.op("g", lambda g: g.iota(self8[:], [[1, 128]], channel_multiplier=-16,
                                allow_small_or_imprecise_dtypes=True), inc=P.gs)
    em.op("g", lambda g: g.memset(C["ones_row"][:], 1.0), inc=P.gs)
    em.op("g", lambda g: g.memset(C["eps"][:], EPS), inc=P.gs)
    em.wait("v", P.gs)
    em.op("v", lambda v: v.tensor_single_scalar(C["eye"][:], eyef[:], 0.0,
                                                ALU.is_equal), inc=P.vs)
    em.wait("v", P.vs)
    em.op("v", lambda v: v.tensor_copy(eyef[:], C["eye"][:]), inc=P.vs)
    em.op("v", lambda v: v.tensor_single_scalar(t8[:], self8[:], -0.5, ALU.is_gt),
          inc=P.vs)
    em.op("v", lambda v: v.tensor_single_scalar(self8[:], self8[:], 15.5, ALU.is_lt),
          inc=P.vs)
    em.wait("v", P.vs)
    em.op("v", lambda v: v.tensor_tensor(C["sel8"][:], t8[:], self8[:], ALU.mult),
          inc=P.vs)
    em.flush()
    return C


def emit_ln(P, x_ap, xn_ap, C=None):
    em = P.em
    xbuf = [P.sbuf("lnx", [128, D], F32) for _ in range(2)]
    xnb = [P.sbuf("lnxn", [128, D], BF16) for _ in range(2)]
    sq = P.sbuf("lnsq", [128, D], F32)
    st = [P.sbuf("lnst", [128, 6], F32) for _ in range(2)]
    app = [0] * NT128
    stor = [0] * NT128
    for i in range(NT128):
        b = st[i % 2]
        if i >= 2:
            em.wait("s", P.hs, stor[i - 2])
        ld = em.dma("s", xbuf[i % 2][:], x_ap[i * 128:(i + 1) * 128, :])
        em.wait("s", P.hs, ld)
        em.op("s", lambda sc, _i=i, _b=b: sc.activation(
            sq[:], xbuf[_i % 2][:], AF.Square, accum_out=_b[:, 1:2]), inc=P.ss)
        sq_done = P.ss.n
        em.wait("v", P.hs, ld)
        em.op("v", lambda v, _i=i, _b=b: v.reduce_sum(
            _b[:, 0:1], xbuf[_i % 2][:], axis=AX.X), inc=P.vs)
        em.wait("v", P.vs)
        em.wait("v", P.ss, sq_done)
        em.op("v", lambda v, _b=b: v.tensor_single_scalar(
            _b[:, 2:3], _b[:, 0:1], 1.0 / D, ALU.mult), inc=P.vs)
        em.wait("v", P.vs)
        em.op("v", lambda v, _b=b: v.tensor_tensor(
            _b[:, 3:4], _b[:, 2:3], _b[:, 2:3], ALU.mult), inc=P.vs)
        em.wait("v", P.vs)
        em.op("v", lambda v, _b=b: v.scalar_tensor_tensor(
            _b[:, 1:2], _b[:, 1:2], 1.0 / D, _b[:, 3:4], ALU.mult, ALU.subtract),
            inc=P.vs)
        var_done = P.vs.n
        em.wait("s", P.vs, var_done)
        em.op("s", lambda sc, _b=b: sc.activation(
            _b[:, 4:5], _b[:, 1:2], AF.Ln, bias=C["eps"][:]))
        em.op("s", lambda sc, _b=b: sc.activation(
            _b[:, 5:6], _b[:, 4:5], AF.Exp, scale=-0.5), inc=P.ss)
        rstd_done = P.ss.n
        em.wait("v", P.ss, rstd_done)
        if i >= 2:
            em.wait("v", P.hs, stor[i - 2])
        em.op("v", lambda v, _i=i, _b=b: v.tensor_scalar(
            xnb[_i % 2][:], xbuf[_i % 2][:], _b[:, 2:3], _b[:, 5:6],
            ALU.subtract, ALU.mult), inc=P.vs)
        app[i] = P.vs.n
        em.wait("y", P.vs, app[i])
        stor[i] = em.dma("y", xn_ap[i * 128:(i + 1) * 128, :], xnb[i % 2][:])
    em.flush()


def emit_cast_load(P, dst_bf, src_f32, stagebuf):
    em = P.em
    p, f = dst_bf.tensor.shape if hasattr(dst_bf, "tensor") else (None, None)
    ld = em.dma("y", stagebuf, src_f32)
    em.wait("v", P.hs, ld)
    em.op("v", lambda v: v.tensor_copy(dst_bf, stagebuf), inc=P.vs)
    return P.vs.n


def emit_load_xnfm(P, xn_ap, xnfm, tok0, ntok, waits=None):
    em = P.em
    for k in range(ND128):
        em.dma("y", xnfm[:, k * ntok:(k + 1) * ntok],
               xn_ap[tok0:tok0 + ntok, k * 128:(k + 1) * 128], transpose=True)
    return P.hs.n



def build_l0ffn(P, C, x_ap, xo_ap, W):
    em = P.em
    xn = P.dram("xn", [T, D], BF16).ap()
    with scope(P):
        emit_ln(P, x_ap, xn, C)

    rwT = P.sbuf("rwT", [128, 8 * ND128], BF16)
    acatT = P.sbuf("acatT", [128, D], BF16)
    bcat = P.sbuf("bcat", [128, D], BF16)
    with scope(P):
        stg = P.sbuf("wstg", [128, D], F32)
        emit_cast_load(P, rwT[:], W["rwT"], stg[:, 0:256])
        em.wait("y", P.vs)
        emit_cast_load(P, acatT[:], W["AtL"], stg[:])
        em.wait("y", P.vs)
        emit_cast_load(P, bcat[:], W["BcatL"], stg[:])
    wprep = P.vs.n
    em.flush()

    xnfm = P.sbuf("xnfm", [128, ND128 * 512], BF16)
    rw4 = P.sbuf("rw4", [128, 32], F32)
    mrow = P.sbuf("mrow", [128, 4], F32)
    rwTt = P.sbuf("rwTt", [8, 512], BF16)
    rwexp = P.sbuf("rwexp", [128, 512], BF16)
    wh = P.sbuf("wh", [128, 512], BF16)
    xres = P.sbuf("xres", [128, D], F32)
    stage = [P.sbuf("stage", [128, 512], F32) for _ in range(2)]
    pL = P.psum("pL", [128, 512], F32)
    pR = P.psum("pR", [8, 512], F32)
    pE = P.psum("pE", [128, 512], F32)
    pH = P.psum("pH", [128, 512], F32)
    pD = [P.psum("pD", [128, 512], F32) for _ in range(2)]
    em.wait("t", P.vs, wprep)

    for tt in range(4):
        tok0 = tt * 512
        em.wait("y", P.ts)
        ldfm = emit_load_xnfm(P, xn, xnfm, tok0, 512)
        em.wait("t", P.hs, ldfm)
        for q in range(4):
            for k in range(ND128):
                em.op("t", lambda t, _q=q, _k=k: t.matmul(
                    pL[:, _q * 8:(_q + 1) * 8],
                    xnfm[:, _k * 512 + _q * 128:_k * 512 + (_q + 1) * 128],
                    rwT[:, _k * 8:(_k + 1) * 8],
                    start=(_k == 0), stop=(_k == ND128 - 1)),
                    inc=(P.ts if k == ND128 - 1 else None))
        em.wait("v", P.ts)
        for q in range(4):
            em.op("v", lambda v, _q=q: v.reduce_max(
                mrow[:, _q:_q + 1], pL[:, _q * 8:(_q + 1) * 8], axis=AX.X),
                inc=P.vs)
            em.wait("v", P.vs)
            em.op("v", lambda v, _q=q: v.tensor_scalar(
                rw4[:, _q * 8:(_q + 1) * 8], pL[:, _q * 8:(_q + 1) * 8],
                mrow[:, _q:_q + 1], mrow[:, _q:_q + 1],
                ALU.is_equal, ALU.mult), inc=P.vs)
        rw_done = P.vs.n
        em.wait("t", P.vs, rw_done)
        for q in range(4):
            em.op("t", lambda t, _q=q: t.transpose(
                pR[:, _q * 128:(_q + 1) * 128], rw4[:, _q * 8:(_q + 1) * 8],
                C["eyef"][:]), inc=P.ts)
        em.wait("v", P.ts)
        em.op("v", lambda v: v.tensor_copy(rwTt[:], pR[:]), inc=P.vs)
        em.wait("t", P.vs)
        em.op("t", lambda t: t.matmul(pE[:], C["sel8"][:], rwTt[:],
                                      start=True, stop=True), inc=P.ts)
        em.wait("v", P.ts)
        em.op("v", lambda v: v.tensor_copy(rwexp[:], pE[:]), inc=P.vs)
        for k in range(ND128):
            em.op("t", lambda t, _k=k: t.matmul(
                pH[:], acatT[:, _k * 128:(_k + 1) * 128],
                xnfm[:, _k * 512:(_k + 1) * 512],
                start=(_k == 0), stop=(_k == ND128 - 1)),
                inc=(P.ts if k == ND128 - 1 else None))
        em.wait("v", P.ts)
        em.wait("v", P.vs)
        em.op("v", lambda v: v.tensor_tensor(wh[:], pH[:], rwexp[:], ALU.mult),
              inc=P.vs)
        wh_done = P.vs.n
        em.wait("t", P.vs, wh_done)
        stt_done = []
        store_done = []
        for q in range(4):
            em.wait("s", P.vs)
            ldx = em.dma("s", xres[:], x_ap[tok0 + q * 128:tok0 + (q + 1) * 128, :])
            for do in range(8):
                i = q * 8 + do
                pd = pD[i % 2]
                if i >= 2:
                    em.wait("t", P.vs, stt_done[i - 2])
                em.op("t", lambda t, _q=q, _do=do, _pd=pd: t.matmul(
                    _pd[:], wh[:, _q * 128:(_q + 1) * 128],
                    bcat[:, _do * 512:(_do + 1) * 512],
                    start=True, stop=True), inc=P.ts)
                mm_done = P.ts.n
                stg = stage[i % 2]
                em.wait("v", P.ts, mm_done)
                em.wait("v", P.hs, ldx)
                if i >= 2:
                    em.wait("v", P.hs, store_done[i - 2])
                em.op("v", lambda v, _pd=pd, _do=do, _stg=stg: v.scalar_tensor_tensor(
                    _stg[:], _pd[:], 0.5, xres[:, _do * 512:(_do + 1) * 512],
                    ALU.mult, ALU.add), inc=P.vs)
                stt_done.append(P.vs.n)
                em.wait("y", P.vs)
                em.dma("y", xo_ap[tok0 + q * 128:tok0 + (q + 1) * 128,
                                  do * 512:(do + 1) * 512], stg[:])
                store_done.append(P.hs.n)
        em.flush()
    em.wait("g", P.ds)
    em.flush()


def _k_last(k):
    return k == ND128 - 1




NJT = 86
SC_LORA = 2.0


DBG_L1 = False


def emit_l1ffn_caches_decl(P):
    w1b = P.dram("w1b", [NJT, 128, 4224], BF16).ap()
    w3b = P.dram("w3b", [NJT, 128, 4224], BF16).ap()
    w2b = P.dram("w2b", [8, 128, NJT, 512], BF16).ap()
    return dict(w1b=w1b, w3b=w3b, w2b=w2b, cast_done=0)


def emit_l1ffn_caches_chunk(P, W, caches, c0, c1):
    em = P.em
    for c in range(c0, c1):
        j0, j1 = c * 11, min((c + 1) * 11, NJT)
        if j0 < NJT:
            em.dma("g", caches["w1b"][j0:j1], W["w1L"][j0:j1])
            em.dma("g", caches["w3b"][j0:j1], W["w3L"][j0:j1])
        em.dma("g", caches["w2b"][c], W["w2L"][c])
    caches["cast_done"] = P.ds.n


def emit_l1ffn_caches(P, W):
    caches = emit_l1ffn_caches_decl(P)
    emit_l1ffn_caches_chunk(P, W, caches, 0, 8)
    return caches


def build_l1ffn(P, C, x_ap, xo_ap, W, caches=None):
    em = P.em
    xn = P.dram("xn", [T, D], BF16).ap()
    if DBG_L1:
        dbg_g = P.nc.dram_tensor("dbg_g", [4, 128, NJT * 512], BF16,
                                 kind="ExternalOutput").ap()
        dbg_xa = P.nc.dram_tensor("dbg_xa", [4, 16, 1024], BF16,
                                  kind="ExternalOutput").ap()

    if caches is None:
        caches = emit_l1ffn_caches(P, W)
    w1b, w3b, w2b = caches["w1b"], caches["w3b"], caches["w2b"]

    with scope(P):
        emit_ln(P, x_ap, xn, C)

    a1t = P.sbuf("a1t", [128, 512], BF16)
    a3t = P.sbuf("a3t", [128, 512], BF16)
    a2t = P.sbuf("a2t", [128, 1376], BF16)
    b2t = P.sbuf("b2t", [16, D], BF16)
    em.dma("g", a1t[:], W["a1tL"])
    em.dma("g", a3t[:], W["a3tL"])
    em.dma("g", a2t[:], W["a2tL"])
    em.dma("g", b2t[:], W["b2T"])
    smalls = P.ds.n
    em.flush()

    with scope(P):
        xnfm = P.sbuf("xnfm", [128, ND128 * 512], BF16)
        w1p = [P.sbuf("w1p", [128, 4224], BF16) for _ in range(2)]
        w3p = [P.sbuf("w3p", [128, 4224], BF16) for _ in range(2)]
        h1s = [P.sbuf("h1s", [128, 512], BF16) for _ in range(2)]
        g_all = P.sbuf("gall", [128, NJT * 512], BF16)
        xa1 = P.sbuf("xa1", [16, 512], BF16)
        xa3 = P.sbuf("xa3", [16, 512], BF16)
        ga2 = P.sbuf("ga2", [16, 512], BF16)
        w2buf = [P.sbuf("w2buf", [128, 4096], BF16) for _ in range(2)]
        xresd = P.sbuf("xresd", [128, 2048], F32)
        stage = [P.sbuf("stg", [128, 512], F32) for _ in range(4)]
        pb = [P.psum("pb", [128, 512], F32) for _ in range(8)]

        for e in ("t", "y", "g", "v", "s"):
            em.wait(e, P.ds, max(smalls, caches.get("cast_done", 0)))

        for tt in range(4):
            tok0 = tt * 512
            em.wait("y", P.ts)
            ldfm = emit_load_xnfm(P, xn, xnfm, tok0, 512)
            em.wait("t", P.hs, ldfm)

            em.wait("t", P.vs)
            for (bank, at, xs_out) in ((4, a1t, xa1), (5, a3t, xa3)):
                for k in range(ND128):
                    em.op("t", lambda t, _b=bank, _a=at, _k=k: t.matmul(
                        pb[_b][0:16, :], _a[:, _k * 16:(_k + 1) * 16],
                        xnfm[:, _k * 512:(_k + 1) * 512],
                        start=(_k == 0), stop=(_k == ND128 - 1)),
                        inc=(P.ts if k == ND128 - 1 else None))
                em.wait("s", P.ts)
                em.op("s", lambda sc, _b=bank, _o=xs_out: sc.activation(
                    _o[:], pb[_b][0:16, :], AF.Identity, scale=SC_LORA), inc=P.ss)
            lora_evac = P.ss.n
            em.wait("t", P.ss, lora_evac)

            t1d = [0] * NJT
            t3d = [0] * NJT
            sild = [0] * NJT
            gd = [0] * NJT
            ld_w = [0] * (NJT + 1)
            for js in range(NJT):
                if js == 0:
                    em.wait("y", P.ts)
                    em.dma("y", w1p[0][:], w1b[0])
                    em.dma("y", w3p[0][:], w3b[0])
                    ld_w[0] = P.hs.n
                if js + 1 < NJT:
                    if js >= 1:
                        em.wait("y", P.ts, t3d[js - 1])
                    em.dma("y", w1p[(js + 1) % 2][:], w1b[js + 1])
                    em.dma("y", w3p[(js + 1) % 2][:], w3b[js + 1])
                    ld_w[js + 1] = P.hs.n
                s = js % 2
                k1, k3 = 2 * s, 2 * s + 1
                em.wait("t", P.hs, ld_w[js])
                if js >= 2:
                    em.wait("t", P.ss, sild[js - 2])
                    em.wait("t", P.vs, gd[js - 2])
                for k in range(ND128):
                    em.op("t", lambda t, _p=pb[k1], _k=k, _s=s: t.matmul(
                        _p[:], w1p[_s][:, _k * 128:(_k + 1) * 128],
                        xnfm[:, _k * 512:(_k + 1) * 512],
                        start=(_k == 0), stop=False))
                em.op("t", lambda t, _p=pb[k1], _s=s: t.matmul(
                    _p[:], w1p[_s][0:16, 4096:4224], xa1[:],
                    start=False, stop=True), inc=P.ts)
                t1d[js] = P.ts.n
                for k in range(ND128):
                    em.op("t", lambda t, _p=pb[k3], _k=k, _s=s: t.matmul(
                        _p[:], w3p[_s][:, _k * 128:(_k + 1) * 128],
                        xnfm[:, _k * 512:(_k + 1) * 512],
                        start=(_k == 0), stop=False))
                em.op("t", lambda t, _p=pb[k3], _s=s: t.matmul(
                    _p[:], w3p[_s][0:16, 4096:4224], xa3[:],
                    start=False, stop=True), inc=P.ts)
                t3d[js] = P.ts.n
                em.wait("s", P.ts, t1d[js])
                if js >= 2:
                    em.wait("s", P.vs, gd[js - 2])
                em.op("s", lambda sc, _o=h1s[s], _p=pb[k1]: sc.activation(
                    _o[:], _p[:], AF.Silu), inc=P.ss)
                sild[js] = P.ss.n
                em.wait("v", P.ts, t3d[js])
                em.wait("v", P.ss, sild[js])
                em.op("v", lambda v, _i=h1s[s], _p=pb[k3], _js=js: v.tensor_tensor(
                    g_all[:, _js * 512:(_js + 1) * 512], _i[:], _p[:], ALU.mult),
                    inc=P.vs)
                gd[js] = P.vs.n

            if DBG_L1:
                em.wait("g", P.vs, gd[NJT - 1])
                em.dma("g", dbg_g[tt], g_all[:])
                em.wait("g", P.ss)
                em.dma("g", dbg_xa[tt, :, 0:512], xa1[:])
                em.dma("g", dbg_xa[tt, :, 512:1024], xa3[:])
            em.wait("t", P.vs, gd[NJT - 1])
            em.wait("t", P.ss, sild[NJT - 1])
            for hc in range(NJT):
                em.op("t", lambda t, _h=hc: t.matmul(
                    pb[0][0:16, :], a2t[:, _h * 16:(_h + 1) * 16],
                    g_all[:, _h * 512:(_h + 1) * 512],
                    start=(_h == 0), stop=(_h == NJT - 1)),
                    inc=(P.ts if hc == NJT - 1 else None))
            em.wait("s", P.ts)
            em.op("s", lambda sc: sc.activation(ga2[:], pb[0][0:16, :],
                                                AF.Identity, scale=SC_LORA),
                  inc=P.ss)
            ga2_done = P.ss.n
            em.wait("t", P.ss, ga2_done)

            sttd = {}
            for dof in range(8):
                ldx = 0
                for q in range(4):
                    ldx = em.dma("g", xresd[:, q * 512:(q + 1) * 512],
                                 x_ap[tok0 + q * 128:tok0 + (q + 1) * 128,
                                      dof * 512:(dof + 1) * 512])
                grp_n = (NJT + 7) // 8
                ld_g = [0] * grp_n
                tw = [0] * grp_n

                def issue_grp(grp, _dof=dof):
                    h0, h1_ = grp * 8, min((grp + 1) * 8, NJT)
                    nh = h1_ - h0
                    em.dma("y", w2buf[grp % 2][:, 0:nh * 512],
                           w2b[_dof][:, h0:h1_, :])
                    ld_g[grp] = P.hs.n

                em.wait("y", P.ts)
                issue_grp(0)
                issue_grp(1)
                pq = [pb[4 * (dof % 2) + q] for q in range(4)]
                if dof >= 2:
                    for q in range(4):
                        em.wait("t", P.vs, sttd[(dof - 2, q)])
                for grp in range(grp_n):
                    h0, h1_ = grp * 8, min((grp + 1) * 8, NJT)
                    em.wait("t", P.hs, ld_g[grp])
                    for hc in range(h0, h1_):
                        for q in range(4):
                            last = (hc == h1_ - 1 and q == 3)
                            em.op("t", lambda t, _h=hc, _q=q, _g=grp, _p=pq[q]: t.matmul(
                                _p[:],
                                g_all[:, _h * 512 + _q * 128:_h * 512 + (_q + 1) * 128],
                                w2buf[_g % 2][:, (_h - _g * 8) * 512:
                                              (_h - _g * 8 + 1) * 512],
                                start=(_h == 0), stop=False),
                                inc=(P.ts if last else None))
                    tw[grp] = P.ts.n
                    if grp + 2 < grp_n:
                        em.wait("y", P.ts, tw[grp])
                        issue_grp(grp + 2)
                for q in range(4):
                    em.op("t", lambda t, _q=q, _p=pq[q], _d=dof: t.matmul(
                        _p[:], ga2[:, _q * 128:(_q + 1) * 128],
                        b2t[:, _d * 512:(_d + 1) * 512],
                        start=False, stop=True), inc=P.ts)
                    mmq = P.ts.n
                    em.wait("v", P.ts, mmq)
                    em.wait("v", P.ds, ldx)
                    stg = stage[q]
                    em.op("v", lambda v, _q=q, _s=stg, _p=pq[q]: v.scalar_tensor_tensor(
                        _s[:], _p[:], 0.5, xresd[:, _q * 512:(_q + 1) * 512],
                        ALU.mult, ALU.add), inc=P.vs)
                    sttd[(dof, q)] = P.vs.n
                    em.wait("g", P.vs, sttd[(dof, q)])
                    em.dma("g", xo_ap[tok0 + q * 128:tok0 + (q + 1) * 128,
                                      dof * 512:(dof + 1) * 512], stg[:])
            em.flush()
    em.wait("g", P.ds)
    em.wait("g", P.hs)
    em.flush()
    return caches



L0A_STAGE = 9
L0A_DBG = False


def build_l0attn(P, C, x_ap, xo_ap, W):
    em = P.em
    xn = P.dram("xn", [T, D], BF16).ap()
    with scope(P):
        emit_ln(P, x_ap, xn, C)

    cc = P.sem("cc")
    ckv_loc = P.dram("ckvloc", [1, 5 * T], BF16)
    ckv_lrows = ckv_loc.ap().rearrange("a (d t) -> (a d) t", d=5)
    ag = P.dram("ag", [8, 5 * T], BF16)
    m2d = P.dram("m2d", [32, T], BF16).ap()
    ckv5d = P.dram("ckv5d", [8, 128, 80], BF16).ap()

    def ag_rows(d0, d1, b):
        return (ag.ap().rearrange("r (d t) -> d r t", d=5)
                [d0:d1, :, b * 256:(b + 1) * 256])

    wqk = P.sbuf("wqk", [12, 128], BF16)
    wvu = P.sbuf("wvu", [4, D], BF16)
    bqc = P.sbuf("bqc", [12, 1], F32)
    bkvc = P.sbuf("bkvc", [4, 1], F32)
    bqkh = P.sbuf("bqkh", [128, 1], F32)
    bqk4h = P.sbuf("bqk4h", [4, 32], F32)
    bvuC = P.sbuf("bvuC", [128, 32], F32)
    boR = P.sbuf("boR", [1, D], BF16)
    cqT = P.sbuf("cqT", [12, T], BF16)
    neg1r = P.sbuf("neg1r", [1, 4], BF16)
    ones4 = P.sbuf("ones4", [4, 1], BF16)
    onesf = P.sbuf("onesf", [1, 512], F32)
    mc4row = P.sbuf("mc4row", [1, T], BF16)
    em.dma("y", bqc[:], W["bqC"])
    em.dma("y", bkvc[:], W["bkvC"])
    em.dma("y", bqkh[:], W["bqkC"])
    em.dma("y", bqk4h[:], W["bqk4"])
    em.dma("y", bvuC[:], W["bvuC"])
    with scope(P):
        stgA = P.sbuf("stgA", [128, 4096], F32)
        emit_cast_load(P, wqk[:], W["Wqk"], stgA[0:12, 0:128])
        em.wait("y", P.vs)
        emit_cast_load(P, wvu[:], W["WvU"], stgA[0:4, :])
        em.wait("y", P.vs)
        emit_cast_load(P, boR[:], W["boR"], stgA[0:1, :])
    em.op("g", lambda g: g.memset(neg1r[:], -1.0), inc=P.gs)
    em.op("g", lambda g: g.memset(ones4[:], 1.0), inc=P.gs)
    em.op("g", lambda g: g.memset(onesf[:], 1.0), inc=P.gs)
    em.wait("v", P.hs)
    em.wait("v", P.vs)
    em.wait("v", P.gs)
    em.op("v", lambda v: v.tensor_single_scalar(bqkh[:], bqkh[:], 0.5, ALU.mult),
          inc=P.vs)
    em.op("v", lambda v: v.tensor_single_scalar(bqk4h[:], bqk4h[:], 0.5, ALU.mult),
          inc=P.vs)
    s4 = P.sbuf("s4", [128, 32], BF16)
    s4f = P.sbuf("s4f", [128, 32], F32)
    s4g = P.sbuf("s4g", [128, 32], F32)
    em.op("g", lambda g: g.iota(s4f[:], [[4, 32]], channel_multiplier=-1,
                                allow_small_or_imprecise_dtypes=True), inc=P.gs)
    em.wait("v", P.gs)
    em.op("v", lambda v: v.tensor_single_scalar(s4g[:], s4f[:], -3.5, ALU.is_gt),
          inc=P.vs)
    em.op("v", lambda v: v.tensor_single_scalar(s4f[:], s4f[:], 0.5, ALU.is_lt),
          inc=P.vs)
    em.wait("v", P.vs)
    em.op("v", lambda v: v.tensor_tensor(s4[:], s4g[:], s4f[:], ALU.mult),
          inc=P.vs)
    em.flush()

    with scope(P):
        xnfm = P.sbuf("xnfm", [128, ND128 * 512], BF16)
        wqL = P.sbuf("wqL", [128, 384], BF16)
        wkvL = P.sbuf("wkvL", [128, 128], BF16)
        ckvT = P.sbuf("ckvT", [4, T], BF16)
        neg1T = P.sbuf("neg1T", [1, T], BF16)
        qfm = P.sbuf("qfm", [128, 512], BF16)
        qsq = P.sbuf("qsq", [128, T], BF16)
        m2sb = P.sbuf("m2sb", [32, T], BF16)
        casq = P.sbuf("casq", [4, T], BF16)
        cnorms = P.sbuf("cnorms", [1, T], F32)
        mcb = P.sbuf("mcb", [1, 2], F32)
        ckvrows = P.sbuf("ckvrows", [5, T], BF16)
        ckv5s = P.sbuf("ckv5s", [128, 80], BF16)
        pA = P.psum("pA", [128, 512], F32)
        pB = P.psum("pB", [128, 512], F32)
        pC = P.psum("pC", [128, 512], F32)
        pTr = P.psum("pTr", [128, 1024], BF16)
        stgB = P.sbuf("stgB", [128, 384], F32)
        emit_cast_load(P, wqL[:], W["WqL"], stgB[:])
        em.wait("y", P.vs)
        emit_cast_load(P, wkvL[:], W["WkvL"], stgB[:, 0:128])
        em.op("g", lambda g: g.memset(neg1T[:], -1.0), inc=P.gs)
        wld = P.vs.n
        em.wait("t", P.vs, wld)
        for tc in range(4):
            em.wait("y", P.ts)
            ldfm = emit_load_xnfm(P, xn, xnfm, tc * 512, 512)
            em.wait("t", P.hs, ldfm)
            em.wait("t", P.ss)
            for k in range(ND128):
                em.op("t", lambda t, _k=k: t.matmul(
                    pA[0:4, :], wkvL[:, _k * 4:(_k + 1) * 4],
                    xnfm[:, _k * 512:(_k + 1) * 512],
                    start=(_k == 0), stop=(_k == ND128 - 1)),
                    inc=(P.ts if k == ND128 - 1 else None))
            em.wait("s", P.ts)
            em.op("s", lambda sc, _tc=tc: sc.activation(
                ckvT[:, _tc * 512:(_tc + 1) * 512], pA[0:4, :], AF.Identity,
                bias=bkvc[:]), inc=P.ss)
            for k in range(ND128):
                em.op("t", lambda t, _k=k: t.matmul(
                    pB[0:12, :], wqL[:, _k * 12:(_k + 1) * 12],
                    xnfm[:, _k * 512:(_k + 1) * 512],
                    start=(_k == 0), stop=(_k == ND128 - 1)),
                    inc=(P.ts if k == ND128 - 1 else None))
            em.wait("s", P.ts)
            em.op("s", lambda sc, _tc=tc: sc.activation(
                cqT[:, _tc * 512:(_tc + 1) * 512], pB[0:12, :], AF.Identity,
                bias=bqc[:]), inc=P.ss)
            cq_done = P.ss.n
            em.wait("t", P.ss, cq_done)
            em.op("t", lambda t, _tc=tc: t.matmul(
                pC[:], wqk[:], cqT[:, _tc * 512:(_tc + 1) * 512],
                start=True, stop=True), inc=P.ts)
            em.wait("s", P.ts)
            em.op("s", lambda sc: sc.activation(
                qfm[:], pC[:], AF.Identity, bias=bqkh[:], scale=0.5), inc=P.ss)
            em.wait("s", P.ss)
            em.op("s", lambda sc, _tc=tc: sc.activation(
                qsq[:, _tc * 512:(_tc + 1) * 512], qfm[:], AF.Square), inc=P.ss)
        if L0A_STAGE <= 0:
            em.wait("g", P.ds)
            em.flush()
            return
        em.wait("y", P.ss)
        em.wait("y", P.gs)
        em.dma("y", ckv_lrows[0:4, :], ckvT[:])
        em.dma("y", ckv_lrows[4:5, :], neg1T[:])
        em.wait("g", P.hs)
        em.wait("g", P.ds)
        em.op("g", lambda g: g.collective_compute(
            "AllGather", ALU.bypass, replica_groups=[list(range(N_CORES))],
            ins=[ckv_loc.ap().opt()], outs=[ag.ap().opt()]).then_inc(cc.h))
        em.wait("g", cc, 1)
        em.wait("y", cc, 1)
        em.wait("t", cc, 1)
        em.wait("s", cc, 1)
        em.wait("v", cc, 1)
        if L0A_STAGE <= 1:
            em.wait("g", P.ds)
            em.flush()
            return
        for b in range(8):
            em.wait("y", P.ts)
            ld = em.dma("y", ckvrows[:], ag_rows(0, 5, b))
            em.wait("s", P.hs, ld)
            em.op("s", lambda sc: sc.activation(casq[:], ckvrows[0:4, :],
                                                AF.Square), inc=P.ss)
            sq_done = P.ss.n
            em.wait("t", P.ss, sq_done)
            em.wait("t", P.vs)
            for tc in range(4):
                em.wait("t", P.vs)
                em.op("t", lambda t, _tc=tc: t.matmul(
                    pA[0:1, :], ones4[:], casq[:, _tc * 512:(_tc + 1) * 512],
                    start=True, stop=True), inc=P.ts)
                nmm = P.ts.n
                em.wait("v", P.ts, nmm)
                em.op("v", lambda v, _tc=tc: v.tensor_copy(
                    cnorms[0:1, _tc * 512:(_tc + 1) * 512], pA[0:1, :]), inc=P.vs)
            em.wait("v", P.vs)
            em.op("v", lambda v: v.reduce_max(mcb[0:1, 0:1], cnorms[:], axis=AX.X),
                  inc=P.vs)
            em.wait("v", P.vs)
            em.op("v", lambda v: v.tensor_single_scalar(
                mcb[0:1, 1:2], mcb[0:1, 0:1], 0.25, ALU.mult), inc=P.vs)
            em.wait("v", P.vs)
            em.op("v", lambda v, _b=b: v.tensor_scalar(
                mc4row[0:1, _b * 256:(_b + 1) * 256], onesf[0:1, 0:256],
                mcb[0:1, 1:2], None, ALU.mult), inc=P.vs)
            for kt in range(16):
                em.op("t", lambda t, _kt=kt: t.transpose(
                    pTr[:, _kt * 8:_kt * 8 + 5],
                    ckvrows[:, _kt * 128:(_kt + 1) * 128], C["eye"][0:5, 0:5]),
                    inc=(P.ts if kt == 15 else None))
            em.wait("v", P.ts)
            em.op("v", lambda v: v.tensor_copy(
                ckv5s[:].rearrange("p (k c) -> p k c", c=5),
                pTr[:, 0:128].rearrange("p (k c) -> p k c", c=8)[:, :, 0:5]),
                inc=P.vs)
            em.wait("y", P.vs)
            em.dma("y", ckv5d[b], ckv5s[:])
        mc_done = P.vs.n
        em.wait("t", P.vs, mc_done)
        em.wait("t", P.ss)
        for tc in range(4):
            em.wait("t", P.vs)
            em.op("t", lambda t, _tc=tc: t.matmul(
                pB[0:32, :], s4[:], qsq[:, _tc * 512:(_tc + 1) * 512],
                start=True, stop=False))
            em.op("t", lambda t, _tc=tc: t.matmul(
                pB[0:32, :], C["ones_row"][0:1, 0:32],
                mc4row[0:1, _tc * 512:(_tc + 1) * 512],
                start=False, stop=True), inc=P.ts)
            m2mm = P.ts.n
            em.wait("v", P.ts, m2mm)
            em.op("v", lambda v, _tc=tc: v.tensor_copy(
                m2sb[:, _tc * 512:(_tc + 1) * 512], pB[0:32, :]), inc=P.vs)
        em.wait("y", P.vs)
        em.dma("y", m2d, m2sb[:])
    if L0A_STAGE <= 2:
        em.wait("g", P.ds)
        em.flush()
        return

    if L0A_DBG:
        dmp = P.nc.dram_tensor("dmp", [64, T], BF16, kind="ExternalOutput").ap()
        em.wait("y", P.ss)
        em.wait("y", P.vs)
        em.dma("y", dmp[0:4, :], ckvT[:])
        em.dma("y", dmp[4:16, :], cqT[:])
        em.dma("y", dmp[16:48, :], m2sb[:])
        em.dma("y", dmp[48:49, :], mc4row[:])
        em.dma("y", dmp[49:50, 0:128], qsq[0:1, 0:128])
        em.wait("y", P.hs)
    with scope(P):
        qa = [P.sbuf("qa", [5, 8192], BF16) for _ in range(2)]
        ckva = [P.sbuf("ckva", [5, T], BF16) for _ in range(2)]
        ckv5 = [P.sbuf("ckv5", [128, 80], BF16) for _ in range(2)]
        pTb = [P.sbuf("pTbuf", [128, 8192], BF16) for _ in range(2)]
        c5sb = P.sbuf("c5sb", [5, 512], F32)
        ell = P.sbuf("ell", [1, 512], F32)
        ellf = P.sbuf("ellf", [1, 512], F32)
        ellr = P.sbuf("ellr", [1, 512], BF16)
        acn = P.sbuf("acn", [4, 512], BF16)
        afm = [P.sbuf("afm", [128, 1024], BF16) for _ in range(32)]
        wob = [P.sbuf("wob", [128, 4096], BF16) for _ in range(2)]
        xres = P.sbuf("xres", [128, 512], F32)
        stage = [P.sbuf("stg", [128, 512], F32) for _ in range(2)]
        pS = [P.psum("pS", [128, 512], F32) for _ in range(3)]
        pC5 = P.psum("pC5", [128, 512], F32)
        pR = P.psum("pR", [128, 512], F32)
        pV = [P.psum("pV", [128, 512], F32) for _ in range(2)]
        pX = P.psum("pX", [128, 512], F32)

        for half in range(2):
            for bi in range(4):
                b = half * 4 + bi
                s = b % 2
                em.wait("t", P.vs)
                em.wait("y", P.ts)
                lda = em.dma("y", ckva[s][:], ag_rows(0, 5, b))
                em.dma("y", qa[s][4:5, :], m2d[:, b * 256:(b + 1) * 256])
                em.dma("y", ckv5[s][:], ckv5d[b])
                ldm = P.hs.n
                em.wait("s", P.ts)
                for h in range(32):
                    em.wait("t", P.ss)
                    em.op("t", lambda t, _h=h, _b=b: t.matmul(
                        pX[0:4, 0:256], wqk[:, _h * 4:(_h + 1) * 4],
                        cqT[:, _b * 256:(_b + 1) * 256],
                        start=True, stop=True), inc=P.ts)
                    qmm = P.ts.n
                    em.wait("s", P.ts, qmm)
                    em.op("s", lambda sc, _s=s, _h=h: sc.activation(
                        qa[_s][0:4, _h * 256:(_h + 1) * 256], pX[0:4, 0:256],
                        AF.Identity, bias=bqk4h[:, _h:_h + 1], scale=0.5),
                        inc=P.ss)
                qa_done = P.ss.n
                em.wait("t", P.ss, qa_done)
                em.wait("t", P.hs, ldm)
                expd = {}
                for hh in range(16):
                    for kt in range(16):
                        pk = pS[kt % 2]
                        if kt >= 2:
                            em.wait("t", P.ss, expd[kt - 2])
                        em.op("t", lambda t, _s=s, _kt=kt, _hh=hh, _pk=pk: t.matmul(
                            _pk[:], ckva[_s][:, _kt * 128:(_kt + 1) * 128],
                            qa[_s][:, _hh * 512:(_hh + 1) * 512],
                            start=True, stop=True), inc=P.ts)
                        smm = P.ts.n
                        em.wait("s", P.ts, smm)
                        em.op("s", lambda sc, _s=s, _kt=kt, _pk=pk: sc.activation(
                            pTb[_s][:, _kt * 512:(_kt + 1) * 512], _pk[:], AF.Exp),
                            inc=P.ss)
                        expd[kt] = P.ss.n
                    exp_done = P.ss.n
                    em.wait("t", P.ss, exp_done)
                    for kt in range(16):
                        em.op("t", lambda t, _s=s, _kt=kt: t.matmul(
                            pC5[0:5, :], ckv5[_s][:, _kt * 5:(_kt + 1) * 5],
                            pTb[_s][:, _kt * 512:(_kt + 1) * 512],
                            start=(_kt == 0), stop=(_kt == 15)),
                            inc=(P.ts if kt == 15 else None))
                    c5_done = P.ts.n
                    em.wait("v", P.ts, c5_done)
                    em.op("v", lambda v: v.tensor_copy(c5sb[:], pC5[0:5, :]),
                          inc=P.vs)
                    em.wait("y", P.vs)
                    em.dma("y", ell[:], c5sb[4:5, :])
                    elld = P.hs.n
                    em.wait("v", P.hs, elld)
                    em.op("v", lambda v: v.reciprocal(ellf[:], ell[:]), inc=P.vs)
                    em.wait("v", P.vs)
                    em.op("v", lambda v: v.tensor_copy(ellr[:], ellf[:]), inc=P.vs)
                    em.wait("t", P.vs)
                    em.op("t", lambda t: t.matmul(pR[0:4, :], neg1r[:], ellr[:],
                                                  start=True, stop=True), inc=P.ts)
                    rbc_done = P.ts.n
                    em.wait("v", P.ts, rbc_done)
                    em.op("v", lambda v: v.tensor_tensor(
                        acn[:], c5sb[0:4, :], pR[0:4, :], ALU.mult), inc=P.vs)
                    acn_done = P.vs.n
                    em.wait("t", P.vs, acn_done)
                    em.wait("t", P.ss)
                    for j in range(2):
                        h = hh * 2 + j
                        em.op("t", lambda t, _h=h, _j=j: t.matmul(
                            pV[_j][:, 0:256], wvu[:, _h * 128:(_h + 1) * 128],
                            acn[:, _j * 256:(_j + 1) * 256],
                            start=True, stop=True), inc=P.ts)
                        vmm = P.ts.n
                        em.wait("s", P.ts, vmm)
                        em.op("s", lambda sc, _h=h, _j=j, _bi=bi: sc.activation(
                            afm[_h][:, _bi * 256:(_bi + 1) * 256],
                            pV[_j][:, 0:256], AF.Identity,
                            bias=bvuC[:, _h:_h + 1]), inc=P.ss)
            if L0A_STAGE <= 3:
                em.flush()
                continue
            em.wait("t", P.vs)
            em.wait("t", P.ss)
            hb = half * 1024
            sttd = {}
            for dof in range(8):
                ldg = [0] * 4
                twd = [0] * 4

                def issue_wo(grp, _dof=dof):
                    em.dma("g", wob[grp % 2][:],
                           W["WoL"][_dof][:, grp * 8:(grp + 1) * 8, :])
                    ldg[grp] = P.ds.n

                em.wait("g", P.ts)
                issue_wo(0)
                issue_wo(1)
                pq = [pS[0], pS[1], pS[2], pC5, pR, pV[0], pV[1], pX]
                if dof >= 1:
                    for tk in range(8):
                        em.wait("t", P.vs, sttd.get((dof - 1, tk), 0))
                for grp in range(4):
                    em.wait("t", P.ds, ldg[grp])
                    for a in range(grp * 8, grp * 8 + 8):
                        ws = (a - grp * 8) * 512
                        for tk in range(8):
                            em.op("t", lambda t, _a=a, _tk=tk, _g=grp, _w=ws,
                                  _hb=hb, _p=pq[tk]: t.matmul(
                                _p[:], afm[_a][:, (_hb // 128 % 8) * 0 + _tk * 128:
                                               (_tk + 1) * 128],
                                wob[_g % 2][:, _w:_w + 512],
                                start=(_a == 0), stop=False),
                                inc=(P.ts if (a == grp * 8 + 7 and tk == 7)
                                     else None))
                    twd[grp] = P.ts.n
                    if grp + 2 < 4:
                        em.wait("g", P.ts, twd[grp])
                        issue_wo(grp + 2)
                for tk in range(8):
                    em.op("t", lambda t, _tk=tk, _d=dof, _p=pq[tk]: t.matmul(
                        _p[:], C["ones_row"][0:1, 0:128],
                        boR[0:1, _d * 512:(_d + 1) * 512],
                        start=False, stop=True), inc=P.ts)
                    mmd = P.ts.n
                    em.wait("s", P.vs)
                    ldx = em.dma("s", xres[:],
                                 x_ap[hb + tk * 128:hb + (tk + 1) * 128,
                                      dof * 512:(dof + 1) * 512])
                    em.wait("v", P.ts, mmd)
                    em.wait("v", P.hs, ldx)
                    stg = stage[tk % 2]
                    em.op("v", lambda v, _s=stg, _p=pq[tk]: v.scalar_tensor_tensor(
                        _s[:], _p[:], 1.0, xres[:], ALU.mult, ALU.add), inc=P.vs)
                    sttd[(dof, tk)] = P.vs.n
                    em.wait("y", P.vs, sttd[(dof, tk)])
                    em.dma("y", xo_ap[hb + tk * 128:hb + (tk + 1) * 128,
                                      dof * 512:(dof + 1) * 512], stg[:])
            em.flush()
    em.wait("g", P.ds)
    em.wait("g", P.hs)
    em.flush()



def build_l1attn(P, C, x_ap, xo_ap, W):
    em = P.em
    xn = P.dram("xn", [T, D], BF16).ap()
    with scope(P):
        emit_ln(P, x_ap, xn, C)

    wdT = P.sbuf("wdT", [128, 4096], BF16)
    wuT = P.sbuf("wuT", [128, 4096], BF16)
    inwT = P.sbuf("inwT", [128, 552], BF16)
    outwT = P.sbuf("outwT", [128, 256], BF16)
    bdR = P.sbuf("bdR", [1, 128], BF16)
    buR = P.sbuf("buR", [1, 4096], BF16)
    rows = P.sbuf("rows", [1, 3840], F32)
    onesf32 = P.sbuf("onesf32", [1, 128], F32)
    em.dma("g", wdT[:], W["wdT"])
    em.dma("g", wuT[:], W["wuT"])
    em.dma("g", inwT[:], W["inwT"])
    em.dma("g", outwT[:], W["outwT"])
    em.dma("g", bdR[:], W["bdR"])
    em.dma("g", buR[:], W["buR"])
    em.dma("g", rows[0:1, 0:2016], W["convw7R"])
    em.dma("g", rows[0:1, 2016:2304], W["convbR"])
    em.dma("g", rows[0:1, 2304:2432], W["dtb16R"])
    em.dma("g", rows[0:1, 2432:2436], W["AlogR"])
    em.dma("g", rows[0:1, 2436:2440], W["DR"])
    em.dma("g", rows[0:1, 2440:2441], W["gateR"])
    em.dma("g", rows[0:1, 2560:3584], W["fcdR"])
    em.dma("g", rows[0:1, 3584:3840], W["normwR"])
    em.op("g", lambda g: g.memset(onesf32[:], 1.0), inc=P.gs)
    ld0 = P.ds.n
    em.flush()

    wcb = P.sbuf("wcb", [128, 2016], F32)
    cvb = P.sbuf("cvb", [128, 288], F32)
    dtb16 = P.sbuf("dtb16", [128, 128], F32)
    A7 = P.sbuf("A7", [128, 28], F32)
    Dbc = P.sbuf("Dbc", [128, 4], F32)
    fcdb = P.sbuf("fcdb", [128, 1024], F32)
    nwbc = P.sbuf("nwbc", [128, 256], F32)
    a0c = P.sbuf("a0c", [128, 1], F32)
    a1c = P.sbuf("a1c", [128, 1], F32)
    scrr = P.sbuf("scrr", [1, 64], F32)
    pb0 = P.psum("pcb", [128, 512], F32)
    em.wait("s", P.ds, ld0)
    em.wait("s", P.gs)
    em.op("s", lambda sc: sc.activation(scrr[0:1, 0:4], rows[0:1, 2432:2436],
                                        AF.Exp), inc=P.ss)
    em.op("s", lambda sc: sc.activation(scrr[0:1, 8:9], rows[0:1, 2440:2441],
                                        AF.Exp, scale=-1.0), inc=P.ss)
    em.wait("v", P.ss)
    em.op("v", lambda v: v.tensor_single_scalar(scrr[0:1, 0:4], scrr[0:1, 0:4],
                                                -1.0, ALU.mult), inc=P.vs)
    for r in range(7):
        em.wait("v", P.vs)
        em.op("v", lambda v, _r=r: v.tensor_copy(
            scrr[0:1, 16 + _r * 4:16 + (_r + 1) * 4], scrr[0:1, 0:4]), inc=P.vs)
    em.op("v", lambda v: v.tensor_single_scalar(scrr[0:1, 9:10], scrr[0:1, 8:9],
                                                1.0, ALU.add), inc=P.vs)
    em.wait("v", P.vs)
    em.op("v", lambda v: v.reciprocal(scrr[0:1, 10:11], scrr[0:1, 9:10]),
          inc=P.vs)
    em.wait("v", P.vs)
    em.op("v", lambda v: v.tensor_single_scalar(scrr[0:1, 11:12], scrr[0:1, 10:11],
                                                1.0, ALU.add), inc=P.vs)
    em.op("v", lambda v: v.tensor_scalar(scrr[0:1, 12:13], scrr[0:1, 10:11],
                                         -1.0, 1.0, ALU.mult, ALU.add), inc=P.vs)
    cst = P.vs.n
    em.wait("t", P.vs, cst)
    bcasts = [
        (wcb, 0, 2016), (cvb, 2016, 288), (dtb16, 2304, 128),
        (Dbc, 2436, 4), (fcdb, 2560, 1024), (nwbc, 3584, 256),
    ]
    for dst, off, n in bcasts:
        for c0 in range(0, n, 512):
            w = min(512, n - c0)
            em.op("t", lambda t, _o=off + c0, _w=w: t.matmul(
                pb0[:, 0:_w], onesf32[:], rows[0:1, _o:_o + _w],
                start=True, stop=True), inc=P.ts)
            mm = P.ts.n
            em.wait("v", P.ts, mm)
            em.op("v", lambda v, _d=dst, _c=c0, _w=w: v.tensor_copy(
                _d[:, _c:_c + _w], pb0[:, 0:_w]), inc=P.vs)
            em.wait("t", P.vs)
    for dst, off, n in [(A7, 16, 28), (a1c, 11, 1), (a0c, 12, 1)]:
        em.op("t", lambda t, _o=off, _n=n: t.matmul(
            pb0[:, 0:_n], onesf32[:], scrr[0:1, _o:_o + _n],
            start=True, stop=True), inc=P.ts)
        mm = P.ts.n
        em.wait("v", P.ts, mm)
        em.op("v", lambda v, _d=dst, _n=n: v.tensor_copy(_d[:], pb0[:, 0:_n]),
              inc=P.vs)
        em.wait("t", P.vs)
    em.flush()

    u0fm = P.sbuf("u0fm", [128, T], BF16)
    u1fm = P.sbuf("u1fm", [128, T], BF16)
    h2fm = P.sbuf("h2fm", [128, T], BF16)
    mixfm = P.sbuf("mixfm", [128, T], BF16)
    zx = P.sbuf("zx", [128, 16 * 552], BF16)
    dt2f = P.sbuf("dt2f", [128, 128], F32)
    dtw = P.sbuf("dtw", [128, 128], F32)
    dts = P.sbuf("dts", [128, 128], F32)
    xc = P.sbuf("xc", [128, 16 * 288], F32)
    zs = P.sbuf("zs", [128, 4096], F32)
    ya = P.sbuf("ya", [128, 4096], F32)
    ynb = P.sbuf("ynb", [128, 4096], BF16)
    yfm = [P.sbuf("yfm", [128, T], BF16) for _ in range(2)]
    Scum = P.sbuf("Scum", [128, 64], F32)
    Rcum = P.sbuf("Rcum", [128, 64], F32)
    bcd = P.sbuf("bcd", [128, 128], F32)
    scr28 = P.sbuf("scr28", [128, 28], F32)
    scr28b = P.sbuf("scr28b", [128, 28], F32)
    fcdc = P.sbuf("fcdc", [128, 8], F32)
    rmsc = P.sbuf("rmsc", [128, 2], F32)
    scrT = P.sbuf("scrT", [128, 288], F32)
    pZ = P.psum("pZ", [128, 512], F32)
    pZ2 = P.psum("pZ2", [128, 512], F32)
    pD = P.psum("pD", [128, 512], F32)
    pO = P.psum("pO", [128, 512], F32)
    pTb = P.psum("pTb2", [128, 1024], BF16)

    def zxs(i, c0, c1):
        return zx[:, i * 552 + c0:i * 552 + c1]

    def xcs(i, c0, c1):
        return xc[:, i * 288 + c0:i * 288 + c1]

    with scope(P):
        xnfm = P.sbuf("xnfmA", [128, ND128 * 512], BF16)
        u0tok = P.sbuf("u0tok", [128, 128], BF16)
        for tc in range(4):
            em.wait("y", P.ts)
            ldfm = emit_load_xnfm(P, xn, xnfm, tc * 512, 512)
            em.wait("t", P.hs, ldfm)
            for i4 in range(4):
                i = tc * 4 + i4
                em.wait("t", P.ss)
                em.wait("t", P.vs)
                for k in range(ND128):
                    em.op("t", lambda t, _k=k, _i4=i4: t.matmul(
                        pD[:, 0:128],
                        xnfm[:, _k * 512 + _i4 * 128:_k * 512 + (_i4 + 1) * 128],
                        wdT[:, _k * 128:(_k + 1) * 128],
                        start=(_k == 0), stop=False))
                em.op("t", lambda t: t.matmul(
                    pD[:, 0:128], C["ones_row"][0:1, 0:128], bdR[:],
                    start=False, stop=True), inc=P.ts)
                dmm = P.ts.n
                em.wait("s", P.ts, dmm)
                em.op("s", lambda sc: sc.activation(u0tok[:], pD[:, 0:128],
                                                    AF.Gelu), inc=P.ss)
                gel = P.ss.n
                em.wait("t", P.ss, gel)
                em.op("t", lambda t: t.transpose(pTb[:, 0:128], u0tok[:],
                                                 C["eye"][:]), inc=P.ts)
                tp = P.ts.n
                em.wait("v", P.ts, tp)
                em.op("v", lambda v, _i=i: v.tensor_copy(
                    u0fm[:, _i * 128:(_i + 1) * 128], pTb[:, 0:128]), inc=P.vs)
        em.flush()

    def hydra(ufm, outbuf):
        em.wait("t", P.vs)
        em.wait("t", P.ss)
        for i in range(16):
            pz = pZ if i % 2 == 0 else pZ2
            if i >= 1:
                em.wait("t", P.vs)
            em.op("t", lambda t, _i=i, _p=pz: t.matmul(
                _p[:], ufm[:, _i * 128:(_i + 1) * 128], inwT[:, 0:512],
                start=True, stop=True), inc=P.ts)
            em.op("t", lambda t, _i=i, _p=pz: t.matmul(
                pD[:, 0:40] if False else pD[:, 0:40],
                ufm[:, _i * 128:(_i + 1) * 128], inwT[:, 512:552],
                start=True, stop=True), inc=P.ts)
            zmm = P.ts.n
            em.wait("v", P.ts, zmm)
            em.op("v", lambda v, _i=i, _p=pz: v.tensor_copy(
                zxs(_i, 0, 512), _p[:]), inc=P.vs)
            em.op("v", lambda v, _i=i: v.tensor_copy(
                zxs(_i, 512, 544), pD[:, 0:32]), inc=P.vs)
            em.op("v", lambda v, _i=i: v.tensor_copy(
                dt2f[:, _i * 8:(_i + 1) * 8], pD[:, 32:40]), inc=P.vs)
        inproj_done = P.vs.n
        em.wait("v", P.vs, inproj_done)
        em.op("v", lambda v: v.tensor_tensor(
            dtw[:], dt2f[:], dtb16[:], ALU.add), inc=P.vs)
        dtw_done = P.vs.n
        em.wait("s", P.vs, dtw_done)
        em.op("s", lambda sc: sc.activation(dts[:], dtw[:], AF.Abs))
        em.op("s", lambda sc: sc.activation(dts[:], dts[:], AF.Exp, scale=-1.0))
        em.op("s", lambda sc: sc.activation(dts[:], dts[:], AF.Ln, bias=1.0))
        em.op("s", lambda sc: sc.activation(dtw[:], dtw[:], AF.Relu), inc=P.ss)
        sp_done = P.ss.n
        em.wait("v", P.ss, sp_done)
        em.op("v", lambda v: v.tensor_tensor(dts[:], dts[:], dtw[:], ALU.add),
              inc=P.vs)
        dts_done = P.vs.n
        em.wait("v", P.vs, dts_done)
        for i in range(16):
            l, sh = i // 2, i % 2
            taps = [(j, (l - 6 + j) * 2 + sh) for j in range(7) if l - 6 + j >= 0]
            first = True
            for (j, ti) in taps:
                if first:
                    em.op("v", lambda v, _i=i, _j=j, _ti=ti: v.tensor_tensor(
                        xcs(_i, 0, 288), zxs(_ti, 256, 544),
                        wcb[:, _j * 288:(_j + 1) * 288], ALU.mult), inc=P.vs)
                    first = False
                else:
                    em.op("v", lambda v, _i=i, _j=j, _ti=ti: v.tensor_tensor(
                        scrT[:], zxs(_ti, 256, 544),
                        wcb[:, _j * 288:(_j + 1) * 288], ALU.mult), inc=P.vs)
                    em.wait("v", P.vs)
                    em.op("v", lambda v, _i=i: v.tensor_tensor(
                        xcs(_i, 0, 288), scrT[:], xcs(_i, 0, 288), ALU.add),
                        inc=P.vs)
                    em.wait("v", P.vs)
            em.op("v", lambda v, _i=i: v.tensor_tensor(
                xcs(_i, 0, 288), xcs(_i, 0, 288), cvb[:], ALU.add), inc=P.vs)
            em.wait("v", P.vs)
        conv_done = P.vs.n
        em.wait("s", P.vs, conv_done)
        em.op("s", lambda sc: sc.activation(xc[:], xc[:], AF.Silu))
        em.op("s", lambda sc: sc.activation(
            zs[:].rearrange("p (i c) -> p i c", c=256),
            zx[:].rearrange("p (i c) -> p i c", c=552)[:, :, 0:256], AF.Silu),
            inc=P.ss)
        silu_done = P.ss.n
        em.wait("v", P.ss, silu_done)
        for sh in range(2):
            for l in range(8):
                i = l * 2 + sh
                if l == 0:
                    em.op("v", lambda v, _i=i: v.tensor_copy(
                        Scum[:, _i * 4:(_i + 1) * 4], dts[:, _i * 8:_i * 8 + 4]),
                        inc=P.vs)
                else:
                    em.op("v", lambda v, _i=i: v.tensor_tensor(
                        Scum[:, _i * 4:(_i + 1) * 4],
                        Scum[:, (_i - 2) * 4:(_i - 1) * 4],
                        dts[:, _i * 8:_i * 8 + 4], ALU.add), inc=P.vs)
                em.wait("v", P.vs)
            for l in range(7, -1, -1):
                i = l * 2 + sh
                if l == 7:
                    em.op("v", lambda v, _i=i: v.tensor_copy(
                        Rcum[:, _i * 4:(_i + 1) * 4],
                        dts[:, _i * 8 + 4:_i * 8 + 8]), inc=P.vs)
                else:
                    em.op("v", lambda v, _i=i: v.tensor_tensor(
                        Rcum[:, _i * 4:(_i + 1) * 4],
                        Rcum[:, (_i + 2) * 4:(_i + 3) * 4],
                        dts[:, _i * 8 + 4:_i * 8 + 8], ALU.add), inc=P.vs)
                em.wait("v", P.vs)
        for sh in range(2):
            for s2 in range(8):
                for tau in range(8):
                    em.op("v", lambda v, _sh=sh, _s2=s2, _tau=tau:
                          v.scalar_tensor_tensor(
                        scrT[:, 0:16], xcs(_s2 * 2 + _sh, 256, 272), 1.0,
                        xcs(_tau * 2 + _sh, 272, 288), ALU.mult, ALU.mult,
                        accum_out=bcd[:, _sh * 64 + _s2 * 8 + _tau:
                                      _sh * 64 + _s2 * 8 + _tau + 1]), inc=P.vs)
        bcd_done = P.vs.n
        em.wait("g", P.vs, bcd_done)
        em.op("g", lambda g: g.memset(ya[:], 0.0), inc=P.gs)
        em.wait("v", P.gs)
        em.wait("v", P.vs, bcd_done)
        for sh in range(2):
            for t_ in range(8):
                s2s = [s2 for s2 in range(8) if s2 != t_]
                for idx, s2 in enumerate(s2s):
                    if s2 < t_:
                        em.op("v", lambda v, _x=idx, _s2=s2, _t=t_, _sh=sh:
                              v.tensor_tensor(
                            scr28[:, _x * 4:(_x + 1) * 4],
                            Scum[:, ((_t - 1) * 2 + _sh) * 4:
                                 ((_t - 1) * 2 + _sh + 1) * 4],
                            Scum[:, (_s2 * 2 + _sh) * 4:(_s2 * 2 + _sh + 1) * 4],
                            ALU.subtract), inc=P.vs)
                    else:
                        em.op("v", lambda v, _x=idx, _s2=s2, _t=t_, _sh=sh:
                              v.tensor_tensor(
                            scr28[:, _x * 4:(_x + 1) * 4],
                            Rcum[:, ((_t + 1) * 2 + _sh) * 4:
                                 ((_t + 1) * 2 + _sh + 1) * 4],
                            Rcum[:, (_s2 * 2 + _sh) * 4:(_s2 * 2 + _sh + 1) * 4],
                            ALU.subtract), inc=P.vs)
                em.wait("v", P.vs)
                em.op("v", lambda v: v.tensor_tensor(scr28[:], scr28[:], A7[:],
                                                     ALU.mult), inc=P.vs)
                coef_pre = P.vs.n
                em.wait("s", P.vs, coef_pre)
                em.op("s", lambda sc: sc.activation(scr28b[:], scr28[:], AF.Exp),
                      inc=P.ss)
                em.wait("v", P.ss)
                for idx, s2 in enumerate(s2s):
                    dcol = 0 if s2 < t_ else 4
                    em.op("v", lambda v, _x=idx, _s2=s2, _sh=sh, _d=dcol:
                          v.tensor_tensor(
                        scr28b[:, _x * 4:(_x + 1) * 4],
                        scr28b[:, _x * 4:(_x + 1) * 4],
                        dts[:, (_s2 * 2 + _sh) * 8 + _d:
                             (_s2 * 2 + _sh) * 8 + _d + 4], ALU.mult), inc=P.vs)
                    tau = t_ - 1 if s2 < t_ else t_ + 1
                    em.wait("v", P.vs)
                    em.op("v", lambda v, _x=idx, _s2=s2, _sh=sh, _tau=tau:
                          v.tensor_scalar(
                        scr28b[:, _x * 4:(_x + 1) * 4],
                        scr28b[:, _x * 4:(_x + 1) * 4],
                        bcd[:, _sh * 64 + _s2 * 8 + _tau:
                            _sh * 64 + _s2 * 8 + _tau + 1], None, ALU.mult),
                        inc=P.vs)
                    em.wait("v", P.vs)
                    ti = t_ * 2 + sh
                    si = s2 * 2 + sh
                    for h in range(4):
                        em.op("v", lambda v, _x=idx, _h=h, _ti=ti, _si=si:
                              v.scalar_tensor_tensor(
                            ya[:, _ti * 256 + _h * 64:_ti * 256 + (_h + 1) * 64],
                            xcs(_si, _h * 64, (_h + 1) * 64),
                            scr28b[:, _x * 4 + _h:_x * 4 + _h + 1],
                            ya[:, _ti * 256 + _h * 64:_ti * 256 + (_h + 1) * 64],
                            ALU.mult, ALU.add), inc=P.vs)
                    em.wait("v", P.vs)
        for i in range(16):
            for h in range(4):
                em.op("v", lambda v, _i=i, _h=h: v.scalar_tensor_tensor(
                    scrT[:, 0:256], xcs(_i, 0, 256), 1.0,
                    fcdb[:, _h * 256:(_h + 1) * 256], ALU.mult, ALU.mult,
                    accum_out=fcdc[:, _h:_h + 1]), inc=P.vs)
            em.wait("v", P.vs)
            em.op("v", lambda v: v.tensor_tensor(fcdc[:, 4:8], fcdc[:, 0:4],
                                                 Dbc[:], ALU.add), inc=P.vs)
            em.wait("v", P.vs)
            for h in range(4):
                em.op("v", lambda v, _i=i, _h=h: v.scalar_tensor_tensor(
                    ya[:, _i * 256 + _h * 64:_i * 256 + (_h + 1) * 64],
                    xcs(_i, _h * 64, (_h + 1) * 64), fcdc[:, 4 + _h:5 + _h],
                    ya[:, _i * 256 + _h * 64:_i * 256 + (_h + 1) * 64],
                    ALU.mult, ALU.add), inc=P.vs)
            em.wait("v", P.vs)
            em.op("v", lambda v, _i=i: v.tensor_tensor(
                ya[:, _i * 256:(_i + 1) * 256], ya[:, _i * 256:(_i + 1) * 256],
                zs[:, _i * 256:(_i + 1) * 256], ALU.mult), inc=P.vs)
            em.wait("v", P.vs)
            em.op("v", lambda v, _i=i: v.scalar_tensor_tensor(
                scrT[:, 0:256], ya[:, _i * 256:(_i + 1) * 256], 1.0,
                ya[:, _i * 256:(_i + 1) * 256], ALU.mult, ALU.mult,
                accum_out=rmsc[:, 0:1]), inc=P.vs)
            ssq_done = P.vs.n
            em.wait("s", P.vs, ssq_done)
            em.op("s", lambda sc: sc.activation(rmsc[:, 1:2], rmsc[:, 0:1],
                                                AF.Ln, scale=1.0 / 256,
                                                bias=C["eps"][:]))
            em.op("s", lambda sc: sc.activation(rmsc[:, 1:2], rmsc[:, 1:2],
                                                AF.Exp, scale=-0.5), inc=P.ss)
            rstd_done = P.ss.n
            em.wait("v", P.ss, rstd_done)
            em.op("v", lambda v, _i=i: v.scalar_tensor_tensor(
                ynb[:, _i * 256:(_i + 1) * 256], ya[:, _i * 256:(_i + 1) * 256],
                rmsc[:, 1:2], nwbc[:], ALU.mult, ALU.mult), inc=P.vs)
            ynb_done = P.vs.n
            em.wait("t", P.vs, ynb_done)
            for c in range(2):
                em.op("t", lambda t, _i=i, _c=c: t.transpose(
                    pTb[:, _c * 128:(_c + 1) * 128],
                    ynb[:, _i * 256 + _c * 128:_i * 256 + (_c + 1) * 128],
                    C["eye"][:]), inc=(P.ts if c == 1 else None))
            tp_done = P.ts.n
            em.wait("v", P.ts, tp_done)
            for c in range(2):
                em.op("v", lambda v, _i=i, _c=c: v.tensor_copy(
                    yfm[_c][:, _i * 128:(_i + 1) * 128],
                    pTb[:, _c * 128:(_c + 1) * 128]), inc=P.vs)
            em.wait("t", P.vs)
        yfm_done = P.vs.n
        em.wait("t", P.vs, yfm_done)
        for c4 in range(4):
            em.op("t", lambda t, _c=c4: t.matmul(
                pO[:], outwT[:, 0:128], yfm[0][:, _c * 512:(_c + 1) * 512],
                start=True, stop=False))
            em.op("t", lambda t, _c=c4: t.matmul(
                pO[:], outwT[:, 128:256], yfm[1][:, _c * 512:(_c + 1) * 512],
                start=False, stop=True), inc=P.ts)
            omm = P.ts.n
            em.wait("v", P.ts, omm)
            em.op("v", lambda v, _c=c4: v.tensor_copy(
                outbuf[:, _c * 512:(_c + 1) * 512], pO[:]), inc=P.vs)
            em.wait("t", P.vs)
        em.flush()

    hydra(u0fm, u1fm)
    hydra(u1fm, h2fm)

    with scope(P):
        xres = P.sbuf("xresB", [128, 512], F32)
        xnr = P.sbuf("xnrB", [128, 512], BF16)
        stage = [P.sbuf("stgB", [128, 512], F32) for _ in range(2)]
        em.wait("v", P.vs)
        em.op("v", lambda v: v.tensor_scalar(
            mixfm[:], u0fm[:], a1c[:, 0:1], None, ALU.mult), inc=P.vs)
        em.wait("v", P.vs)
        em.op("v", lambda v: v.scalar_tensor_tensor(
            mixfm[:], h2fm[:], a0c[:, 0:1], mixfm[:], ALU.mult, ALU.add),
            inc=P.vs)
        mix_done = P.vs.n
        em.wait("t", P.vs, mix_done)
        for i in range(16):
            for hc in range(8):
                pz = pZ if (i * 8 + hc) % 2 == 0 else pZ2
                if i * 8 + hc >= 2:
                    em.wait("t", P.vs)
                em.op("t", lambda t, _i=i, _h=hc, _p=pz: t.matmul(
                    _p[:], mixfm[:, _i * 128:(_i + 1) * 128],
                    wuT[:, _h * 512:(_h + 1) * 512],
                    start=True, stop=False))
                em.op("t", lambda t, _h=hc, _p=pz: t.matmul(
                    _p[:], C["ones_row"][0:1, 0:128],
                    buR[0:1, _h * 512:(_h + 1) * 512],
                    start=False, stop=True), inc=P.ts)
                umm = P.ts.n
                ldx = em.dma("g", xres[:],
                             x_ap[i * 128:(i + 1) * 128,
                                  hc * 512:(hc + 1) * 512])
                ldn = em.dma("g", xnr[:],
                             xn[i * 128:(i + 1) * 128,
                                hc * 512:(hc + 1) * 512])
                stg = stage[(i * 8 + hc) % 2]
                em.wait("v", P.ts, umm)
                em.wait("v", P.ds, ldn)
                em.op("v", lambda v, _p=pz, _s=stg: v.scalar_tensor_tensor(
                    _s[:], _p[:], 1.0, xres[:], ALU.mult, ALU.add), inc=P.vs)
                em.wait("v", P.vs)
                em.op("v", lambda v, _s=stg: v.tensor_tensor(
                    _s[:], _s[:], xnr[:], ALU.add), inc=P.vs)
                sdone = P.vs.n
                em.wait("g", P.vs, sdone)
                em.dma("g", xo_ap[i * 128:(i + 1) * 128,
                                  hc * 512:(hc + 1) * 512], stg[:])
        em.flush()
    em.wait("g", P.ds)
    em.wait("g", P.hs)
    em.flush()


def _pm(a):
    return np.ascontiguousarray(a, dtype=np.float32)


def _w13L(w, bT):
    base = np.asarray(w).T.reshape(32, 128, 86, 128).transpose(2, 1, 0, 3) \
        .reshape(86, 128, 4096)
    out = np.zeros((86, 128, 4224), np.float32)
    out[:, :, :4096] = base
    bt = np.asarray(bT).reshape(86, 128, 16).transpose(0, 2, 1)
    out[:, 0:16, 4096:4224] = bt
    return _pm(out)


def _w2L(w):
    return _pm(np.asarray(w).T.reshape(86, 128, 8, 512).transpose(2, 1, 0, 3))


HOST_LAYOUTS = {
    "l1ffn": [
        ("w1L", (86, 128, 4224), lambda i: _w13L(i["L1_ffn_w1"], i["L1_ffn_lora_B1"])),
        ("w3L", (86, 128, 4224), lambda i: _w13L(i["L1_ffn_w3"], i["L1_ffn_lora_B3"])),
        ("w2L", (8, 128, 86, 512), lambda i: _w2L(i["L1_ffn_w2"])),
        ("a1tL", (128, 512), lambda i: _pm(np.asarray(i["L1_ffn_lora_A1"]).T
            .reshape(32, 128, 16).transpose(1, 0, 2).reshape(128, 512))),
        ("a3tL", (128, 512), lambda i: _pm(np.asarray(i["L1_ffn_lora_A3"]).T
            .reshape(32, 128, 16).transpose(1, 0, 2).reshape(128, 512))),
        ("a2tL", (128, 1376), lambda i: _pm(np.asarray(i["L1_ffn_lora_A2"]).T
            .reshape(86, 128, 16).transpose(1, 0, 2).reshape(128, 1376))),
        ("b2T", (16, 4096), lambda i: _pm(np.asarray(i["L1_ffn_lora_B2"]).T)),
    ],
    "l1attn": [
        ("wdT", (128, 4096), lambda i: _pm(np.asarray(i["L1_attn_w_down"]).T
            .reshape(32, 128, 128).transpose(1, 0, 2).reshape(128, 4096))),
        ("wuT", (128, 4096), lambda i: _pm(np.asarray(i["L1_attn_w_up"]).T)),
        ("inwT", (128, 552), lambda i: _pm(np.asarray(i["L1_attn_hy_in_w"]).T)),
        ("outwT", (128, 256), lambda i: _pm(np.asarray(i["L1_attn_hy_out_w"]).T
            .reshape(2, 128, 128).transpose(1, 0, 2).reshape(128, 256))),
        ("bdR", (1, 128), lambda i: _pm(np.asarray(i["L1_attn_b_down"])
            .reshape(1, 128))),
        ("buR", (1, 4096), lambda i: _pm(np.asarray(i["L1_attn_b_up"])
            .reshape(1, 4096))),
        ("convw7R", (1, 2016), lambda i: _pm(np.asarray(i["L1_attn_hy_conv_w"])
            [:, 0, :].T.reshape(1, 2016))),
        ("convbR", (1, 288), lambda i: _pm(np.asarray(i["L1_attn_hy_conv_b"])
            .reshape(1, 288))),
        ("dtb16R", (1, 128), lambda i: _pm(np.tile(
            np.asarray(i["L1_attn_hy_dt_bias"]), 32).reshape(1, 128))),
        ("AlogR", (1, 4), lambda i: _pm(np.asarray(i["L1_attn_hy_A_log"])
            .reshape(1, 4))),
        ("DR", (1, 4), lambda i: _pm(np.asarray(i["L1_attn_hy_D"]).reshape(1, 4))),
        ("gateR", (1, 1), lambda i: _pm(np.asarray(i["L1_attn_gate1"])
            .reshape(1, 1))),
        ("fcdR", (1, 1024), lambda i: _pm(np.asarray(i["L1_attn_hy_fcD_w"])
            .reshape(1, 1024))),
        ("normwR", (1, 256), lambda i: _pm(np.asarray(i["L1_attn_hy_norm_w"])
            .reshape(1, 256))),
    ],
    "l0attn": [
        ("WqL", (128, 384), lambda i: _pm(np.asarray(i["L0_attn_Wq_d"])
            .reshape(32, 128, 12).transpose(1, 0, 2).reshape(128, 384))),
        ("WkvL", (128, 128), lambda i: _pm(np.asarray(i["L0_attn_Wkv_d"])
            .reshape(32, 128, 4).transpose(1, 0, 2).reshape(128, 128))),
        ("Wqk", (12, 128), lambda i: _pm(i["L0_attn_W_qk"])),
        ("WvU", (4, 4096), lambda i: _pm(i["L0_attn_Wv_u"])),
        ("WoL", (8, 128, 32, 512), lambda i: _pm(np.asarray(i["L0_attn_Wo"])
            .reshape(32, 128, 8, 512).transpose(2, 1, 0, 3))),
        ("bqC", (12, 1), lambda i: _pm(np.asarray(i["L0_attn_bq_d"]).reshape(12, 1))),
        ("bkvC", (4, 1), lambda i: _pm(np.asarray(i["L0_attn_bkv_d"]).reshape(4, 1))),
        ("bqkC", (128, 1), lambda i: _pm(np.asarray(i["L0_attn_b_qk"]).reshape(128, 1))),
        ("bqk4", (4, 32), lambda i: _pm(np.asarray(i["L0_attn_b_qk"]).reshape(32, 4).T)),
        ("bvuC", (128, 32), lambda i: _pm(np.asarray(i["L0_attn_bv_u"])
            .reshape(32, 128).T)),
        ("boR", (1, 4096), lambda i: _pm(np.asarray(i["L0_attn_bo"]).reshape(1, 4096))),
    ],
    "l0ffn": [
        ("rwT", (128, 256), lambda i: _pm(np.asarray(i["L0_ffn_route_w"]).T
            .reshape(32, 128, 8).transpose(1, 0, 2).reshape(128, 256))),
        ("AtL", (128, 4096), lambda i: _pm(np.asarray(i["L0_ffn_A"]).reshape(128, 4096).T
            .reshape(32, 128, 128).transpose(1, 0, 2).reshape(128, 4096))),
        ("BcatL", (128, 4096), lambda i: _pm(np.asarray(i["L0_ffn_Bw"])
            .transpose(0, 2, 1).reshape(128, 4096))),
    ],
}

BUILDERS = {}


def build_sub_program(sub):
    P = Prog()
    x_in = P.nc.dram_tensor("x_in", [T, D], F32, kind="ExternalInput").ap()
    x_out = P.nc.dram_tensor("x_out", [T, D], F32, kind="ExternalOutput").ap()
    W = {}
    for lname, shape, _fn in HOST_LAYOUTS[sub]:
        W[lname] = P.nc.dram_tensor(lname, list(shape), F32,
                                    kind="ExternalInput").ap()
    C = emit_setup(P)
    base = sub.rstrip("2")
    BUILDERS[base](P, C, x_in, x_out, W)
    P.stack.close()
    return P.nc


def shard_x(x_full):
    out = []
    for c in range(N_CORES):
        out.append(np.ascontiguousarray(
            x_full[:, c * 256:(c + 1) * 256, :]).reshape(T, D))
    return out


def unshard_x(slices):
    full = np.empty((8, 2048, 4096), np.float32)
    for c in range(N_CORES):
        full[:, c * 256:(c + 1) * 256, :] = slices[c].reshape(8, 256, D)
    return full


def run_sublayer(sub, x_full, inputs):
    nc = build_sub_program(sub)
    xs = shard_x(np.asarray(x_full, np.float32))
    wmap = {lname: fn(inputs) for lname, _s, fn in HOST_LAYOUTS[sub]}
    maps = [dict(x_in=xs[c], **wmap) for c in range(N_CORES)]
    res = run_bass_kernel_spmd(nc, maps, core_ids=list(range(N_CORES)))
    return unshard_x([np.asarray(res.results[c]["x_out"]) for c in range(N_CORES)])


BUILDERS["l0ffn"] = build_l0ffn
BUILDERS["l1ffn"] = build_l1ffn
BUILDERS["l0attn"] = build_l0attn
BUILDERS["l1attn"] = build_l1attn

SUB_ORDER = ["l0ffn", "l0attn", "l0ffn", "l1ffn", "l1attn", "l1ffn"]

_prog_cache = {}


def build_fused():
    P = Prog()
    x_in = P.nc.dram_tensor("x_in", [T, D], F32, kind="ExternalInput").ap()
    x_out = P.nc.dram_tensor("x_out", [T, D], F32, kind="ExternalOutput").ap()
    W = {}
    for sub in ("l0ffn", "l0attn", "l1ffn", "l1attn"):
        W[sub] = {ln: P.nc.dram_tensor(ln, list(shape), F32,
                                       kind="ExternalInput").ap()
                  for ln, shape, _fn in HOST_LAYOUTS[sub]}
    C = emit_setup(P)
    bufs = [P.dram(f"xb{i}", [T, D], F32).ap()
            for i in range(len(SUB_ORDER) - 1)]
    chain = [x_in] + bufs + [x_out]
    caches = emit_l1ffn_caches_decl(P) if "l1ffn" in SUB_ORDER else None
    for i, sub in enumerate(SUB_ORDER):
        with scope(P):
            if sub == "l1ffn":
                caches = build_l1ffn(P, C, chain[i], chain[i + 1],
                                     W["l1ffn"], caches)
            else:
                BUILDERS[sub](P, C, chain[i], chain[i + 1], W[sub])
        if caches is not None and i == 0:
            P.em.wait("g", P.ts)
            emit_l1ffn_caches_chunk(P, W["l1ffn"], caches, 0, 4)
            P.em.flush()
        if caches is not None and i == 1:
            P.em.wait("g", P.ts)
            emit_l1ffn_caches_chunk(P, W["l1ffn"], caches, 4, 8)
            P.em.flush()
    P.stack.close()
    return P.nc


def kernel(**inputs):
    if "fused" not in _prog_cache:
        _prog_cache["fused"] = build_fused()
    xs = shard_x(np.asarray(inputs["L0_attn_x"], np.float32))
    wmap = {}
    for sub in ("l0ffn", "l0attn", "l1ffn", "l1attn"):
        for ln, _s, fn in HOST_LAYOUTS[sub]:
            wmap[ln] = fn(inputs)
    maps = [dict(x_in=xs[c], **wmap) for c in range(N_CORES)]
    res = run_bass_kernel_spmd(_prog_cache["fused"], maps,
                               core_ids=list(range(N_CORES)))
    return unshard_x([np.asarray(res.results[c]["x_out"])
                      for c in range(N_CORES)])


# version 07
# speedup vs baseline: 1.0270x; 1.0225x over previous
import numpy as np
import ml_dtypes
from contextlib import ExitStack

import concourse.bass as bass
import concourse.mybir as mybir
from concourse.bass_utils import run_bass_kernel_spmd

F32 = mybir.dt.float32
BF16 = mybir.dt.bfloat16
AF = mybir.ActivationFunctionType
ALU = mybir.AluOpType
AX = mybir.AxisListType

N_CORES = 8
T = 2048
D = 4096
NT128 = T // 128
ND128 = D // 128
EPS = 1e-5
BF = ml_dtypes.bfloat16


class Sem:
    __slots__ = ("h", "n")

    def __init__(self, h):
        self.h = h
        self.n = 0


class Prog:
    def __init__(self):
        self.nc = bass.Bass("TRN2", target_bir_lowering=False, debug=False)
        self.stack = ExitStack()
        self._uid = 0
        self.em = Em(self)
        self.ds = self.sem("ds")
        self.hs = self.sem("hs")
        self.ts = self.sem("ts")
        self.vs = self.sem("vs")
        self.ss = self.sem("ss")
        self.gs = self.sem("gs")
        self.cast_q = []
        self.cast_done_val = 0

    def uid(self, p):
        self._uid += 1
        return f"{p}{self._uid}"

    def sem(self, name):
        return Sem(self.stack.enter_context(self.nc.semaphore(self.uid(name))))

    def sbuf(self, name, shape, dt):
        return self.stack.enter_context(self.nc.sbuf_tensor(self.uid(name), shape, dt))

    def psum(self, name, shape, dt):
        return self.stack.enter_context(self.nc.psum_tensor(self.uid(name), shape, dt))

    def dram(self, name, shape, dt, kind="Internal"):
        return self.nc.dram_tensor(self.uid(name), shape, dt, kind=kind)

    def drain_casts(self, n=1):
        if n < 0:
            n = len(self.cast_q)
        while n > 0 and self.cast_q:
            fn = self.cast_q.pop(0)
            self.em.wait("g", self.ts)
            fn()
            self.cast_done_val = self.ds.n
            n -= 1


class Em:
    ENGS = ("g", "y", "t", "v", "s")

    def __init__(self, P):
        self.P = P
        self.q = {e: [] for e in self.ENGS}

    def wait(self, e, sem, val=None):
        v = sem.n if val is None else val
        if v > 0:
            self.q[e].append(lambda eng, _s=sem.h, _v=v: eng.wait_ge(_s, _v))

    def op(self, e, build, inc=None, k=1):
        if inc is None:
            self.q[e].append(build)
            return None
        inc.n += k
        self.q[e].append(lambda eng, _b=build, _s=inc.h, _k=k: _b(eng).then_inc(_s, _k))
        return inc.n

    def dma(self, e, out, in_, **kw):
        sem = self.P.ds if e == "g" else self.P.hs
        sem.n += 16
        self.q[e].append(
            lambda eng, _o=out, _i=in_, _k=dict(kw), _s=sem.h: eng.dma_start(
                _o, _i, **_k).then_inc(_s, 16))
        return sem.n

    def flush(self):
        nc = self.P.nc
        qs = {e: lst for e, lst in self.q.items() if lst}
        self.q = {e: [] for e in self.ENGS}
        if not qs:
            return
        with nc.Block() as b:
            decs = {"g": b.gpsimd, "y": b.sync, "t": b.tensor,
                    "v": b.vector, "s": b.scalar}
            for e, lst in qs.items():
                def runner(eng, _lst=lst):
                    for f in _lst:
                        f(eng)
                decs[e](runner)


from contextlib import contextmanager


@contextmanager
def scope(P):
    st = ExitStack()
    old = P.stack
    P.stack = st
    try:
        yield
    finally:
        for e in Em.ENGS:
            P.em.wait(e, P.ds)
            P.em.wait(e, P.hs)
        P.em.flush()
        st.close()
        P.stack = old



def emit_setup(P):
    em = P.em
    C = {}
    C["eye"] = P.sbuf("eye", [128, 128], BF16)
    C["eyef"] = eyef = P.sbuf("eyef", [128, 128], F32)
    C["ones_row"] = P.sbuf("onesrow", [1, 512], BF16)
    C["sel8"] = P.sbuf("sel8", [8, 128], BF16)
    C["eps"] = P.sbuf("epsc", [128, 1], F32)
    self8 = P.sbuf("self8", [8, 128], F32)
    t8 = P.sbuf("t8", [8, 128], F32)
    em.op("g", lambda g: g.iota(eyef[:], [[1, 128]], channel_multiplier=-1,
                                allow_small_or_imprecise_dtypes=True), inc=P.gs)
    em.op("g", lambda g: g.iota(self8[:], [[1, 128]], channel_multiplier=-16,
                                allow_small_or_imprecise_dtypes=True), inc=P.gs)
    em.op("g", lambda g: g.memset(C["ones_row"][:], 1.0), inc=P.gs)
    em.op("g", lambda g: g.memset(C["eps"][:], EPS), inc=P.gs)
    em.wait("v", P.gs)
    em.op("v", lambda v: v.tensor_single_scalar(C["eye"][:], eyef[:], 0.0,
                                                ALU.is_equal), inc=P.vs)
    em.wait("v", P.vs)
    em.op("v", lambda v: v.tensor_copy(eyef[:], C["eye"][:]), inc=P.vs)
    em.op("v", lambda v: v.tensor_single_scalar(t8[:], self8[:], -0.5, ALU.is_gt),
          inc=P.vs)
    em.op("v", lambda v: v.tensor_single_scalar(self8[:], self8[:], 15.5, ALU.is_lt),
          inc=P.vs)
    em.wait("v", P.vs)
    em.op("v", lambda v: v.tensor_tensor(C["sel8"][:], t8[:], self8[:], ALU.mult),
          inc=P.vs)
    em.flush()
    return C


def emit_ln(P, x_ap, xn_ap, C=None):
    em = P.em
    xbuf = [P.sbuf("lnx", [128, D], F32) for _ in range(2)]
    xnb = [P.sbuf("lnxn", [128, D], BF16) for _ in range(2)]
    sq = P.sbuf("lnsq", [128, D], F32)
    st = [P.sbuf("lnst", [128, 6], F32) for _ in range(2)]
    app = [0] * NT128
    stor = [0] * NT128
    for i in range(NT128):
        b = st[i % 2]
        if i >= 2:
            em.wait("s", P.hs, stor[i - 2])
        ld = em.dma("s", xbuf[i % 2][:], x_ap[i * 128:(i + 1) * 128, :])
        em.wait("s", P.hs, ld)
        em.op("s", lambda sc, _i=i, _b=b: sc.activation(
            sq[:], xbuf[_i % 2][:], AF.Square, accum_out=_b[:, 1:2]), inc=P.ss)
        sq_done = P.ss.n
        em.wait("v", P.hs, ld)
        em.op("v", lambda v, _i=i, _b=b: v.reduce_sum(
            _b[:, 0:1], xbuf[_i % 2][:], axis=AX.X), inc=P.vs)
        em.wait("v", P.vs)
        em.wait("v", P.ss, sq_done)
        em.op("v", lambda v, _b=b: v.tensor_single_scalar(
            _b[:, 2:3], _b[:, 0:1], 1.0 / D, ALU.mult), inc=P.vs)
        em.wait("v", P.vs)
        em.op("v", lambda v, _b=b: v.tensor_tensor(
            _b[:, 3:4], _b[:, 2:3], _b[:, 2:3], ALU.mult), inc=P.vs)
        em.wait("v", P.vs)
        em.op("v", lambda v, _b=b: v.scalar_tensor_tensor(
            _b[:, 1:2], _b[:, 1:2], 1.0 / D, _b[:, 3:4], ALU.mult, ALU.subtract),
            inc=P.vs)
        var_done = P.vs.n
        em.wait("s", P.vs, var_done)
        em.op("s", lambda sc, _b=b: sc.activation(
            _b[:, 4:5], _b[:, 1:2], AF.Ln, bias=C["eps"][:]))
        em.op("s", lambda sc, _b=b: sc.activation(
            _b[:, 5:6], _b[:, 4:5], AF.Exp, scale=-0.5), inc=P.ss)
        rstd_done = P.ss.n
        em.wait("v", P.ss, rstd_done)
        if i >= 2:
            em.wait("v", P.hs, stor[i - 2])
        em.op("v", lambda v, _i=i, _b=b: v.tensor_scalar(
            xnb[_i % 2][:], xbuf[_i % 2][:], _b[:, 2:3], _b[:, 5:6],
            ALU.subtract, ALU.mult), inc=P.vs)
        app[i] = P.vs.n
        em.wait("y", P.vs, app[i])
        stor[i] = em.dma("y", xn_ap[i * 128:(i + 1) * 128, :], xnb[i % 2][:])
    em.flush()


def emit_cast_load(P, dst_bf, src_f32, stagebuf):
    em = P.em
    p, f = dst_bf.tensor.shape if hasattr(dst_bf, "tensor") else (None, None)
    ld = em.dma("y", stagebuf, src_f32)
    em.wait("v", P.hs, ld)
    em.op("v", lambda v: v.tensor_copy(dst_bf, stagebuf), inc=P.vs)
    return P.vs.n


def emit_load_xnfm(P, xn_ap, xnfm, tok0, ntok, waits=None):
    P.drain_casts(1)
    em = P.em
    for k in range(ND128):
        em.dma("y", xnfm[:, k * ntok:(k + 1) * ntok],
               xn_ap[tok0:tok0 + ntok, k * 128:(k + 1) * 128], transpose=True)
    return P.hs.n



def build_l0ffn(P, C, x_ap, xo_ap, W):
    em = P.em
    xn = P.dram("xn", [T, D], BF16).ap()
    with scope(P):
        emit_ln(P, x_ap, xn, C)

    rwT = P.sbuf("rwT", [128, 8 * ND128], BF16)
    acatT = P.sbuf("acatT", [128, D], BF16)
    bcat = P.sbuf("bcat", [128, D], BF16)
    with scope(P):
        stg = P.sbuf("wstg", [128, D], F32)
        emit_cast_load(P, rwT[:], W["rwT"], stg[:, 0:256])
        em.wait("y", P.vs)
        emit_cast_load(P, acatT[:], W["AtL"], stg[:])
        em.wait("y", P.vs)
        emit_cast_load(P, bcat[:], W["BcatL"], stg[:])
    wprep = P.vs.n
    em.flush()

    xnfm = P.sbuf("xnfm", [128, ND128 * 512], BF16)
    rw4 = P.sbuf("rw4", [128, 32], F32)
    mrow = P.sbuf("mrow", [128, 4], F32)
    rwTt = P.sbuf("rwTt", [8, 512], BF16)
    rwexp = P.sbuf("rwexp", [128, 512], BF16)
    wh = P.sbuf("wh", [128, 512], BF16)
    xres = P.sbuf("xres", [128, D], F32)
    stage = [P.sbuf("stage", [128, 512], F32) for _ in range(2)]
    pL = P.psum("pL", [128, 512], F32)
    pR = P.psum("pR", [8, 512], F32)
    pE = P.psum("pE", [128, 512], F32)
    pH = P.psum("pH", [128, 512], F32)
    pD = [P.psum("pD", [128, 512], F32) for _ in range(2)]
    em.wait("t", P.vs, wprep)

    for tt in range(4):
        tok0 = tt * 512
        em.wait("y", P.ts)
        ldfm = emit_load_xnfm(P, xn, xnfm, tok0, 512)
        em.wait("t", P.hs, ldfm)
        for q in range(4):
            for k in range(ND128):
                em.op("t", lambda t, _q=q, _k=k: t.matmul(
                    pL[:, _q * 8:(_q + 1) * 8],
                    xnfm[:, _k * 512 + _q * 128:_k * 512 + (_q + 1) * 128],
                    rwT[:, _k * 8:(_k + 1) * 8],
                    start=(_k == 0), stop=(_k == ND128 - 1)),
                    inc=(P.ts if k == ND128 - 1 else None))
        em.wait("v", P.ts)
        for q in range(4):
            em.op("v", lambda v, _q=q: v.reduce_max(
                mrow[:, _q:_q + 1], pL[:, _q * 8:(_q + 1) * 8], axis=AX.X),
                inc=P.vs)
            em.wait("v", P.vs)
            em.op("v", lambda v, _q=q: v.tensor_scalar(
                rw4[:, _q * 8:(_q + 1) * 8], pL[:, _q * 8:(_q + 1) * 8],
                mrow[:, _q:_q + 1], mrow[:, _q:_q + 1],
                ALU.is_equal, ALU.mult), inc=P.vs)
        rw_done = P.vs.n
        em.wait("t", P.vs, rw_done)
        for q in range(4):
            em.op("t", lambda t, _q=q: t.transpose(
                pR[:, _q * 128:(_q + 1) * 128], rw4[:, _q * 8:(_q + 1) * 8],
                C["eyef"][:]), inc=P.ts)
        em.wait("v", P.ts)
        em.op("v", lambda v: v.tensor_copy(rwTt[:], pR[:]), inc=P.vs)
        em.wait("t", P.vs)
        em.op("t", lambda t: t.matmul(pE[:], C["sel8"][:], rwTt[:],
                                      start=True, stop=True), inc=P.ts)
        em.wait("v", P.ts)
        em.op("v", lambda v: v.tensor_copy(rwexp[:], pE[:]), inc=P.vs)
        for k in range(ND128):
            em.op("t", lambda t, _k=k: t.matmul(
                pH[:], acatT[:, _k * 128:(_k + 1) * 128],
                xnfm[:, _k * 512:(_k + 1) * 512],
                start=(_k == 0), stop=(_k == ND128 - 1)),
                inc=(P.ts if k == ND128 - 1 else None))
        em.wait("v", P.ts)
        em.wait("v", P.vs)
        em.op("v", lambda v: v.tensor_tensor(wh[:], pH[:], rwexp[:], ALU.mult),
              inc=P.vs)
        wh_done = P.vs.n
        em.wait("t", P.vs, wh_done)
        stt_done = []
        store_done = []
        for q in range(4):
            em.wait("s", P.vs)
            ldx = em.dma("s", xres[:], x_ap[tok0 + q * 128:tok0 + (q + 1) * 128, :])
            for do in range(8):
                i = q * 8 + do
                pd = pD[i % 2]
                if i >= 2:
                    em.wait("t", P.vs, stt_done[i - 2])
                em.op("t", lambda t, _q=q, _do=do, _pd=pd: t.matmul(
                    _pd[:], wh[:, _q * 128:(_q + 1) * 128],
                    bcat[:, _do * 512:(_do + 1) * 512],
                    start=True, stop=True), inc=P.ts)
                mm_done = P.ts.n
                stg = stage[i % 2]
                em.wait("v", P.ts, mm_done)
                em.wait("v", P.hs, ldx)
                if i >= 2:
                    em.wait("v", P.hs, store_done[i - 2])
                em.op("v", lambda v, _pd=pd, _do=do, _stg=stg: v.scalar_tensor_tensor(
                    _stg[:], _pd[:], 0.5, xres[:, _do * 512:(_do + 1) * 512],
                    ALU.mult, ALU.add), inc=P.vs)
                stt_done.append(P.vs.n)
                em.wait("y", P.vs)
                em.dma("y", xo_ap[tok0 + q * 128:tok0 + (q + 1) * 128,
                                  do * 512:(do + 1) * 512], stg[:])
                store_done.append(P.hs.n)
        em.flush()
    em.wait("g", P.ds)
    em.flush()


def _k_last(k):
    return k == ND128 - 1




NJT = 86
SC_LORA = 2.0


DBG_L1 = False


def emit_l1ffn_caches_decl(P):
    w1b = P.dram("w1b", [NJT, 128, 4224], BF16).ap()
    w3b = P.dram("w3b", [NJT, 128, 4224], BF16).ap()
    w2b = P.dram("w2b", [8, 128, NJT, 512], BF16).ap()
    return dict(w1b=w1b, w3b=w3b, w2b=w2b, cast_done=0)


def queue_l1ffn_casts(P, W, caches):
    em = P.em
    for j0 in range(0, NJT, 4):
        j1 = min(j0 + 4, NJT)
        def fn(_j0=j0, _j1=j1):
            em.dma("g", caches["w1b"][_j0:_j1], W["w1L"][_j0:_j1])
            em.dma("g", caches["w3b"][_j0:_j1], W["w3L"][_j0:_j1])
        P.cast_q.append(fn)
    for c in range(8):
        def fn2(_c=c):
            em.dma("g", caches["w2b"][_c], W["w2L"][_c])
        P.cast_q.append(fn2)


def emit_l1ffn_caches(P, W):
    caches = emit_l1ffn_caches_decl(P)
    queue_l1ffn_casts(P, W, caches)
    return caches


def build_l1ffn(P, C, x_ap, xo_ap, W, caches=None):
    em = P.em
    xn = P.dram("xn", [T, D], BF16).ap()
    if DBG_L1:
        dbg_g = P.nc.dram_tensor("dbg_g", [4, 128, NJT * 512], BF16,
                                 kind="ExternalOutput").ap()
        dbg_xa = P.nc.dram_tensor("dbg_xa", [4, 16, 1024], BF16,
                                  kind="ExternalOutput").ap()

    if caches is None:
        caches = emit_l1ffn_caches(P, W)
    w1b, w3b, w2b = caches["w1b"], caches["w3b"], caches["w2b"]

    with scope(P):
        emit_ln(P, x_ap, xn, C)

    a1t = P.sbuf("a1t", [128, 512], BF16)
    a3t = P.sbuf("a3t", [128, 512], BF16)
    a2t = P.sbuf("a2t", [128, 1376], BF16)
    b2t = P.sbuf("b2t", [16, D], BF16)
    em.dma("g", a1t[:], W["a1tL"])
    em.dma("g", a3t[:], W["a3tL"])
    em.dma("g", a2t[:], W["a2tL"])
    em.dma("g", b2t[:], W["b2T"])
    smalls = P.ds.n
    em.flush()

    with scope(P):
        xnfm = P.sbuf("xnfm", [128, ND128 * 512], BF16)
        w1p = [P.sbuf("w1p", [128, 4224], BF16) for _ in range(2)]
        w3p = [P.sbuf("w3p", [128, 4224], BF16) for _ in range(2)]
        h1s = [P.sbuf("h1s", [128, 512], BF16) for _ in range(2)]
        g_all = P.sbuf("gall", [128, NJT * 512], BF16)
        xa1 = P.sbuf("xa1", [16, 512], BF16)
        xa3 = P.sbuf("xa3", [16, 512], BF16)
        ga2 = P.sbuf("ga2", [16, 512], BF16)
        w2buf = [P.sbuf("w2buf", [128, 4096], BF16) for _ in range(2)]
        xresd = P.sbuf("xresd", [128, 2048], F32)
        stage = [P.sbuf("stg", [128, 512], F32) for _ in range(4)]
        pb = [P.psum("pb", [128, 512], F32) for _ in range(8)]

        P.drain_casts(-1)
        for e in ("t", "y", "g", "v", "s"):
            em.wait(e, P.ds, max(smalls, P.cast_done_val))

        for tt in range(4):
            tok0 = tt * 512
            em.wait("y", P.ts)
            ldfm = emit_load_xnfm(P, xn, xnfm, tok0, 512)
            em.wait("t", P.hs, ldfm)

            em.wait("t", P.vs)
            for (bank, at, xs_out) in ((4, a1t, xa1), (5, a3t, xa3)):
                for k in range(ND128):
                    em.op("t", lambda t, _b=bank, _a=at, _k=k: t.matmul(
                        pb[_b][0:16, :], _a[:, _k * 16:(_k + 1) * 16],
                        xnfm[:, _k * 512:(_k + 1) * 512],
                        start=(_k == 0), stop=(_k == ND128 - 1)),
                        inc=(P.ts if k == ND128 - 1 else None))
                em.wait("s", P.ts)
                em.op("s", lambda sc, _b=bank, _o=xs_out: sc.activation(
                    _o[:], pb[_b][0:16, :], AF.Identity, scale=SC_LORA), inc=P.ss)
            lora_evac = P.ss.n
            em.wait("t", P.ss, lora_evac)

            t1d = [0] * NJT
            t3d = [0] * NJT
            sild = [0] * NJT
            gd = [0] * NJT
            ld_w = [0] * (NJT + 1)
            for js in range(NJT):
                if js == 0:
                    em.wait("y", P.ts)
                    em.dma("y", w1p[0][:], w1b[0])
                    em.dma("y", w3p[0][:], w3b[0])
                    ld_w[0] = P.hs.n
                if js + 1 < NJT:
                    if js >= 1:
                        em.wait("y", P.ts, t3d[js - 1])
                    em.dma("y", w1p[(js + 1) % 2][:], w1b[js + 1])
                    em.dma("y", w3p[(js + 1) % 2][:], w3b[js + 1])
                    ld_w[js + 1] = P.hs.n
                s = js % 2
                k1, k3 = 2 * s, 2 * s + 1
                em.wait("t", P.hs, ld_w[js])
                if js >= 2:
                    em.wait("t", P.ss, sild[js - 2])
                    em.wait("t", P.vs, gd[js - 2])
                for k in range(ND128):
                    em.op("t", lambda t, _p=pb[k1], _k=k, _s=s: t.matmul(
                        _p[:], w1p[_s][:, _k * 128:(_k + 1) * 128],
                        xnfm[:, _k * 512:(_k + 1) * 512],
                        start=(_k == 0), stop=False))
                em.op("t", lambda t, _p=pb[k1], _s=s: t.matmul(
                    _p[:], w1p[_s][0:16, 4096:4224], xa1[:],
                    start=False, stop=True), inc=P.ts)
                t1d[js] = P.ts.n
                for k in range(ND128):
                    em.op("t", lambda t, _p=pb[k3], _k=k, _s=s: t.matmul(
                        _p[:], w3p[_s][:, _k * 128:(_k + 1) * 128],
                        xnfm[:, _k * 512:(_k + 1) * 512],
                        start=(_k == 0), stop=False))
                em.op("t", lambda t, _p=pb[k3], _s=s: t.matmul(
                    _p[:], w3p[_s][0:16, 4096:4224], xa3[:],
                    start=False, stop=True), inc=P.ts)
                t3d[js] = P.ts.n
                em.wait("s", P.ts, t1d[js])
                if js >= 2:
                    em.wait("s", P.vs, gd[js - 2])
                em.op("s", lambda sc, _o=h1s[s], _p=pb[k1]: sc.activation(
                    _o[:], _p[:], AF.Silu), inc=P.ss)
                sild[js] = P.ss.n
                em.wait("v", P.ts, t3d[js])
                em.wait("v", P.ss, sild[js])
                em.op("v", lambda v, _i=h1s[s], _p=pb[k3], _js=js: v.tensor_tensor(
                    g_all[:, _js * 512:(_js + 1) * 512], _i[:], _p[:], ALU.mult),
                    inc=P.vs)
                gd[js] = P.vs.n

            if DBG_L1:
                em.wait("g", P.vs, gd[NJT - 1])
                em.dma("g", dbg_g[tt], g_all[:])
                em.wait("g", P.ss)
                em.dma("g", dbg_xa[tt, :, 0:512], xa1[:])
                em.dma("g", dbg_xa[tt, :, 512:1024], xa3[:])
            em.wait("t", P.vs, gd[NJT - 1])
            em.wait("t", P.ss, sild[NJT - 1])
            for hc in range(NJT):
                em.op("t", lambda t, _h=hc: t.matmul(
                    pb[0][0:16, :], a2t[:, _h * 16:(_h + 1) * 16],
                    g_all[:, _h * 512:(_h + 1) * 512],
                    start=(_h == 0), stop=(_h == NJT - 1)),
                    inc=(P.ts if hc == NJT - 1 else None))
            em.wait("s", P.ts)
            em.op("s", lambda sc: sc.activation(ga2[:], pb[0][0:16, :],
                                                AF.Identity, scale=SC_LORA),
                  inc=P.ss)
            ga2_done = P.ss.n
            em.wait("t", P.ss, ga2_done)

            sttd = {}
            for dof in range(8):
                ldx = 0
                for q in range(4):
                    ldx = em.dma("g", xresd[:, q * 512:(q + 1) * 512],
                                 x_ap[tok0 + q * 128:tok0 + (q + 1) * 128,
                                      dof * 512:(dof + 1) * 512])
                grp_n = (NJT + 7) // 8
                ld_g = [0] * grp_n
                tw = [0] * grp_n

                def issue_grp(grp, _dof=dof):
                    h0, h1_ = grp * 8, min((grp + 1) * 8, NJT)
                    nh = h1_ - h0
                    em.dma("y", w2buf[grp % 2][:, 0:nh * 512],
                           w2b[_dof][:, h0:h1_, :])
                    ld_g[grp] = P.hs.n

                em.wait("y", P.ts)
                issue_grp(0)
                issue_grp(1)
                pq = [pb[4 * (dof % 2) + q] for q in range(4)]
                if dof >= 2:
                    for q in range(4):
                        em.wait("t", P.vs, sttd[(dof - 2, q)])
                for grp in range(grp_n):
                    h0, h1_ = grp * 8, min((grp + 1) * 8, NJT)
                    em.wait("t", P.hs, ld_g[grp])
                    for hc in range(h0, h1_):
                        for q in range(4):
                            last = (hc == h1_ - 1 and q == 3)
                            em.op("t", lambda t, _h=hc, _q=q, _g=grp, _p=pq[q]: t.matmul(
                                _p[:],
                                g_all[:, _h * 512 + _q * 128:_h * 512 + (_q + 1) * 128],
                                w2buf[_g % 2][:, (_h - _g * 8) * 512:
                                              (_h - _g * 8 + 1) * 512],
                                start=(_h == 0), stop=False),
                                inc=(P.ts if last else None))
                    tw[grp] = P.ts.n
                    if grp + 2 < grp_n:
                        em.wait("y", P.ts, tw[grp])
                        issue_grp(grp + 2)
                for q in range(4):
                    em.op("t", lambda t, _q=q, _p=pq[q], _d=dof: t.matmul(
                        _p[:], ga2[:, _q * 128:(_q + 1) * 128],
                        b2t[:, _d * 512:(_d + 1) * 512],
                        start=False, stop=True), inc=P.ts)
                    mmq = P.ts.n
                    em.wait("v", P.ts, mmq)
                    em.wait("v", P.ds, ldx)
                    stg = stage[q]
                    em.op("v", lambda v, _q=q, _s=stg, _p=pq[q]: v.scalar_tensor_tensor(
                        _s[:], _p[:], 0.5, xresd[:, _q * 512:(_q + 1) * 512],
                        ALU.mult, ALU.add), inc=P.vs)
                    sttd[(dof, q)] = P.vs.n
                    em.wait("g", P.vs, sttd[(dof, q)])
                    em.dma("g", xo_ap[tok0 + q * 128:tok0 + (q + 1) * 128,
                                      dof * 512:(dof + 1) * 512], stg[:])
            em.flush()
    em.wait("g", P.ds)
    em.wait("g", P.hs)
    em.flush()
    return caches



L0A_STAGE = 9
L0A_DBG = False


def build_l0attn(P, C, x_ap, xo_ap, W):
    em = P.em
    xn = P.dram("xn", [T, D], BF16).ap()
    with scope(P):
        emit_ln(P, x_ap, xn, C)

    cc = P.sem("cc")
    ckv_loc = P.dram("ckvloc", [1, 5 * T], BF16)
    ckv_lrows = ckv_loc.ap().rearrange("a (d t) -> (a d) t", d=5)
    ag = P.dram("ag", [8, 5 * T], BF16)
    m2d = P.dram("m2d", [32, T], BF16).ap()
    ckv5d = P.dram("ckv5d", [8, 128, 80], BF16).ap()

    def ag_rows(d0, d1, b):
        return (ag.ap().rearrange("r (d t) -> d r t", d=5)
                [d0:d1, :, b * 256:(b + 1) * 256])

    wqk = P.sbuf("wqk", [12, 128], BF16)
    wvu = P.sbuf("wvu", [4, D], BF16)
    bqc = P.sbuf("bqc", [12, 1], F32)
    bkvc = P.sbuf("bkvc", [4, 1], F32)
    bqkh = P.sbuf("bqkh", [128, 1], F32)
    bqk4h = P.sbuf("bqk4h", [4, 32], F32)
    bvuC = P.sbuf("bvuC", [128, 32], F32)
    boR = P.sbuf("boR", [1, D], BF16)
    cqT = P.sbuf("cqT", [12, T], BF16)
    neg1r = P.sbuf("neg1r", [1, 4], BF16)
    ones4 = P.sbuf("ones4", [4, 1], BF16)
    onesf = P.sbuf("onesf", [1, 512], F32)
    mc4row = P.sbuf("mc4row", [1, T], BF16)
    em.dma("y", bqc[:], W["bqC"])
    em.dma("y", bkvc[:], W["bkvC"])
    em.dma("y", bqkh[:], W["bqkC"])
    em.dma("y", bqk4h[:], W["bqk4"])
    em.dma("y", bvuC[:], W["bvuC"])
    with scope(P):
        stgA = P.sbuf("stgA", [128, 4096], F32)
        emit_cast_load(P, wqk[:], W["Wqk"], stgA[0:12, 0:128])
        em.wait("y", P.vs)
        emit_cast_load(P, wvu[:], W["WvU"], stgA[0:4, :])
        em.wait("y", P.vs)
        emit_cast_load(P, boR[:], W["boR"], stgA[0:1, :])
    em.op("g", lambda g: g.memset(neg1r[:], -1.0), inc=P.gs)
    em.op("g", lambda g: g.memset(ones4[:], 1.0), inc=P.gs)
    em.op("g", lambda g: g.memset(onesf[:], 1.0), inc=P.gs)
    em.wait("v", P.hs)
    em.wait("v", P.vs)
    em.wait("v", P.gs)
    em.op("v", lambda v: v.tensor_single_scalar(bqkh[:], bqkh[:], 0.5, ALU.mult),
          inc=P.vs)
    em.op("v", lambda v: v.tensor_single_scalar(bqk4h[:], bqk4h[:], 0.5, ALU.mult),
          inc=P.vs)
    s4 = P.sbuf("s4", [128, 32], BF16)
    s4f = P.sbuf("s4f", [128, 32], F32)
    s4g = P.sbuf("s4g", [128, 32], F32)
    em.op("g", lambda g: g.iota(s4f[:], [[4, 32]], channel_multiplier=-1,
                                allow_small_or_imprecise_dtypes=True), inc=P.gs)
    em.wait("v", P.gs)
    em.op("v", lambda v: v.tensor_single_scalar(s4g[:], s4f[:], -3.5, ALU.is_gt),
          inc=P.vs)
    em.op("v", lambda v: v.tensor_single_scalar(s4f[:], s4f[:], 0.5, ALU.is_lt),
          inc=P.vs)
    em.wait("v", P.vs)
    em.op("v", lambda v: v.tensor_tensor(s4[:], s4g[:], s4f[:], ALU.mult),
          inc=P.vs)
    em.flush()

    with scope(P):
        xnfm = P.sbuf("xnfm", [128, ND128 * 512], BF16)
        wqL = P.sbuf("wqL", [128, 384], BF16)
        wkvL = P.sbuf("wkvL", [128, 128], BF16)
        ckvT = P.sbuf("ckvT", [4, T], BF16)
        neg1T = P.sbuf("neg1T", [1, T], BF16)
        qfm = P.sbuf("qfm", [128, 512], BF16)
        qsq = P.sbuf("qsq", [128, T], BF16)
        m2sb = P.sbuf("m2sb", [32, T], BF16)
        casq = P.sbuf("casq", [4, T], BF16)
        cnorms = P.sbuf("cnorms", [1, T], F32)
        mcb = P.sbuf("mcb", [1, 2], F32)
        ckvrows = P.sbuf("ckvrows", [5, T], BF16)
        ckv5s = P.sbuf("ckv5s", [128, 80], BF16)
        pA = P.psum("pA", [128, 512], F32)
        pB = P.psum("pB", [128, 512], F32)
        pC = P.psum("pC", [128, 512], F32)
        pTr = P.psum("pTr", [128, 1024], BF16)
        stgB = P.sbuf("stgB", [128, 384], F32)
        emit_cast_load(P, wqL[:], W["WqL"], stgB[:])
        em.wait("y", P.vs)
        emit_cast_load(P, wkvL[:], W["WkvL"], stgB[:, 0:128])
        em.op("g", lambda g: g.memset(neg1T[:], -1.0), inc=P.gs)
        wld = P.vs.n
        em.wait("t", P.vs, wld)
        for tc in range(4):
            em.wait("y", P.ts)
            ldfm = emit_load_xnfm(P, xn, xnfm, tc * 512, 512)
            em.wait("t", P.hs, ldfm)
            em.wait("t", P.ss)
            for k in range(ND128):
                em.op("t", lambda t, _k=k: t.matmul(
                    pA[0:4, :], wkvL[:, _k * 4:(_k + 1) * 4],
                    xnfm[:, _k * 512:(_k + 1) * 512],
                    start=(_k == 0), stop=(_k == ND128 - 1)),
                    inc=(P.ts if k == ND128 - 1 else None))
            em.wait("s", P.ts)
            em.op("s", lambda sc, _tc=tc: sc.activation(
                ckvT[:, _tc * 512:(_tc + 1) * 512], pA[0:4, :], AF.Identity,
                bias=bkvc[:]), inc=P.ss)
            for k in range(ND128):
                em.op("t", lambda t, _k=k: t.matmul(
                    pB[0:12, :], wqL[:, _k * 12:(_k + 1) * 12],
                    xnfm[:, _k * 512:(_k + 1) * 512],
                    start=(_k == 0), stop=(_k == ND128 - 1)),
                    inc=(P.ts if k == ND128 - 1 else None))
            em.wait("s", P.ts)
            em.op("s", lambda sc, _tc=tc: sc.activation(
                cqT[:, _tc * 512:(_tc + 1) * 512], pB[0:12, :], AF.Identity,
                bias=bqc[:]), inc=P.ss)
            cq_done = P.ss.n
            em.wait("t", P.ss, cq_done)
            em.op("t", lambda t, _tc=tc: t.matmul(
                pC[:], wqk[:], cqT[:, _tc * 512:(_tc + 1) * 512],
                start=True, stop=True), inc=P.ts)
            em.wait("s", P.ts)
            em.op("s", lambda sc: sc.activation(
                qfm[:], pC[:], AF.Identity, bias=bqkh[:], scale=0.5), inc=P.ss)
            em.wait("s", P.ss)
            em.op("s", lambda sc, _tc=tc: sc.activation(
                qsq[:, _tc * 512:(_tc + 1) * 512], qfm[:], AF.Square), inc=P.ss)
        if L0A_STAGE <= 0:
            em.wait("g", P.ds)
            em.flush()
            return
        em.wait("y", P.ss)
        em.wait("y", P.gs)
        em.dma("y", ckv_lrows[0:4, :], ckvT[:])
        em.dma("y", ckv_lrows[4:5, :], neg1T[:])
        em.wait("g", P.hs)
        em.wait("g", P.ds)
        em.op("g", lambda g: g.collective_compute(
            "AllGather", ALU.bypass, replica_groups=[list(range(N_CORES))],
            ins=[ckv_loc.ap().opt()], outs=[ag.ap().opt()]).then_inc(cc.h))
        em.wait("g", cc, 1)
        em.wait("y", cc, 1)
        em.wait("t", cc, 1)
        em.wait("s", cc, 1)
        em.wait("v", cc, 1)
        if L0A_STAGE <= 1:
            em.wait("g", P.ds)
            em.flush()
            return
        for b in range(8):
            P.drain_casts(1)
            em.wait("y", P.ts)
            ld = em.dma("y", ckvrows[:], ag_rows(0, 5, b))
            em.wait("s", P.hs, ld)
            em.op("s", lambda sc: sc.activation(casq[:], ckvrows[0:4, :],
                                                AF.Square), inc=P.ss)
            sq_done = P.ss.n
            em.wait("t", P.ss, sq_done)
            em.wait("t", P.vs)
            for tc in range(4):
                em.wait("t", P.vs)
                em.op("t", lambda t, _tc=tc: t.matmul(
                    pA[0:1, :], ones4[:], casq[:, _tc * 512:(_tc + 1) * 512],
                    start=True, stop=True), inc=P.ts)
                nmm = P.ts.n
                em.wait("v", P.ts, nmm)
                em.op("v", lambda v, _tc=tc: v.tensor_copy(
                    cnorms[0:1, _tc * 512:(_tc + 1) * 512], pA[0:1, :]), inc=P.vs)
            em.wait("v", P.vs)
            em.op("v", lambda v: v.reduce_max(mcb[0:1, 0:1], cnorms[:], axis=AX.X),
                  inc=P.vs)
            em.wait("v", P.vs)
            em.op("v", lambda v: v.tensor_single_scalar(
                mcb[0:1, 1:2], mcb[0:1, 0:1], 0.25, ALU.mult), inc=P.vs)
            em.wait("v", P.vs)
            em.op("v", lambda v, _b=b: v.tensor_scalar(
                mc4row[0:1, _b * 256:(_b + 1) * 256], onesf[0:1, 0:256],
                mcb[0:1, 1:2], None, ALU.mult), inc=P.vs)
            for kt in range(16):
                em.op("t", lambda t, _kt=kt: t.transpose(
                    pTr[:, _kt * 8:_kt * 8 + 5],
                    ckvrows[:, _kt * 128:(_kt + 1) * 128], C["eye"][0:5, 0:5]),
                    inc=(P.ts if kt == 15 else None))
            em.wait("v", P.ts)
            em.op("v", lambda v: v.tensor_copy(
                ckv5s[:].rearrange("p (k c) -> p k c", c=5),
                pTr[:, 0:128].rearrange("p (k c) -> p k c", c=8)[:, :, 0:5]),
                inc=P.vs)
            em.wait("y", P.vs)
            em.dma("y", ckv5d[b], ckv5s[:])
        mc_done = P.vs.n
        em.wait("t", P.vs, mc_done)
        em.wait("t", P.ss)
        for tc in range(4):
            em.wait("t", P.vs)
            em.op("t", lambda t, _tc=tc: t.matmul(
                pB[0:32, :], s4[:], qsq[:, _tc * 512:(_tc + 1) * 512],
                start=True, stop=False))
            em.op("t", lambda t, _tc=tc: t.matmul(
                pB[0:32, :], C["ones_row"][0:1, 0:32],
                mc4row[0:1, _tc * 512:(_tc + 1) * 512],
                start=False, stop=True), inc=P.ts)
            m2mm = P.ts.n
            em.wait("v", P.ts, m2mm)
            em.op("v", lambda v, _tc=tc: v.tensor_copy(
                m2sb[:, _tc * 512:(_tc + 1) * 512], pB[0:32, :]), inc=P.vs)
        em.wait("y", P.vs)
        em.dma("y", m2d, m2sb[:])
    if L0A_STAGE <= 2:
        em.wait("g", P.ds)
        em.flush()
        return

    if L0A_DBG:
        dmp = P.nc.dram_tensor("dmp", [64, T], BF16, kind="ExternalOutput").ap()
        em.wait("y", P.ss)
        em.wait("y", P.vs)
        em.dma("y", dmp[0:4, :], ckvT[:])
        em.dma("y", dmp[4:16, :], cqT[:])
        em.dma("y", dmp[16:48, :], m2sb[:])
        em.dma("y", dmp[48:49, :], mc4row[:])
        em.dma("y", dmp[49:50, 0:128], qsq[0:1, 0:128])
        em.wait("y", P.hs)
    with scope(P):
        qa = [P.sbuf("qa", [5, 8192], BF16) for _ in range(2)]
        ckva = [P.sbuf("ckva", [5, T], BF16) for _ in range(2)]
        ckv5 = [P.sbuf("ckv5", [128, 80], BF16) for _ in range(2)]
        pTb = [P.sbuf("pTbuf", [128, 8192], BF16) for _ in range(2)]
        c5sb = P.sbuf("c5sb", [5, 512], F32)
        ell = P.sbuf("ell", [1, 512], F32)
        ellf = P.sbuf("ellf", [1, 512], F32)
        ellr = P.sbuf("ellr", [1, 512], BF16)
        acn = P.sbuf("acn", [4, 512], BF16)
        afm = [P.sbuf("afm", [128, 1024], BF16) for _ in range(32)]
        wob = [P.sbuf("wob", [128, 4096], BF16) for _ in range(2)]
        xres = P.sbuf("xres", [128, 512], F32)
        stage = [P.sbuf("stg", [128, 512], F32) for _ in range(2)]
        pS = [P.psum("pS", [128, 512], F32) for _ in range(3)]
        pC5 = P.psum("pC5", [128, 512], F32)
        pR = P.psum("pR", [128, 512], F32)
        pV = [P.psum("pV", [128, 512], F32) for _ in range(2)]
        pX = P.psum("pX", [128, 512], F32)

        for half in range(2):
            for bi in range(4):
                b = half * 4 + bi
                s = b % 2
                P.drain_casts(1)
                em.wait("t", P.vs)
                em.wait("y", P.ts)
                lda = em.dma("y", ckva[s][:], ag_rows(0, 5, b))
                em.dma("y", qa[s][4:5, :], m2d[:, b * 256:(b + 1) * 256])
                em.dma("y", ckv5[s][:], ckv5d[b])
                ldm = P.hs.n
                em.wait("s", P.ts)
                for h in range(32):
                    em.wait("t", P.ss)
                    em.op("t", lambda t, _h=h, _b=b: t.matmul(
                        pX[0:4, 0:256], wqk[:, _h * 4:(_h + 1) * 4],
                        cqT[:, _b * 256:(_b + 1) * 256],
                        start=True, stop=True), inc=P.ts)
                    qmm = P.ts.n
                    em.wait("s", P.ts, qmm)
                    em.op("s", lambda sc, _s=s, _h=h: sc.activation(
                        qa[_s][0:4, _h * 256:(_h + 1) * 256], pX[0:4, 0:256],
                        AF.Identity, bias=bqk4h[:, _h:_h + 1], scale=0.5),
                        inc=P.ss)
                qa_done = P.ss.n
                em.wait("t", P.ss, qa_done)
                em.wait("t", P.hs, ldm)
                expd = {}
                for hh in range(16):
                    for kt in range(16):
                        pk = pS[kt % 2]
                        if kt >= 2:
                            em.wait("t", P.ss, expd[kt - 2])
                        em.op("t", lambda t, _s=s, _kt=kt, _hh=hh, _pk=pk: t.matmul(
                            _pk[:], ckva[_s][:, _kt * 128:(_kt + 1) * 128],
                            qa[_s][:, _hh * 512:(_hh + 1) * 512],
                            start=True, stop=True), inc=P.ts)
                        smm = P.ts.n
                        em.wait("s", P.ts, smm)
                        em.op("s", lambda sc, _s=s, _kt=kt, _pk=pk: sc.activation(
                            pTb[_s][:, _kt * 512:(_kt + 1) * 512], _pk[:], AF.Exp),
                            inc=P.ss)
                        expd[kt] = P.ss.n
                    exp_done = P.ss.n
                    em.wait("t", P.ss, exp_done)
                    for kt in range(16):
                        em.op("t", lambda t, _s=s, _kt=kt: t.matmul(
                            pC5[0:5, :], ckv5[_s][:, _kt * 5:(_kt + 1) * 5],
                            pTb[_s][:, _kt * 512:(_kt + 1) * 512],
                            start=(_kt == 0), stop=(_kt == 15)),
                            inc=(P.ts if kt == 15 else None))
                    c5_done = P.ts.n
                    em.wait("v", P.ts, c5_done)
                    em.op("v", lambda v: v.tensor_copy(c5sb[:], pC5[0:5, :]),
                          inc=P.vs)
                    em.wait("y", P.vs)
                    em.dma("y", ell[:], c5sb[4:5, :])
                    elld = P.hs.n
                    em.wait("v", P.hs, elld)
                    em.op("v", lambda v: v.reciprocal(ellf[:], ell[:]), inc=P.vs)
                    em.wait("v", P.vs)
                    em.op("v", lambda v: v.tensor_copy(ellr[:], ellf[:]), inc=P.vs)
                    em.wait("t", P.vs)
                    em.op("t", lambda t: t.matmul(pR[0:4, :], neg1r[:], ellr[:],
                                                  start=True, stop=True), inc=P.ts)
                    rbc_done = P.ts.n
                    em.wait("v", P.ts, rbc_done)
                    em.op("v", lambda v: v.tensor_tensor(
                        acn[:], c5sb[0:4, :], pR[0:4, :], ALU.mult), inc=P.vs)
                    acn_done = P.vs.n
                    em.wait("t", P.vs, acn_done)
                    em.wait("t", P.ss)
                    for j in range(2):
                        h = hh * 2 + j
                        em.op("t", lambda t, _h=h, _j=j: t.matmul(
                            pV[_j][:, 0:256], wvu[:, _h * 128:(_h + 1) * 128],
                            acn[:, _j * 256:(_j + 1) * 256],
                            start=True, stop=True), inc=P.ts)
                        vmm = P.ts.n
                        em.wait("s", P.ts, vmm)
                        em.op("s", lambda sc, _h=h, _j=j, _bi=bi: sc.activation(
                            afm[_h][:, _bi * 256:(_bi + 1) * 256],
                            pV[_j][:, 0:256], AF.Identity,
                            bias=bvuC[:, _h:_h + 1]), inc=P.ss)
            if L0A_STAGE <= 3:
                em.flush()
                continue
            em.wait("t", P.vs)
            em.wait("t", P.ss)
            hb = half * 1024
            sttd = {}
            for dof in range(8):
                P.drain_casts(1)
                ldg = [0] * 4
                twd = [0] * 4

                def issue_wo(grp, _dof=dof):
                    em.dma("g", wob[grp % 2][:],
                           W["WoL"][_dof][:, grp * 8:(grp + 1) * 8, :])
                    ldg[grp] = P.ds.n

                em.wait("g", P.ts)
                issue_wo(0)
                issue_wo(1)
                pq = [pS[0], pS[1], pS[2], pC5, pR, pV[0], pV[1], pX]
                if dof >= 1:
                    for tk in range(8):
                        em.wait("t", P.vs, sttd.get((dof - 1, tk), 0))
                for grp in range(4):
                    em.wait("t", P.ds, ldg[grp])
                    for a in range(grp * 8, grp * 8 + 8):
                        ws = (a - grp * 8) * 512
                        for tk in range(8):
                            em.op("t", lambda t, _a=a, _tk=tk, _g=grp, _w=ws,
                                  _hb=hb, _p=pq[tk]: t.matmul(
                                _p[:], afm[_a][:, (_hb // 128 % 8) * 0 + _tk * 128:
                                               (_tk + 1) * 128],
                                wob[_g % 2][:, _w:_w + 512],
                                start=(_a == 0), stop=False),
                                inc=(P.ts if (a == grp * 8 + 7 and tk == 7)
                                     else None))
                    twd[grp] = P.ts.n
                    if grp + 2 < 4:
                        em.wait("g", P.ts, twd[grp])
                        issue_wo(grp + 2)
                for tk in range(8):
                    em.op("t", lambda t, _tk=tk, _d=dof, _p=pq[tk]: t.matmul(
                        _p[:], C["ones_row"][0:1, 0:128],
                        boR[0:1, _d * 512:(_d + 1) * 512],
                        start=False, stop=True), inc=P.ts)
                    mmd = P.ts.n
                    em.wait("s", P.vs)
                    ldx = em.dma("s", xres[:],
                                 x_ap[hb + tk * 128:hb + (tk + 1) * 128,
                                      dof * 512:(dof + 1) * 512])
                    em.wait("v", P.ts, mmd)
                    em.wait("v", P.hs, ldx)
                    stg = stage[tk % 2]
                    em.op("v", lambda v, _s=stg, _p=pq[tk]: v.scalar_tensor_tensor(
                        _s[:], _p[:], 1.0, xres[:], ALU.mult, ALU.add), inc=P.vs)
                    sttd[(dof, tk)] = P.vs.n
                    em.wait("y", P.vs, sttd[(dof, tk)])
                    em.dma("y", xo_ap[hb + tk * 128:hb + (tk + 1) * 128,
                                      dof * 512:(dof + 1) * 512], stg[:])
            em.flush()
    em.wait("g", P.ds)
    em.wait("g", P.hs)
    em.flush()



def build_l1attn(P, C, x_ap, xo_ap, W):
    em = P.em
    xn = P.dram("xn", [T, D], BF16).ap()
    with scope(P):
        emit_ln(P, x_ap, xn, C)

    wdT = P.sbuf("wdT", [128, 4096], BF16)
    wuT = P.sbuf("wuT", [128, 4096], BF16)
    inwT = P.sbuf("inwT", [128, 552], BF16)
    outwT = P.sbuf("outwT", [128, 256], BF16)
    bdR = P.sbuf("bdR", [1, 128], BF16)
    buR = P.sbuf("buR", [1, 4096], BF16)
    rows = P.sbuf("rows", [1, 3840], F32)
    onesf32 = P.sbuf("onesf32", [1, 128], F32)
    em.dma("g", wdT[:], W["wdT"])
    em.dma("g", wuT[:], W["wuT"])
    em.dma("g", inwT[:], W["inwT"])
    em.dma("g", outwT[:], W["outwT"])
    em.dma("g", bdR[:], W["bdR"])
    em.dma("g", buR[:], W["buR"])
    em.dma("g", rows[0:1, 0:2016], W["convw7R"])
    em.dma("g", rows[0:1, 2016:2304], W["convbR"])
    em.dma("g", rows[0:1, 2304:2432], W["dtb16R"])
    em.dma("g", rows[0:1, 2432:2436], W["AlogR"])
    em.dma("g", rows[0:1, 2436:2440], W["DR"])
    em.dma("g", rows[0:1, 2440:2441], W["gateR"])
    em.dma("g", rows[0:1, 2560:3584], W["fcdR"])
    em.dma("g", rows[0:1, 3584:3840], W["normwR"])
    em.op("g", lambda g: g.memset(onesf32[:], 1.0), inc=P.gs)
    ld0 = P.ds.n
    em.flush()

    wcb = P.sbuf("wcb", [128, 2016], F32)
    cvb = P.sbuf("cvb", [128, 288], F32)
    dtb16 = P.sbuf("dtb16", [128, 128], F32)
    A7 = P.sbuf("A7", [128, 28], F32)
    Dbc = P.sbuf("Dbc", [128, 4], F32)
    fcdb = P.sbuf("fcdb", [128, 1024], F32)
    nwbc = P.sbuf("nwbc", [128, 256], F32)
    a0c = P.sbuf("a0c", [128, 1], F32)
    a1c = P.sbuf("a1c", [128, 1], F32)
    scrr = P.sbuf("scrr", [1, 64], F32)
    pb0 = P.psum("pcb", [128, 512], F32)
    em.wait("s", P.ds, ld0)
    em.wait("s", P.gs)
    em.op("s", lambda sc: sc.activation(scrr[0:1, 0:4], rows[0:1, 2432:2436],
                                        AF.Exp), inc=P.ss)
    em.op("s", lambda sc: sc.activation(scrr[0:1, 8:9], rows[0:1, 2440:2441],
                                        AF.Exp, scale=-1.0), inc=P.ss)
    em.wait("v", P.ss)
    em.op("v", lambda v: v.tensor_single_scalar(scrr[0:1, 0:4], scrr[0:1, 0:4],
                                                -1.0, ALU.mult), inc=P.vs)
    for r in range(7):
        em.wait("v", P.vs)
        em.op("v", lambda v, _r=r: v.tensor_copy(
            scrr[0:1, 16 + _r * 4:16 + (_r + 1) * 4], scrr[0:1, 0:4]), inc=P.vs)
    em.op("v", lambda v: v.tensor_single_scalar(scrr[0:1, 9:10], scrr[0:1, 8:9],
                                                1.0, ALU.add), inc=P.vs)
    em.wait("v", P.vs)
    em.op("v", lambda v: v.reciprocal(scrr[0:1, 10:11], scrr[0:1, 9:10]),
          inc=P.vs)
    em.wait("v", P.vs)
    em.op("v", lambda v: v.tensor_single_scalar(scrr[0:1, 11:12], scrr[0:1, 10:11],
                                                1.0, ALU.add), inc=P.vs)
    em.op("v", lambda v: v.tensor_scalar(scrr[0:1, 12:13], scrr[0:1, 10:11],
                                         -1.0, 1.0, ALU.mult, ALU.add), inc=P.vs)
    cst = P.vs.n
    em.wait("t", P.vs, cst)
    bcasts = [
        (wcb, 0, 2016), (cvb, 2016, 288), (dtb16, 2304, 128),
        (Dbc, 2436, 4), (fcdb, 2560, 1024), (nwbc, 3584, 256),
    ]
    for dst, off, n in bcasts:
        for c0 in range(0, n, 512):
            w = min(512, n - c0)
            em.op("t", lambda t, _o=off + c0, _w=w: t.matmul(
                pb0[:, 0:_w], onesf32[:], rows[0:1, _o:_o + _w],
                start=True, stop=True), inc=P.ts)
            mm = P.ts.n
            em.wait("v", P.ts, mm)
            em.op("v", lambda v, _d=dst, _c=c0, _w=w: v.tensor_copy(
                _d[:, _c:_c + _w], pb0[:, 0:_w]), inc=P.vs)
            em.wait("t", P.vs)
    for dst, off, n in [(A7, 16, 28), (a1c, 11, 1), (a0c, 12, 1)]:
        em.op("t", lambda t, _o=off, _n=n: t.matmul(
            pb0[:, 0:_n], onesf32[:], scrr[0:1, _o:_o + _n],
            start=True, stop=True), inc=P.ts)
        mm = P.ts.n
        em.wait("v", P.ts, mm)
        em.op("v", lambda v, _d=dst, _n=n: v.tensor_copy(_d[:], pb0[:, 0:_n]),
              inc=P.vs)
        em.wait("t", P.vs)
    em.flush()

    u0fm = P.sbuf("u0fm", [128, T], BF16)
    u1fm = P.sbuf("u1fm", [128, T], BF16)
    h2fm = P.sbuf("h2fm", [128, T], BF16)
    mixfm = P.sbuf("mixfm", [128, T], BF16)
    zx = P.sbuf("zx", [128, 16 * 552], BF16)
    dt2f = P.sbuf("dt2f", [128, 128], F32)
    dtw = P.sbuf("dtw", [128, 128], F32)
    dts = P.sbuf("dts", [128, 128], F32)
    xc = P.sbuf("xc", [128, 16 * 288], F32)
    zs = P.sbuf("zs", [128, 4096], F32)
    ya = P.sbuf("ya", [128, 4096], F32)
    ynb = P.sbuf("ynb", [128, 4096], BF16)
    yfm = [P.sbuf("yfm", [128, T], BF16) for _ in range(2)]
    Scum = P.sbuf("Scum", [128, 64], F32)
    Rcum = P.sbuf("Rcum", [128, 64], F32)
    bcd = P.sbuf("bcd", [128, 128], F32)
    scr28 = P.sbuf("scr28", [128, 28], F32)
    scr28b = P.sbuf("scr28b", [128, 28], F32)
    fcdc = P.sbuf("fcdc", [128, 8], F32)
    rmsc = P.sbuf("rmsc", [128, 2], F32)
    scrT = P.sbuf("scrT", [128, 288], F32)
    pZ = P.psum("pZ", [128, 512], F32)
    pZ2 = P.psum("pZ2", [128, 512], F32)
    pD = P.psum("pD", [128, 512], F32)
    pO = P.psum("pO", [128, 512], F32)
    pTb = P.psum("pTb2", [128, 1024], BF16)

    def zxs(i, c0, c1):
        return zx[:, i * 552 + c0:i * 552 + c1]

    def xcs(i, c0, c1):
        return xc[:, i * 288 + c0:i * 288 + c1]

    with scope(P):
        xnfm = P.sbuf("xnfmA", [128, ND128 * 512], BF16)
        u0tok = P.sbuf("u0tok", [128, 128], BF16)
        for tc in range(4):
            em.wait("y", P.ts)
            ldfm = emit_load_xnfm(P, xn, xnfm, tc * 512, 512)
            em.wait("t", P.hs, ldfm)
            for i4 in range(4):
                i = tc * 4 + i4
                em.wait("t", P.ss)
                em.wait("t", P.vs)
                for k in range(ND128):
                    em.op("t", lambda t, _k=k, _i4=i4: t.matmul(
                        pD[:, 0:128],
                        xnfm[:, _k * 512 + _i4 * 128:_k * 512 + (_i4 + 1) * 128],
                        wdT[:, _k * 128:(_k + 1) * 128],
                        start=(_k == 0), stop=False))
                em.op("t", lambda t: t.matmul(
                    pD[:, 0:128], C["ones_row"][0:1, 0:128], bdR[:],
                    start=False, stop=True), inc=P.ts)
                dmm = P.ts.n
                em.wait("s", P.ts, dmm)
                em.op("s", lambda sc: sc.activation(u0tok[:], pD[:, 0:128],
                                                    AF.Gelu), inc=P.ss)
                gel = P.ss.n
                em.wait("t", P.ss, gel)
                em.op("t", lambda t: t.transpose(pTb[:, 0:128], u0tok[:],
                                                 C["eye"][:]), inc=P.ts)
                tp = P.ts.n
                em.wait("v", P.ts, tp)
                em.op("v", lambda v, _i=i: v.tensor_copy(
                    u0fm[:, _i * 128:(_i + 1) * 128], pTb[:, 0:128]), inc=P.vs)
        em.flush()

    def hydra(ufm, outbuf):
        em.wait("t", P.vs)
        em.wait("t", P.ss)
        for i in range(16):
            pz = pZ if i % 2 == 0 else pZ2
            if i >= 1:
                em.wait("t", P.vs)
            em.op("t", lambda t, _i=i, _p=pz: t.matmul(
                _p[:], ufm[:, _i * 128:(_i + 1) * 128], inwT[:, 0:512],
                start=True, stop=True), inc=P.ts)
            em.op("t", lambda t, _i=i, _p=pz: t.matmul(
                pD[:, 0:40] if False else pD[:, 0:40],
                ufm[:, _i * 128:(_i + 1) * 128], inwT[:, 512:552],
                start=True, stop=True), inc=P.ts)
            zmm = P.ts.n
            em.wait("v", P.ts, zmm)
            em.op("v", lambda v, _i=i, _p=pz: v.tensor_copy(
                zxs(_i, 0, 512), _p[:]), inc=P.vs)
            em.op("v", lambda v, _i=i: v.tensor_copy(
                zxs(_i, 512, 544), pD[:, 0:32]), inc=P.vs)
            em.op("v", lambda v, _i=i: v.tensor_copy(
                dt2f[:, _i * 8:(_i + 1) * 8], pD[:, 32:40]), inc=P.vs)
        inproj_done = P.vs.n
        em.wait("v", P.vs, inproj_done)
        em.op("v", lambda v: v.tensor_tensor(
            dtw[:], dt2f[:], dtb16[:], ALU.add), inc=P.vs)
        dtw_done = P.vs.n
        em.wait("s", P.vs, dtw_done)
        em.op("s", lambda sc: sc.activation(dts[:], dtw[:], AF.Abs))
        em.op("s", lambda sc: sc.activation(dts[:], dts[:], AF.Exp, scale=-1.0))
        em.op("s", lambda sc: sc.activation(dts[:], dts[:], AF.Ln, bias=1.0))
        em.op("s", lambda sc: sc.activation(dtw[:], dtw[:], AF.Relu), inc=P.ss)
        sp_done = P.ss.n
        em.wait("v", P.ss, sp_done)
        em.op("v", lambda v: v.tensor_tensor(dts[:], dts[:], dtw[:], ALU.add),
              inc=P.vs)
        dts_done = P.vs.n
        em.wait("v", P.vs, dts_done)
        for i in range(16):
            l, sh = i // 2, i % 2
            taps = [(j, (l - 6 + j) * 2 + sh) for j in range(7) if l - 6 + j >= 0]
            first = True
            for (j, ti) in taps:
                if first:
                    em.op("v", lambda v, _i=i, _j=j, _ti=ti: v.tensor_tensor(
                        xcs(_i, 0, 288), zxs(_ti, 256, 544),
                        wcb[:, _j * 288:(_j + 1) * 288], ALU.mult), inc=P.vs)
                    first = False
                else:
                    em.op("v", lambda v, _i=i, _j=j, _ti=ti: v.tensor_tensor(
                        scrT[:], zxs(_ti, 256, 544),
                        wcb[:, _j * 288:(_j + 1) * 288], ALU.mult), inc=P.vs)
                    em.wait("v", P.vs)
                    em.op("v", lambda v, _i=i: v.tensor_tensor(
                        xcs(_i, 0, 288), scrT[:], xcs(_i, 0, 288), ALU.add),
                        inc=P.vs)
                    em.wait("v", P.vs)
            em.op("v", lambda v, _i=i: v.tensor_tensor(
                xcs(_i, 0, 288), xcs(_i, 0, 288), cvb[:], ALU.add), inc=P.vs)
            em.wait("v", P.vs)
        conv_done = P.vs.n
        em.wait("s", P.vs, conv_done)
        em.op("s", lambda sc: sc.activation(xc[:], xc[:], AF.Silu))
        em.op("s", lambda sc: sc.activation(
            zs[:].rearrange("p (i c) -> p i c", c=256),
            zx[:].rearrange("p (i c) -> p i c", c=552)[:, :, 0:256], AF.Silu),
            inc=P.ss)
        silu_done = P.ss.n
        em.wait("v", P.ss, silu_done)
        for sh in range(2):
            for l in range(8):
                i = l * 2 + sh
                if l == 0:
                    em.op("v", lambda v, _i=i: v.tensor_copy(
                        Scum[:, _i * 4:(_i + 1) * 4], dts[:, _i * 8:_i * 8 + 4]),
                        inc=P.vs)
                else:
                    em.op("v", lambda v, _i=i: v.tensor_tensor(
                        Scum[:, _i * 4:(_i + 1) * 4],
                        Scum[:, (_i - 2) * 4:(_i - 1) * 4],
                        dts[:, _i * 8:_i * 8 + 4], ALU.add), inc=P.vs)
                em.wait("v", P.vs)
            for l in range(7, -1, -1):
                i = l * 2 + sh
                if l == 7:
                    em.op("v", lambda v, _i=i: v.tensor_copy(
                        Rcum[:, _i * 4:(_i + 1) * 4],
                        dts[:, _i * 8 + 4:_i * 8 + 8]), inc=P.vs)
                else:
                    em.op("v", lambda v, _i=i: v.tensor_tensor(
                        Rcum[:, _i * 4:(_i + 1) * 4],
                        Rcum[:, (_i + 2) * 4:(_i + 3) * 4],
                        dts[:, _i * 8 + 4:_i * 8 + 8], ALU.add), inc=P.vs)
                em.wait("v", P.vs)
        for sh in range(2):
            for s2 in range(8):
                for tau in range(8):
                    em.op("v", lambda v, _sh=sh, _s2=s2, _tau=tau:
                          v.scalar_tensor_tensor(
                        scrT[:, 0:16], xcs(_s2 * 2 + _sh, 256, 272), 1.0,
                        xcs(_tau * 2 + _sh, 272, 288), ALU.mult, ALU.mult,
                        accum_out=bcd[:, _sh * 64 + _s2 * 8 + _tau:
                                      _sh * 64 + _s2 * 8 + _tau + 1]), inc=P.vs)
        bcd_done = P.vs.n
        em.wait("g", P.vs, bcd_done)
        em.op("g", lambda g: g.memset(ya[:], 0.0), inc=P.gs)
        em.wait("v", P.gs)
        em.wait("v", P.vs, bcd_done)
        for sh in range(2):
            for t_ in range(8):
                s2s = [s2 for s2 in range(8) if s2 != t_]
                for idx, s2 in enumerate(s2s):
                    if s2 < t_:
                        em.op("v", lambda v, _x=idx, _s2=s2, _t=t_, _sh=sh:
                              v.tensor_tensor(
                            scr28[:, _x * 4:(_x + 1) * 4],
                            Scum[:, ((_t - 1) * 2 + _sh) * 4:
                                 ((_t - 1) * 2 + _sh + 1) * 4],
                            Scum[:, (_s2 * 2 + _sh) * 4:(_s2 * 2 + _sh + 1) * 4],
                            ALU.subtract), inc=P.vs)
                    else:
                        em.op("v", lambda v, _x=idx, _s2=s2, _t=t_, _sh=sh:
                              v.tensor_tensor(
                            scr28[:, _x * 4:(_x + 1) * 4],
                            Rcum[:, ((_t + 1) * 2 + _sh) * 4:
                                 ((_t + 1) * 2 + _sh + 1) * 4],
                            Rcum[:, (_s2 * 2 + _sh) * 4:(_s2 * 2 + _sh + 1) * 4],
                            ALU.subtract), inc=P.vs)
                em.wait("v", P.vs)
                em.op("v", lambda v: v.tensor_tensor(scr28[:], scr28[:], A7[:],
                                                     ALU.mult), inc=P.vs)
                coef_pre = P.vs.n
                em.wait("s", P.vs, coef_pre)
                em.op("s", lambda sc: sc.activation(scr28b[:], scr28[:], AF.Exp),
                      inc=P.ss)
                em.wait("v", P.ss)
                for idx, s2 in enumerate(s2s):
                    dcol = 0 if s2 < t_ else 4
                    em.op("v", lambda v, _x=idx, _s2=s2, _sh=sh, _d=dcol:
                          v.tensor_tensor(
                        scr28b[:, _x * 4:(_x + 1) * 4],
                        scr28b[:, _x * 4:(_x + 1) * 4],
                        dts[:, (_s2 * 2 + _sh) * 8 + _d:
                             (_s2 * 2 + _sh) * 8 + _d + 4], ALU.mult), inc=P.vs)
                    tau = t_ - 1 if s2 < t_ else t_ + 1
                    em.wait("v", P.vs)
                    em.op("v", lambda v, _x=idx, _s2=s2, _sh=sh, _tau=tau:
                          v.tensor_scalar(
                        scr28b[:, _x * 4:(_x + 1) * 4],
                        scr28b[:, _x * 4:(_x + 1) * 4],
                        bcd[:, _sh * 64 + _s2 * 8 + _tau:
                            _sh * 64 + _s2 * 8 + _tau + 1], None, ALU.mult),
                        inc=P.vs)
                    em.wait("v", P.vs)
                    ti = t_ * 2 + sh
                    si = s2 * 2 + sh
                    for h in range(4):
                        em.op("v", lambda v, _x=idx, _h=h, _ti=ti, _si=si:
                              v.scalar_tensor_tensor(
                            ya[:, _ti * 256 + _h * 64:_ti * 256 + (_h + 1) * 64],
                            xcs(_si, _h * 64, (_h + 1) * 64),
                            scr28b[:, _x * 4 + _h:_x * 4 + _h + 1],
                            ya[:, _ti * 256 + _h * 64:_ti * 256 + (_h + 1) * 64],
                            ALU.mult, ALU.add), inc=P.vs)
                    em.wait("v", P.vs)
        for i in range(16):
            for h in range(4):
                em.op("v", lambda v, _i=i, _h=h: v.scalar_tensor_tensor(
                    scrT[:, 0:256], xcs(_i, 0, 256), 1.0,
                    fcdb[:, _h * 256:(_h + 1) * 256], ALU.mult, ALU.mult,
                    accum_out=fcdc[:, _h:_h + 1]), inc=P.vs)
            em.wait("v", P.vs)
            em.op("v", lambda v: v.tensor_tensor(fcdc[:, 4:8], fcdc[:, 0:4],
                                                 Dbc[:], ALU.add), inc=P.vs)
            em.wait("v", P.vs)
            for h in range(4):
                em.op("v", lambda v, _i=i, _h=h: v.scalar_tensor_tensor(
                    ya[:, _i * 256 + _h * 64:_i * 256 + (_h + 1) * 64],
                    xcs(_i, _h * 64, (_h + 1) * 64), fcdc[:, 4 + _h:5 + _h],
                    ya[:, _i * 256 + _h * 64:_i * 256 + (_h + 1) * 64],
                    ALU.mult, ALU.add), inc=P.vs)
            em.wait("v", P.vs)
            em.op("v", lambda v, _i=i: v.tensor_tensor(
                ya[:, _i * 256:(_i + 1) * 256], ya[:, _i * 256:(_i + 1) * 256],
                zs[:, _i * 256:(_i + 1) * 256], ALU.mult), inc=P.vs)
            em.wait("v", P.vs)
            em.op("v", lambda v, _i=i: v.scalar_tensor_tensor(
                scrT[:, 0:256], ya[:, _i * 256:(_i + 1) * 256], 1.0,
                ya[:, _i * 256:(_i + 1) * 256], ALU.mult, ALU.mult,
                accum_out=rmsc[:, 0:1]), inc=P.vs)
            ssq_done = P.vs.n
            em.wait("s", P.vs, ssq_done)
            em.op("s", lambda sc: sc.activation(rmsc[:, 1:2], rmsc[:, 0:1],
                                                AF.Ln, scale=1.0 / 256,
                                                bias=C["eps"][:]))
            em.op("s", lambda sc: sc.activation(rmsc[:, 1:2], rmsc[:, 1:2],
                                                AF.Exp, scale=-0.5), inc=P.ss)
            rstd_done = P.ss.n
            em.wait("v", P.ss, rstd_done)
            em.op("v", lambda v, _i=i: v.scalar_tensor_tensor(
                ynb[:, _i * 256:(_i + 1) * 256], ya[:, _i * 256:(_i + 1) * 256],
                rmsc[:, 1:2], nwbc[:], ALU.mult, ALU.mult), inc=P.vs)
            ynb_done = P.vs.n
            em.wait("t", P.vs, ynb_done)
            for c in range(2):
                em.op("t", lambda t, _i=i, _c=c: t.transpose(
                    pTb[:, _c * 128:(_c + 1) * 128],
                    ynb[:, _i * 256 + _c * 128:_i * 256 + (_c + 1) * 128],
                    C["eye"][:]), inc=(P.ts if c == 1 else None))
            tp_done = P.ts.n
            em.wait("v", P.ts, tp_done)
            for c in range(2):
                em.op("v", lambda v, _i=i, _c=c: v.tensor_copy(
                    yfm[_c][:, _i * 128:(_i + 1) * 128],
                    pTb[:, _c * 128:(_c + 1) * 128]), inc=P.vs)
            em.wait("t", P.vs)
        yfm_done = P.vs.n
        em.wait("t", P.vs, yfm_done)
        for c4 in range(4):
            em.op("t", lambda t, _c=c4: t.matmul(
                pO[:], outwT[:, 0:128], yfm[0][:, _c * 512:(_c + 1) * 512],
                start=True, stop=False))
            em.op("t", lambda t, _c=c4: t.matmul(
                pO[:], outwT[:, 128:256], yfm[1][:, _c * 512:(_c + 1) * 512],
                start=False, stop=True), inc=P.ts)
            omm = P.ts.n
            em.wait("v", P.ts, omm)
            em.op("v", lambda v, _c=c4: v.tensor_copy(
                outbuf[:, _c * 512:(_c + 1) * 512], pO[:]), inc=P.vs)
            em.wait("t", P.vs)
        em.flush()

    hydra(u0fm, u1fm)
    hydra(u1fm, h2fm)

    with scope(P):
        xres = P.sbuf("xresB", [128, 512], F32)
        xnr = P.sbuf("xnrB", [128, 512], BF16)
        stage = [P.sbuf("stgB", [128, 512], F32) for _ in range(2)]
        em.wait("v", P.vs)
        em.op("v", lambda v: v.tensor_scalar(
            mixfm[:], u0fm[:], a1c[:, 0:1], None, ALU.mult), inc=P.vs)
        em.wait("v", P.vs)
        em.op("v", lambda v: v.scalar_tensor_tensor(
            mixfm[:], h2fm[:], a0c[:, 0:1], mixfm[:], ALU.mult, ALU.add),
            inc=P.vs)
        mix_done = P.vs.n
        em.wait("t", P.vs, mix_done)
        for i in range(16):
            for hc in range(8):
                pz = pZ if (i * 8 + hc) % 2 == 0 else pZ2
                if i * 8 + hc >= 2:
                    em.wait("t", P.vs)
                em.op("t", lambda t, _i=i, _h=hc, _p=pz: t.matmul(
                    _p[:], mixfm[:, _i * 128:(_i + 1) * 128],
                    wuT[:, _h * 512:(_h + 1) * 512],
                    start=True, stop=False))
                em.op("t", lambda t, _h=hc, _p=pz: t.matmul(
                    _p[:], C["ones_row"][0:1, 0:128],
                    buR[0:1, _h * 512:(_h + 1) * 512],
                    start=False, stop=True), inc=P.ts)
                umm = P.ts.n
                ldx = em.dma("g", xres[:],
                             x_ap[i * 128:(i + 1) * 128,
                                  hc * 512:(hc + 1) * 512])
                ldn = em.dma("g", xnr[:],
                             xn[i * 128:(i + 1) * 128,
                                hc * 512:(hc + 1) * 512])
                stg = stage[(i * 8 + hc) % 2]
                em.wait("v", P.ts, umm)
                em.wait("v", P.ds, ldn)
                em.op("v", lambda v, _p=pz, _s=stg: v.scalar_tensor_tensor(
                    _s[:], _p[:], 1.0, xres[:], ALU.mult, ALU.add), inc=P.vs)
                em.wait("v", P.vs)
                em.op("v", lambda v, _s=stg: v.tensor_tensor(
                    _s[:], _s[:], xnr[:], ALU.add), inc=P.vs)
                sdone = P.vs.n
                em.wait("g", P.vs, sdone)
                em.dma("g", xo_ap[i * 128:(i + 1) * 128,
                                  hc * 512:(hc + 1) * 512], stg[:])
        em.flush()
    em.wait("g", P.ds)
    em.wait("g", P.hs)
    em.flush()


def _pm(a):
    return np.ascontiguousarray(a, dtype=np.float32)


def _w13L(w, bT):
    base = np.asarray(w).T.reshape(32, 128, 86, 128).transpose(2, 1, 0, 3) \
        .reshape(86, 128, 4096)
    out = np.zeros((86, 128, 4224), np.float32)
    out[:, :, :4096] = base
    bt = np.asarray(bT).reshape(86, 128, 16).transpose(0, 2, 1)
    out[:, 0:16, 4096:4224] = bt
    return _pm(out)


def _w2L(w):
    return _pm(np.asarray(w).T.reshape(86, 128, 8, 512).transpose(2, 1, 0, 3))


HOST_LAYOUTS = {
    "l1ffn": [
        ("w1L", (86, 128, 4224), lambda i: _w13L(i["L1_ffn_w1"], i["L1_ffn_lora_B1"])),
        ("w3L", (86, 128, 4224), lambda i: _w13L(i["L1_ffn_w3"], i["L1_ffn_lora_B3"])),
        ("w2L", (8, 128, 86, 512), lambda i: _w2L(i["L1_ffn_w2"])),
        ("a1tL", (128, 512), lambda i: _pm(np.asarray(i["L1_ffn_lora_A1"]).T
            .reshape(32, 128, 16).transpose(1, 0, 2).reshape(128, 512))),
        ("a3tL", (128, 512), lambda i: _pm(np.asarray(i["L1_ffn_lora_A3"]).T
            .reshape(32, 128, 16).transpose(1, 0, 2).reshape(128, 512))),
        ("a2tL", (128, 1376), lambda i: _pm(np.asarray(i["L1_ffn_lora_A2"]).T
            .reshape(86, 128, 16).transpose(1, 0, 2).reshape(128, 1376))),
        ("b2T", (16, 4096), lambda i: _pm(np.asarray(i["L1_ffn_lora_B2"]).T)),
    ],
    "l1attn": [
        ("wdT", (128, 4096), lambda i: _pm(np.asarray(i["L1_attn_w_down"]).T
            .reshape(32, 128, 128).transpose(1, 0, 2).reshape(128, 4096))),
        ("wuT", (128, 4096), lambda i: _pm(np.asarray(i["L1_attn_w_up"]).T)),
        ("inwT", (128, 552), lambda i: _pm(np.asarray(i["L1_attn_hy_in_w"]).T)),
        ("outwT", (128, 256), lambda i: _pm(np.asarray(i["L1_attn_hy_out_w"]).T
            .reshape(2, 128, 128).transpose(1, 0, 2).reshape(128, 256))),
        ("bdR", (1, 128), lambda i: _pm(np.asarray(i["L1_attn_b_down"])
            .reshape(1, 128))),
        ("buR", (1, 4096), lambda i: _pm(np.asarray(i["L1_attn_b_up"])
            .reshape(1, 4096))),
        ("convw7R", (1, 2016), lambda i: _pm(np.asarray(i["L1_attn_hy_conv_w"])
            [:, 0, :].T.reshape(1, 2016))),
        ("convbR", (1, 288), lambda i: _pm(np.asarray(i["L1_attn_hy_conv_b"])
            .reshape(1, 288))),
        ("dtb16R", (1, 128), lambda i: _pm(np.tile(
            np.asarray(i["L1_attn_hy_dt_bias"]), 32).reshape(1, 128))),
        ("AlogR", (1, 4), lambda i: _pm(np.asarray(i["L1_attn_hy_A_log"])
            .reshape(1, 4))),
        ("DR", (1, 4), lambda i: _pm(np.asarray(i["L1_attn_hy_D"]).reshape(1, 4))),
        ("gateR", (1, 1), lambda i: _pm(np.asarray(i["L1_attn_gate1"])
            .reshape(1, 1))),
        ("fcdR", (1, 1024), lambda i: _pm(np.asarray(i["L1_attn_hy_fcD_w"])
            .reshape(1, 1024))),
        ("normwR", (1, 256), lambda i: _pm(np.asarray(i["L1_attn_hy_norm_w"])
            .reshape(1, 256))),
    ],
    "l0attn": [
        ("WqL", (128, 384), lambda i: _pm(np.asarray(i["L0_attn_Wq_d"])
            .reshape(32, 128, 12).transpose(1, 0, 2).reshape(128, 384))),
        ("WkvL", (128, 128), lambda i: _pm(np.asarray(i["L0_attn_Wkv_d"])
            .reshape(32, 128, 4).transpose(1, 0, 2).reshape(128, 128))),
        ("Wqk", (12, 128), lambda i: _pm(i["L0_attn_W_qk"])),
        ("WvU", (4, 4096), lambda i: _pm(i["L0_attn_Wv_u"])),
        ("WoL", (8, 128, 32, 512), lambda i: _pm(np.asarray(i["L0_attn_Wo"])
            .reshape(32, 128, 8, 512).transpose(2, 1, 0, 3))),
        ("bqC", (12, 1), lambda i: _pm(np.asarray(i["L0_attn_bq_d"]).reshape(12, 1))),
        ("bkvC", (4, 1), lambda i: _pm(np.asarray(i["L0_attn_bkv_d"]).reshape(4, 1))),
        ("bqkC", (128, 1), lambda i: _pm(np.asarray(i["L0_attn_b_qk"]).reshape(128, 1))),
        ("bqk4", (4, 32), lambda i: _pm(np.asarray(i["L0_attn_b_qk"]).reshape(32, 4).T)),
        ("bvuC", (128, 32), lambda i: _pm(np.asarray(i["L0_attn_bv_u"])
            .reshape(32, 128).T)),
        ("boR", (1, 4096), lambda i: _pm(np.asarray(i["L0_attn_bo"]).reshape(1, 4096))),
    ],
    "l0ffn": [
        ("rwT", (128, 256), lambda i: _pm(np.asarray(i["L0_ffn_route_w"]).T
            .reshape(32, 128, 8).transpose(1, 0, 2).reshape(128, 256))),
        ("AtL", (128, 4096), lambda i: _pm(np.asarray(i["L0_ffn_A"]).reshape(128, 4096).T
            .reshape(32, 128, 128).transpose(1, 0, 2).reshape(128, 4096))),
        ("BcatL", (128, 4096), lambda i: _pm(np.asarray(i["L0_ffn_Bw"])
            .transpose(0, 2, 1).reshape(128, 4096))),
    ],
}

BUILDERS = {}


def build_sub_program(sub):
    P = Prog()
    x_in = P.nc.dram_tensor("x_in", [T, D], F32, kind="ExternalInput").ap()
    x_out = P.nc.dram_tensor("x_out", [T, D], F32, kind="ExternalOutput").ap()
    W = {}
    for lname, shape, _fn in HOST_LAYOUTS[sub]:
        W[lname] = P.nc.dram_tensor(lname, list(shape), F32,
                                    kind="ExternalInput").ap()
    C = emit_setup(P)
    base = sub.rstrip("2")
    BUILDERS[base](P, C, x_in, x_out, W)
    P.stack.close()
    return P.nc


def shard_x(x_full):
    out = []
    for c in range(N_CORES):
        out.append(np.ascontiguousarray(
            x_full[:, c * 256:(c + 1) * 256, :]).reshape(T, D))
    return out


def unshard_x(slices):
    full = np.empty((8, 2048, 4096), np.float32)
    for c in range(N_CORES):
        full[:, c * 256:(c + 1) * 256, :] = slices[c].reshape(8, 256, D)
    return full


def run_sublayer(sub, x_full, inputs):
    nc = build_sub_program(sub)
    xs = shard_x(np.asarray(x_full, np.float32))
    wmap = {lname: fn(inputs) for lname, _s, fn in HOST_LAYOUTS[sub]}
    maps = [dict(x_in=xs[c], **wmap) for c in range(N_CORES)]
    res = run_bass_kernel_spmd(nc, maps, core_ids=list(range(N_CORES)))
    return unshard_x([np.asarray(res.results[c]["x_out"]) for c in range(N_CORES)])


BUILDERS["l0ffn"] = build_l0ffn
BUILDERS["l1ffn"] = build_l1ffn
BUILDERS["l0attn"] = build_l0attn
BUILDERS["l1attn"] = build_l1attn

SUB_ORDER = ["l0ffn", "l0attn", "l0ffn", "l1ffn", "l1attn", "l1ffn"]

_prog_cache = {}


def build_fused():
    P = Prog()
    x_in = P.nc.dram_tensor("x_in", [T, D], F32, kind="ExternalInput").ap()
    x_out = P.nc.dram_tensor("x_out", [T, D], F32, kind="ExternalOutput").ap()
    W = {}
    for sub in ("l0ffn", "l0attn", "l1ffn", "l1attn"):
        W[sub] = {ln: P.nc.dram_tensor(ln, list(shape), F32,
                                       kind="ExternalInput").ap()
                  for ln, shape, _fn in HOST_LAYOUTS[sub]}
    C = emit_setup(P)
    bufs = [P.dram(f"xb{i}", [T, D], F32).ap()
            for i in range(len(SUB_ORDER) - 1)]
    chain = [x_in] + bufs + [x_out]
    caches = None
    if "l1ffn" in SUB_ORDER:
        caches = emit_l1ffn_caches_decl(P)
        queue_l1ffn_casts(P, W["l1ffn"], caches)
    for i, sub in enumerate(SUB_ORDER):
        with scope(P):
            if sub == "l1ffn":
                caches = build_l1ffn(P, C, chain[i], chain[i + 1],
                                     W["l1ffn"], caches)
            else:
                BUILDERS[sub](P, C, chain[i], chain[i + 1], W[sub])
    P.stack.close()
    return P.nc


def kernel(**inputs):
    if "fused" not in _prog_cache:
        _prog_cache["fused"] = build_fused()
    xs = shard_x(np.asarray(inputs["L0_attn_x"], np.float32))
    wmap = {}
    for sub in ("l0ffn", "l0attn", "l1ffn", "l1attn"):
        for ln, _s, fn in HOST_LAYOUTS[sub]:
            wmap[ln] = fn(inputs)
    maps = [dict(x_in=xs[c], **wmap) for c in range(N_CORES)]
    res = run_bass_kernel_spmd(_prog_cache["fused"], maps,
                               core_ids=list(range(N_CORES)))
    return unshard_x([np.asarray(res.results[c]["x_out"])
                      for c in range(N_CORES)])
